# version 07
# baseline (speedup 1.0000x reference)
import math
import os
from contextlib import ExitStack

import numpy as np
import ml_dtypes

import concourse.bass as bass
from concourse.bass import ds
import concourse.mybir as mybir
from concourse.bass_utils import run_bass_kernel_spmd

F32 = mybir.dt.float32
BF16 = mybir.dt.bfloat16
I32 = mybir.dt.int32
AF = mybir.ActivationFunctionType
ALU = mybir.AluOpType
AX = mybir.AxisListType

NCORES = 8
D = 1024
T = 4096
NH = 16
HD = 64
DCH = D // 128
BF = ml_dtypes.bfloat16

ENGINES = ("sync", "gpsimd", "vector", "scalar", "tensor")


class Sched:

    def __init__(self, nc, ctx):
        self.nc = nc
        self.ctx = ctx
        self.ops = {e: [] for e in ENGINES}
        self.esem = {e: f"_e_{e}" for e in ENGINES}
        self.cnt = {}
        for s in self.esem.values():
            self.cnt[s] = 0
        self.last_w = {}
        self.readers = {}
        self.n_emitted = {e: 0 for e in ENGINES}

    def _edges(self, eng, sem, reads, writes):
        waits = {}

        def add(dep):
            if dep is None:
                return
            dsem, dval = dep
            if dsem == sem and eng == "tensor":
                return
            if waits.get(dsem, -1) < dval:
                waits[dsem] = dval
        for k in reads:
            add(self.last_w.get(k))
        for k in writes:
            add(self.last_w.get(k))
            for r in self.readers.get(k, ()):
                add(r)
        return waits

    def _record(self, sem, val, reads, writes):
        for k in writes:
            self.last_w[k] = (sem, val)
            self.readers[k] = []
        for k in reads:
            self.readers.setdefault(k, []).append((sem, val))

    def op(self, eng, fn, reads=(), writes=(), sem=None):
        sem = sem or self.esem[eng]
        if sem not in self.cnt:
            self.cnt[sem] = 0
        waits = self._edges(eng, sem, reads, writes)
        self.cnt[sem] += 1
        v = self.cnt[sem]
        self.ops[eng].append((fn, waits, sem, v, 1))
        self._record(sem, v, reads, writes)
        return v

    def dma(self, eng, out, in_, reads=(), writes=(), slot="d", **kw):
        qsem = f"_d_{eng}_{slot}"
        if qsem not in self.cnt:
            self.cnt[qsem] = 0
        waits = self._edges(eng, qsem, reads, writes)
        if self.cnt[qsem] > 0:
            if waits.get(qsem, -1) < self.cnt[qsem]:
                waits[qsem] = self.cnt[qsem]
        self.cnt[qsem] += 16
        v = self.cnt[qsem]
        fn = lambda E: E.dma_start(
            out=out, in_=(in_(E) if callable(in_) else in_), **kw)
        self.ops[eng].append((fn, waits, qsem, v, 16))
        self._record(qsem, v, list(reads), list(writes) + [("_dmaslot", qsem)])
        return v

    def emit(self):
        import kernel as _k
        _k._LAST_S = self
        sems = {}
        for s, c in self.cnt.items():
            if c > 0:
                sems[s] = self.ctx.enter_context(self.nc.semaphore(s))
        block = self.ctx.enter_context(self.nc.Block())
        for eng in ENGINES:
            items = self.ops[eng]
            if not items:
                continue

            def body(E, items=items, eng=eng):
                waited = {}
                for fn, waits, sem, v, n in items:
                    for ws, wv in sorted(waits.items()):
                        if waited.get(ws, 0) >= wv:
                            continue
                        E.wait_ge(sems[ws], wv)
                        waited[ws] = wv
                    ins = fn(E)
                    assert ins is not None
                    ins.then_inc(sems[sem], n)
                    self.n_emitted[eng] += 1
            getattr(block, eng)(body)


_NAMEC = [0]


def sbuf(ctx, nc, name, shape, dtype):
    _NAMEC[0] += 1
    return ctx.enter_context(nc.sbuf_tensor(f"sb{_NAMEC[0]}_{name}", shape, dtype))


def psum(ctx, nc, name, shape, dtype=F32):
    _NAMEC[0] += 1
    return ctx.enter_context(nc.psum_tensor(f"ps{_NAMEC[0]}_{name}", shape, dtype))


def barrier(S):
    S._barno = getattr(S, "_barno", 0) + 1
    b = S._barno
    slots = [("_dmaslot", s) for s in S.cnt
             if s.startswith("_d_") and S.cnt[s] > 0]
    for eng in ENGINES:
        S.op(eng, lambda E: E.nop(nofuse=True), reads=slots,
             writes=((("_bar", b, eng)),), sem=f"_bs_{eng}")
    for eng in ENGINES:
        S.op(eng, lambda E: E.nop(nofuse=True),
             reads=[("_bar", b, e2) for e2 in ENGINES], sem=f"_bs_{eng}")


CONST_VALS = {"eps": 1e-5, "mhalf": -0.5, "half": 0.5, "m128": -128.0,
              "quart": 0.25, "mtwenty": -20.0, "one": 1.0, "big": 1e5,
              "eighth": 0.125, "mone": -1.0}
CONST_IDX = {k: i for i, k in enumerate(CONST_VALS)}


def FC(C, name, p=128):
    i = CONST_IDX[name]
    return C["fc"][0:p, i:i + 1]


def host_fconst():
    v = np.array(list(CONST_VALS.values()), dtype=np.float32)
    return np.tile(v[None, :], (128, 1))


def dma_tile(S, xs, k, x, d, fg):
    n = len(xs)
    xb = xs[k % n]
    S.dma("sync", xb[:], x[d * 128:(d + 1) * 128, fg * 512:(fg + 1) * 512],
          writes=[("xs", k % n)], slot=f"xs{k % n}")
    return xb[:], [("xs", k % n)]


def emit_ln_stats_generic(S, nc, C, tag, m_row, r_row, ntok, xs, sq,
                          ps_sx, ps_sq, get_tile, ones_x=None):
    if ones_x is None:
        ones_x = (C["onesA"], ("cA",))
    nfg = ntok // 512
    k = 0
    for fg in range(nfg):
        sub = 0
        for d in range(DCH):
            ap, rds = get_tile(d, fg, k)
            qb = sq[k % 2]
            S.op("scalar", lambda E, ap=ap, qb=qb: E.activation(
                out=qb[:], in_=ap, func=AF.Square),
                reads=rds, writes=[(tag + "sq", k % 2)])
            S.op("tensor", lambda E, ap=ap, sub=sub, d=d: E.matmul(
                ps_sx[sub][:], ones_x[0][:], ap,
                start=(d == 0), stop=(d == DCH - 1)),
                reads=rds + [ones_x[1], (tag + "psxr", sub)],
                writes=[(tag + "psx", sub)])
            S.op("tensor", lambda E, qb=qb, sub=sub, d=d, k=k: E.matmul(
                ps_sq[sub][:], C["onesA"][:], qb[:],
                start=(d == 0), stop=(d == DCH - 1)),
                reads=[(tag + "sq", k % 2), ("cA",), (tag + "psqr", sub)],
                writes=[(tag + "psq", sub)])
            k += 1
        sub = 0
        c0 = fg * 512
        S.op("vector", lambda E, sub=sub, c0=c0: E.tensor_scalar(
            out=m_row[:, c0:c0 + 512], in0=ps_sx[sub][:],
            scalar1=1.0 / D, scalar2=None, op0=ALU.mult),
            reads=[(tag + "psx", sub)],
            writes=[(tag + "mrow", fg), (tag + "psxr", sub)])
        S.op("vector", lambda E, sub=sub, c0=c0: E.tensor_scalar(
            out=r_row[:, c0:c0 + 512], in0=ps_sq[sub][:],
            scalar1=1.0 / D, scalar2=None, op0=ALU.mult),
            reads=[(tag + "psq", sub)],
            writes=[(tag + "vrow", fg), (tag + "psqr", sub)])
        S.op("vector", lambda E, c0=c0: E.scalar_tensor_tensor(
            out=sq[0][0:1, 0:512], in0=m_row[:, c0:c0 + 512],
            scalar=-1.0, in1=m_row[:, c0:c0 + 512],
            op0=ALU.mult, op1=ALU.mult),
            reads=[(tag + "mrow", fg)],
            writes=[(tag + "msq",), (tag + "sq", 0)])
        S.op("vector", lambda E, c0=c0: E.tensor_add(
            out=r_row[:, c0:c0 + 512], in0=r_row[:, c0:c0 + 512],
            in1=sq[0][0:1, 0:512]),
            reads=[(tag + "msq",), (tag + "sq", 0), (tag + "vrow", fg)],
            writes=[(tag + "vrow", fg)])
    S.op("scalar", lambda E: E.activation(
        out=r_row[:], in_=r_row[:], func=AF.Ln, bias=FC(C, "eps", 1),
        scale=1.0),
        reads=[(tag + "vrow", fg) for fg in range(nfg)] + [("cfc",)],
        writes=[(tag + "rln",)])
    S.op("scalar", lambda E: E.activation(
        out=r_row[:], in_=r_row[:], func=AF.Exp, bias=0.0,
        scale=FC(C, "mhalf", 1)),
        reads=[(tag + "rln",), ("cfc",)], writes=[(tag + "rrow",)])



def build_ffn_l0():
    nc = bass.Bass("TRN2", target_bir_lowering=False)
    io = {}
    io["x"] = nc.dram_tensor("x", [D, T], F32, kind="ExternalInput").ap()
    io["wg"] = nc.dram_tensor("wg", [9, D, 2048], F32, kind="ExternalInput").ap()
    io["wu"] = nc.dram_tensor("wu", [9, D, 2048], F32, kind="ExternalInput").ap()
    io["wd"] = nc.dram_tensor("wd", [9, 2048, D], F32, kind="ExternalInput").ap()
    io["wr"] = nc.dram_tensor("wr", [D, 8], F32, kind="ExternalInput").ap()
    io["rb"] = nc.dram_tensor("rb", [8, 1], F32, kind="ExternalInput").ap()
    io["onesA"] = nc.dram_tensor("onesA", [128, 1], F32, kind="ExternalInput").ap()
    io["onesB"] = nc.dram_tensor("onesB", [1, 128], F32, kind="ExternalInput").ap()
    io["ident"] = nc.dram_tensor("ident", [128, 128], F32, kind="ExternalInput").ap()
    io["fconst"] = nc.dram_tensor("fconst", [128, len(CONST_VALS)], F32,
                                  kind="ExternalInput").ap()
    io["y"] = nc.dram_tensor("y", [D, T], F32, kind="ExternalOutput").ap()
    with ExitStack() as ctx:
        S = Sched(nc, ctx)
        emit_ffn_l0(S, nc, ctx, io, tag="f0")
        S.op("gpsimd", lambda E: E.nop(nofuse=True),
             reads=[("f0y", ho, tb) for ho in range(8) for tb in range(16)])
        S.emit()
    return nc


def emit_ffn_l0(S, nc, ctx, io, tag):
    x, wg, wu, wd, wr, rb = (io["x"], io["wg"], io["wu"], io["wd"], io["wr"],
                             io["rb"])
    onesA, onesB, ident, fconst, y = (io["onesA"], io["onesB"], io["ident"],
                                      io["fconst"], io["y"])
    if True:
        C = {}
        C["onesA"] = sbuf(ctx, nc, "onesA", [128, 1], F32)
        C["onesB"] = sbuf(ctx, nc, "onesB", [1, 128], F32)
        C["rb"] = sbuf(ctx, nc, "rb", [8, 1], F32)
        C["wr"] = sbuf(ctx, nc, "wr", [128, 64], BF16)
        C["id"] = sbuf(ctx, nc, "id", [128, 128], F32)
        C["fc"] = sbuf(ctx, nc, "fc", [128, len(CONST_VALS)], F32)
        xln = [sbuf(ctx, nc, f"xln{d}", [128, 2048], BF16) for d in range(DCH)]
        asl = [sbuf(ctx, nc, f"asl{c}", [128, 2048], BF16) for c in range(16)]
        cwtok = sbuf(ctx, nc, "cwtok", [128, 128], F32)
        cw_bc = sbuf(ctx, nc, "cw_bc", [128, 2048], BF16)
        probs = sbuf(ctx, nc, "probs", [8, 2048], F32)
        stage = [sbuf(ctx, nc, f"stage{i}", [128, 512], F32) for i in range(4)]
        wgsl = [sbuf(ctx, nc, f"wgsl{i}", [128, 1024], BF16) for i in range(2)]
        wusl = [sbuf(ctx, nc, f"wusl{i}", [128, 1024], BF16) for i in range(2)]
        wdsl = [sbuf(ctx, nc, f"wdsl{i}", [128, 2048], BF16) for i in range(2)]

        S.dma("sync", C["onesA"][:], onesA, writes=[("cA",)], slot="c")
        S.dma("sync", C["onesB"][:], onesB, writes=[("cB",)], slot="c")
        S.dma("sync", C["rb"][:], rb, writes=[("crb",)], slot="c")
        S.dma("sync", C["id"][:], ident, writes=[("cid",)], slot="c")
        S.dma("sync", C["fc"][:], fconst, writes=[("cfc",)], slot="c")
        S.dma("gpsimd", C["wr"][:].rearrange("p (d e) -> p d e", e=8),
              wr.rearrange("(d p) e -> p d e", p=128), writes=[("cwr",)],
              slot="c")
        S.dma("gpsimd", y, x, slot="y0",
              writes=[(tag + "y", ho, tb) for ho in range(8)
                      for tb in range(16)])

        nhalf = int(os.environ.get("FFN0_HALVES", "2"))
        for half in range(nhalf):
            emit_ffn_l0_half(S, nc, half, x, y, wg, wu, wd, C, xln, asl,
                             cwtok, cw_bc, probs, stage, wgsl, wusl, wdsl, tag)
        barrier(S)


def emit_ffn_l0_half(S, nc, half, x, y, wg, wu, wd, C, xln, asl,
                     cwtok, cw_bc, probs, stage, wgsl, wusl, wdsl, tag="f0"):
    tok0 = half * 2048
    NFG = 4
    xh = lambda d, c0, n: x[d * 128:(d + 1) * 128, tok0 + c0:tok0 + c0 + n]

    with ExitStack() as ctx:
        m_row = sbuf(ctx, nc, "mrow", [1, 2048], F32)
        r_row = sbuf(ctx, nc, "rrow", [1, 2048], F32)
        msq = sbuf(ctx, nc, "msq", [1, 512], F32)
        xs = [sbuf(ctx, nc, f"xs{i}", [128, 512], F32) for i in range(4)]
        sq = [sbuf(ctx, nc, f"sq{i}", [128, 512], F32) for i in range(2)]
        ps_sx = [psum(ctx, nc, f"psx{i}", [1, 512]) for i in range(2)]
        ps_sq = [psum(ctx, nc, f"psq{i}", [1, 512]) for i in range(2)]
        k = 0
        for pair in range(2):
            for d in range(DCH):
                for sub in range(2):
                    fg = pair * 2 + sub
                    xb, qb = xs[k % 4], sq[k % 2]
                    S.dma("sync", xb[:], xh(d, fg * 512, 512),
                          writes=[("xs", k % 4)], slot=f"xs{k % 4}")
                    S.op("scalar", lambda E, xb=xb, qb=qb: E.activation(
                        out=qb[:], in_=xb[:], func=AF.Square),
                        reads=[("xs", k % 4)], writes=[("sq", k % 2)])
                    S.op("tensor", lambda E, xb=xb, sub=sub, d=d: E.matmul(
                        ps_sx[sub][:], C["onesA"][:], xb[:],
                        start=(d == 0), stop=(d == DCH - 1)),
                        reads=[("xs", k % 4), ("cA",), ("psxr", sub)],
                        writes=[("psx", sub)])
                    S.op("tensor", lambda E, qb=qb, sub=sub, d=d, k=k: E.matmul(
                        ps_sq[sub][:], C["onesA"][:], qb[:],
                        start=(d == 0), stop=(d == DCH - 1)),
                        reads=[("sq", k % 2), ("cA",), ("psqr", sub)],
                        writes=[("psq", sub)])
                    k += 1
            for sub in range(2):
                fg = pair * 2 + sub
                c0 = fg * 512
                S.op("vector", lambda E, sub=sub, c0=c0: E.tensor_scalar(
                    out=m_row[:, c0:c0 + 512], in0=ps_sx[sub][:],
                    scalar1=1.0 / D, scalar2=None, op0=ALU.mult),
                    reads=[("psx", sub)], writes=[("mrow", fg), ("psxr", sub)])
                S.op("vector", lambda E, sub=sub, c0=c0: E.tensor_scalar(
                    out=r_row[:, c0:c0 + 512], in0=ps_sq[sub][:],
                    scalar1=1.0 / D, scalar2=None, op0=ALU.mult),
                    reads=[("psq", sub)], writes=[("vrow", fg), ("psqr", sub)])
                S.op("vector", lambda E, c0=c0: E.tensor_mul(
                    out=msq[:], in0=m_row[:, c0:c0 + 512],
                    in1=m_row[:, c0:c0 + 512]),
                    reads=[("mrow", fg)], writes=[("msq",)])
                S.op("vector", lambda E, c0=c0: E.tensor_sub(
                    out=r_row[:, c0:c0 + 512], in0=r_row[:, c0:c0 + 512],
                    in1=msq[:]),
                    reads=[("msq",), ("vrow", fg)], writes=[("vrow", fg)])
        S.op("scalar", lambda E: E.activation(
            out=r_row[:], in_=r_row[:], func=AF.Ln, bias=FC(C, "eps", 1),
            scale=1.0),
            reads=[("vrow", fg) for fg in range(NFG)] + [("cfc",)],
            writes=[("rln",)])
        S.op("scalar", lambda E: E.activation(
            out=r_row[:], in_=r_row[:], func=AF.Exp, bias=0.0,
            scale=FC(C, "mhalf", 1)),
            reads=[("rln",), ("cfc",)], writes=[("rrow",)])

        mb = sbuf(ctx, nc, "mb", [128, 512], F32)
        rbc = sbuf(ctx, nc, "rbc", [128, 512], F32)
        pbc = psum(ctx, nc, "pbc", [128, 512])
        tmp = [sbuf(ctx, nc, f"lntmp{i}", [128, 512], F32) for i in range(2)]
        for fg in range(NFG):
            c0 = fg * 512
            S.op("tensor", lambda E, c0=c0: E.matmul(
                pbc[:], C["onesB"][:], m_row[:, c0:c0 + 512], start=True,
                stop=True),
                reads=[("mrow", fg), ("cB",), ("mbv",)], writes=[("pbc",)])
            S.op("vector", lambda E: E.tensor_copy(out=mb[:], in_=pbc[:]),
                 reads=[("pbc",)], writes=[("mbv",)])
            S.op("tensor", lambda E, c0=c0: E.matmul(
                pbc[:], C["onesB"][:], r_row[:, c0:c0 + 512], start=True,
                stop=True),
                reads=[("rrow",), ("cB",), ("mbv",), ("rbcv",)],
                writes=[("pbc",)])
            S.op("vector", lambda E: E.tensor_copy(out=rbc[:], in_=pbc[:]),
                 reads=[("pbc",)], writes=[("rbcv",)])
            for d in range(DCH):
                xb = xs[d % 4]
                S.dma("sync", xb[:], xh(d, fg * 512, 512),
                      writes=[("xs", d % 4)], slot=f"xs{d % 4}")
                S.op("vector", lambda E, xb=xb, d=d: E.tensor_sub(
                    out=tmp[d % 2][:], in0=xb[:], in1=mb[:]),
                    reads=[("xs", d % 4), ("mbv",)], writes=[("lntmp", d % 2)])
                S.op("vector", lambda E, d=d, c0=c0: E.tensor_mul(
                    out=xln[d][:, c0:c0 + 512], in0=tmp[d % 2][:], in1=rbc[:]),
                    reads=[("lntmp", d % 2), ("rbcv",)], writes=[("xln", d, fg)])
        ps_r = psum(ctx, nc, "ps_r", [8, 512])
        for fg in range(NFG):
            c0 = fg * 512
            S.op("tensor", lambda E, c0=c0: [
                E.matmul(ps_r[:], C["wr"][:, 8 * d:8 * d + 8],
                         xln[d][:, c0:c0 + 512],
                         start=(d == 0), stop=(d == DCH - 1))
                for d in range(DCH)][-1],
                reads=[("xln", d, fg) for d in range(DCH)]
                + [("cwr",), ("psrr",)],
                writes=[("psr",)])
            S.op("scalar", lambda E, c0=c0, fg=fg: E.activation(
                out=probs[:, c0:c0 + 512], in_=ps_r[:], func=AF.Sigmoid,
                bias=C["rb"][:], scale=1.0),
                reads=[("psr",), ("crb",)], writes=[("probs", fg), ("psrr",)])
        ps_pt = psum(ctx, nc, "ps_pt", [128, 8])
        ptok = sbuf(ctx, nc, "ptok", [128, 8], F32)
        mx8 = sbuf(ctx, nc, "mx8", [128, 8], F32)
        ge8 = sbuf(ctx, nc, "ge8", [128, 8], F32)
        ntile = int(os.environ.get("FFN0_NTILE", "16"))
        for tile in range(ntile):
            S.op("tensor", lambda E, tile=tile: E.transpose(
                ps_pt[:], probs[:, tile * 128:(tile + 1) * 128],
                C["id"][0:8, 0:8]),
                reads=[("probs", tile // 4), ("cid",), ("ptokv",)],
                writes=[("pspt",)])
            S.op("vector", lambda E: E.tensor_copy(out=ptok[:], in_=ps_pt[:]),
                 reads=[("pspt",)], writes=[("ptokv",)])
            S.op("vector", lambda E: E.max(out=mx8[:], in_=ptok[:]),
                 reads=[("ptokv",)], writes=[("mx8",)])
            S.op("vector", lambda E: E.tensor_scalar(
                out=ge8[:], in0=ptok[:], scalar1=mx8[:, 1:2], scalar2=None,
                op0=ALU.is_ge),
                reads=[("mx8",), ("ptokv",)], writes=[("ge8",)])
            S.op("vector", lambda E, tile=tile: E.tensor_mul(
                out=cwtok[:, tile * 8:(tile + 1) * 8], in0=ptok[:], in1=ge8[:]),
                reads=[("ge8",), ("ptokv",)], writes=[("cwtok", tile)])
        barrier(S)

    with ExitStack() as ctx:
        ps_g = [psum(ctx, nc, f"psg{i}", [128, 512]) for i in range(2)]
        ps_u = [psum(ctx, nc, f"psu{i}", [128, 512]) for i in range(2)]
        ps_z = [psum(ctx, nc, f"psz{i}", [128, 512]) for i in range(2)]
        ps_row = psum(ctx, nc, "ps_row", [1, 512])
        ps_cb = psum(ctx, nc, "ps_cb", [128, 512])
        cwrow = sbuf(ctx, nc, "cwrow", [1, 2048], F32)
        sil = sbuf(ctx, nc, "sil", [128, 512], BF16)
        a0b = sbuf(ctx, nc, "a0b", [128, 512], BF16)
        for e in range(int(os.environ.get("FFN0_NEXP", "9"))):
            if e > 0:
                for grp in range(4):
                    S.op("tensor", lambda E, grp=grp, e=e: [
                        E.transpose(ps_row[:, i * 128:(i + 1) * 128],
                                    cwtok[:, (grp * 4 + i) * 8 + (e - 1):
                                          (grp * 4 + i) * 8 + e],
                                    C["id"][:])
                        for i in range(4)][-1],
                        reads=[("cwtok", grp * 4 + i) for i in range(4)]
                        + [("cid",), ("psrowr",)],
                        writes=[("psrow",)])
                    S.op("vector", lambda E, grp=grp: E.tensor_copy(
                        out=cwrow[:, grp * 512:(grp + 1) * 512], in_=ps_row[:]),
                        reads=[("psrow",)],
                        writes=[("cwrowv", grp), ("psrowr",)])
                    S.op("tensor", lambda E, grp=grp: E.matmul(
                        ps_cb[:], C["onesB"][:],
                        cwrow[:, grp * 512:(grp + 1) * 512],
                        start=True, stop=True),
                        reads=[("cwrowv", grp), ("cB",), ("pscbr",)],
                        writes=[("pscb",)])
                    S.op("vector", lambda E, grp=grp: E.tensor_copy(
                        out=cw_bc[:, grp * 512:(grp + 1) * 512], in_=ps_cb[:]),
                        reads=[("pscb",)], writes=[("cwbcv", grp), ("pscbr",)])
            for c in range(16):
                sl = c % 2
                S.dma("gpsimd", wgsl[sl][:].rearrange("p (d c) -> p d c", c=128),
                      wg[e, :, c * 128:(c + 1) * 128]
                      .rearrange("(d p) c -> p d c", p=128),
                      writes=[("wgsl", sl)], slot=f"wg{sl}")
                S.dma("gpsimd", wusl[sl][:].rearrange("p (d c) -> p d c", c=128),
                      wu[e, :, c * 128:(c + 1) * 128]
                      .rearrange("(d p) c -> p d c", p=128),
                      writes=[("wusl", sl)], slot=f"wu{sl}")
                for tb in range(4):
                    pp = (c * 4 + tb) % 2
                    S.op("tensor", lambda E, sl=sl, tb=tb, pp=pp: [
                        E.matmul(ps_g[pp][:],
                                 wgsl[sl][:, d * 128:(d + 1) * 128],
                                 xln[d][:, tb * 512:(tb + 1) * 512],
                                 start=(d == 0), stop=(d == DCH - 1))
                        for d in range(DCH)][-1],
                        reads=[("wgsl", sl)]
                        + [("xln", d, tb) for d in range(DCH)],
                        writes=[("psg", pp)])
                    S.op("tensor", lambda E, sl=sl, tb=tb, pp=pp: [
                        E.matmul(ps_u[pp][:],
                                 wusl[sl][:, d * 128:(d + 1) * 128],
                                 xln[d][:, tb * 512:(tb + 1) * 512],
                                 start=(d == 0), stop=(d == DCH - 1))
                        for d in range(DCH)][-1],
                        reads=[("wusl", sl)]
                        + [("xln", d, tb) for d in range(DCH)],
                        writes=[("psu", pp)])
                    S.op("scalar", lambda E, pp=pp: E.activation(
                        out=sil[:], in_=ps_g[pp][:], func=AF.Silu),
                        reads=[("psg", pp)], writes=[("sil",)])
                    S.op("vector", lambda E, c=c, tb=tb, pp=pp: E.tensor_mul(
                        out=asl[c][:, tb * 512:(tb + 1) * 512], in0=sil[:],
                        in1=ps_u[pp][:]),
                        reads=[("sil",), ("psu", pp)],
                        writes=[("asl", c, tb)])
            for ho in range(8):
                sl = ho % 2
                S.dma("gpsimd", wdsl[sl][:].rearrange("p (c h) -> p c h", h=128),
                      wd[e, :, ho * 128:(ho + 1) * 128]
                      .rearrange("(c p) h -> p c h", p=128),
                      writes=[("wdsl", sl)], slot=f"wd{sl}")
                for tq2 in range(4):
                    sg = stage[tq2 % 4]
                    pz = tq2 % 2
                    S.op("tensor", lambda E, sl=sl, tq2=tq2, pz=pz: [
                        E.matmul(ps_z[pz][:],
                                 wdsl[sl][:, c * 128:(c + 1) * 128],
                                 asl[c][:, tq2 * 512:(tq2 + 1) * 512],
                                 start=(c == 0), stop=(c == 15))
                        for c in range(16)][-1],
                        reads=[("wdsl", sl)]
                        + [("asl", c, tq2) for c in range(16)],
                        writes=[("psz", pz)])
                    if e > 0:
                        S.op("vector", lambda E, sg=sg, pz=pz, tq2=tq2:
                             E.scalar_tensor_tensor(
                                 out=sg[:], in0=ps_z[pz][:], scalar=0.5,
                                 in1=cw_bc[:, tq2 * 512:(tq2 + 1) * 512],
                                 op0=ALU.mult, op1=ALU.mult),
                             reads=[("psz", pz), ("cwbcv", tq2),
                                    ("stgr", tq2 % 4)],
                             writes=[("stg", tq2 % 4), ("psz", pz)])
                    else:
                        S.op("vector", lambda E, sg=sg, pz=pz: E.tensor_scalar(
                            out=sg[:], in0=ps_z[pz][:], scalar1=0.5,
                            scalar2=None, op0=ALU.mult),
                            reads=[("psz", pz), ("stgr", tq2 % 4)],
                            writes=[("stg", tq2 % 4), ("psz", pz)])
                    yk = [(tag + "y", ho, half * 8 + tq2 * 2),
                          (tag + "y", ho, half * 8 + tq2 * 2 + 1)]
                    S.dma("gpsimd",
                          y[ho * 128:(ho + 1) * 128,
                            tok0 + tq2 * 512:tok0 + (tq2 + 1) * 512],
                          sg[:],
                          reads=[("stg", tq2 % 4)] + yk,
                          writes=yk + [("stgr", tq2 % 4)],
                          slot=f"y{ho % 4}", accum_op=ALU.add)
        barrier(S)



def build_ffn_l1():
    nc = bass.Bass("TRN2", target_bir_lowering=False)
    io = {}
    io["x"] = nc.dram_tensor("x", [D, T], F32, kind="ExternalInput").ap()
    io["q1"] = nc.dram_tensor("q1", [D, 4096], I32, kind="ExternalInput").ap()
    io["q2"] = nc.dram_tensor("q2", [4096, D], I32, kind="ExternalInput").ap()
    io["sc1"] = nc.dram_tensor("sc1", [8, 4096], F32, kind="ExternalInput").ap()
    io["sc2"] = nc.dram_tensor("sc2", [32, D], F32, kind="ExternalInput").ap()
    io["b1"] = nc.dram_tensor("b1", [128, 32], F32, kind="ExternalInput").ap()
    io["b2"] = nc.dram_tensor("b2", [128, 8], F32, kind="ExternalInput").ap()
    io["gam"] = nc.dram_tensor("gam", [128, 8], F32, kind="ExternalInput").ap()
    io["bet"] = nc.dram_tensor("bet", [128, 8], F32, kind="ExternalInput").ap()
    io["onesA"] = nc.dram_tensor("onesA", [128, 1], F32, kind="ExternalInput").ap()
    io["onesB"] = nc.dram_tensor("onesB", [1, 128], F32, kind="ExternalInput").ap()
    io["fconst"] = nc.dram_tensor("fconst", [128, len(CONST_VALS)], F32,
                                  kind="ExternalInput").ap()
    io["y"] = nc.dram_tensor("y", [D, T], F32, kind="ExternalOutput").ap()
    io["w2s"] = nc.dram_tensor("w2s", [4096, D], BF16, kind="Internal").ap()
    with ExitStack() as ctx:
        S = Sched(nc, ctx)
        emit_ffn_l1(S, nc, ctx, io, tag="f1")
        S.op("gpsimd", lambda E: E.nop(nofuse=True),
             reads=[("f1yout", ho, t) for ho in range(8) for t in range(16)])
        S.emit()
    return nc


def emit_ffn_l1(S, nc, ctx, io, tag):
    x, q1, q2, sc1, sc2 = io["x"], io["q1"], io["q2"], io["sc1"], io["sc2"]
    b1, b2, gam, bet = io["b1"], io["b2"], io["gam"], io["bet"]
    onesA, onesB, fconst, y, w2s = (io["onesA"], io["onesB"], io["fconst"],
                                    io["y"], io["w2s"])
    if True:
        C = {}
        C["onesA"] = sbuf(ctx, nc, "onesA", [128, 1], F32)
        C["onesB"] = sbuf(ctx, nc, "onesB", [1, 128], F32)
        C["fc"] = sbuf(ctx, nc, "fc", [128, len(CONST_VALS)], F32)
        b1_sb = sbuf(ctx, nc, "b1", [128, 32], F32)
        b2_sb = sbuf(ctx, nc, "b2", [128, 8], F32)
        gam_sb = sbuf(ctx, nc, "gam", [128, 8], F32)
        bet_sb = sbuf(ctx, nc, "bet", [128, 8], F32)
        for t_, src, key in ((C["onesA"], onesA, "cA"), (C["onesB"], onesB, "cB"),
                             (C["fc"], fconst, "cfc"), (b1_sb, b1, "cb1"),
                             (b2_sb, b2, "cb2"), (gam_sb, gam, "cgam"),
                             (bet_sb, bet, "cbet")):
            S.dma("sync", t_[:], src, writes=[(key,)], slot="c")
        w1 = [sbuf(ctx, nc, f"w1_{d}", [128, 4096], BF16) for d in range(8)]
        w2sl = [sbuf(ctx, nc, f"w2sl{i}", [128, 4096], BF16) for i in range(2)]
        C["onesAb"] = sbuf(ctx, nc, "onesAb", [128, 1], BF16)
        S.op("vector", lambda E: E.tensor_copy(out=C["onesAb"][:],
                                               in_=C["onesA"][:]),
             reads=[("cA",)], writes=[("cAb",)])
        m1_row = sbuf(ctx, nc, "m1row", [1, T], F32)
        r1_row = sbuf(ctx, nc, "r1row", [1, T], F32)

        with ExitStack() as c3:
            qb = [sbuf(c3, nc, f"qb{i}", [128, 2048], I32) for i in range(2)]
            qf = sbuf(c3, nc, "qf", [128, 2048], F32)
            scb = sbuf(c3, nc, "scb", [128, 2048], F32)
            scr = sbuf(c3, nc, "scr", [1, 2048], F32)
            pbc2 = psum(c3, nc, "pbc2", [128, 512])
            w2d = [sbuf(c3, nc, f"w2d{i}", [128, 1024], BF16) for i in range(2)]

            def dequant(dst, dkey, qsrc, scsrc, ncols):
                for hh in range(max(1, ncols // 2048)):
                    c0 = hh * 2048
                    nn = min(2048, ncols)
                    hb = qb[hh % 2]
                    S.dma("sync", scr[:, 0:nn], scsrc[:, c0:c0 + nn],
                          writes=[("scr",)], slot="scr")
                    for fg in range(nn // 512):
                        S.op("tensor", lambda E, fg=fg: E.matmul(
                            pbc2[:], C["onesB"][:],
                            scr[:, fg * 512:(fg + 1) * 512],
                            start=True, stop=True),
                            reads=[("scr",), ("cB",), ("scbv", fg)],
                            writes=[("pbc2",)])
                        S.op("vector", lambda E, fg=fg: E.tensor_copy(
                            out=scb[:, fg * 512:(fg + 1) * 512], in_=pbc2[:]),
                            reads=[("pbc2",)], writes=[("scbv", fg)])
                    S.dma("sync", hb[:, 0:nn], qsrc[:, c0:c0 + nn],
                          writes=[("qb", hh % 2)], slot=f"q{hh % 2}")
                    S.op("vector", lambda E, hb=hb, nn=nn: E.tensor_copy(
                        out=qf[:, 0:nn], in_=hb[:, 0:nn]),
                        reads=[("qb", hh % 2)], writes=[("qf",)])
                    S.op("vector", lambda E, c0=c0, nn=nn, dst=dst:
                         E.scalar_tensor_tensor(
                             out=dst[:, c0:c0 + nn], in0=qf[:, 0:nn],
                             scalar=-128.0, in1=scb[:, 0:nn],
                             op0=ALU.add, op1=ALU.mult),
                         reads=[("qf",)] + [("scbv", fg)
                                            for fg in range(nn // 512)],
                         writes=[dkey])
            for d in range(8):
                dequant(w1[d], ("w1", d), q1[d * 128:(d + 1) * 128, :],
                        sc1[d:d + 1, :], 4096)
            for f in range(32):
                dequant(w2d[f % 2], ("w2d", f % 2),
                        q2[f * 128:(f + 1) * 128, :], sc2[f:f + 1, :], 1024)
                S.dma("sync", w2s[f * 128:(f + 1) * 128, :], w2d[f % 2][:],
                      reads=[("w2d", f % 2)], writes=[("w2s", f)],
                      slot=f"w2s{f % 2}")
            barrier(S)

        with ExitStack() as c2:
            xs2 = [sbuf(c2, nc, f"xs{i}", [128, 512], F32) for i in range(4)]
            sq2 = [sbuf(c2, nc, f"sq{i}", [128, 512], F32) for i in range(2)]
            psx2 = [psum(c2, nc, "psx", [1, 512])]
            psq2 = [psum(c2, nc, "psq", [1, 512])]
            emit_ln_stats_generic(
                S, nc, C, "L1", m1_row, r1_row, T, xs2, sq2, psx2, psq2,
                lambda d, fg, k: dma_tile(S, xs2, k, x, d, fg))
            barrier(S)

        u = [sbuf(ctx, nc, f"u{d}", [128, 1024], BF16) for d in range(DCH)]
        h = [sbuf(ctx, nc, f"h{d}", [128, 1024], BF16) for d in range(DCH)]
        m2_row = sbuf(ctx, nc, "m2row", [1, 1024], F32)
        r2_row = sbuf(ctx, nc, "r2row", [1, 1024], F32)
        xs = [sbuf(ctx, nc, f"xsq{i}", [128, 512], F32) for i in range(4)]
        sq = [sbuf(ctx, nc, f"sqq{i}", [128, 512], F32) for i in range(2)]
        ps_sx = [psum(ctx, nc, "psx", [1, 512])]
        ps_sq = [psum(ctx, nc, "psq", [1, 512])]
        mb = sbuf(ctx, nc, "mb", [128, 512], F32)
        rbc = sbuf(ctx, nc, "rbc", [128, 512], F32)
        pbc = psum(ctx, nc, "pbc", [128, 512])
        tmp = [sbuf(ctx, nc, f"lt{i}", [128, 512], F32) for i in range(2)]
        ps_y = [psum(ctx, nc, f"psy{i}", [128, 256]) for i in range(2)]
        ps_z = [psum(ctx, nc, f"psz{i}", [128, 256]) for i in range(2)]
        y1 = [sbuf(ctx, nc, f"y1_{f}", [128, 256], BF16) for f in range(32)]
        zb = sbuf(ctx, nc, "zb", [128, 256], F32)
        usl = sbuf(ctx, nc, "usl", [128, 256], F32)
        o1 = sbuf(ctx, nc, "o1", [128, 256], F32)
        ost = [sbuf(ctx, nc, f"ost{i}", [128, 256], F32) for i in range(4)]

        def bcast(row_ap, dst, rkeys, dkey):
            nn = row_ap.shape[-1]
            S.op("tensor", lambda E, nn=nn, row_ap=row_ap, dst=dst: E.matmul(
                pbc[:, 0:nn], C["onesB"][:], row_ap, start=True, stop=True),
                reads=list(rkeys) + [("cB",), (dkey,)], writes=[("pbcq",)])
            S.op("vector", lambda E, nn=nn, dst=dst: E.tensor_copy(
                out=dst[:, 0:nn], in_=pbc[:, 0:nn]),
                reads=[("pbcq",)], writes=[(dkey,)])

        for qt in range(4):
            tq0 = qt * 1024
            for fg in range(2):
                c0 = tq0 + fg * 512
                bcast(m1_row[:, c0:c0 + 512], mb, [("L1mrow", c0 // 512)],
                      "mbv")
                bcast(r1_row[:, c0:c0 + 512], rbc, [("L1rrow",)], "rbcv")
                for d in range(DCH):
                    S.dma("sync", xs[d % 4][:],
                          x[d * 128:(d + 1) * 128, c0:c0 + 512],
                          writes=[("xs", d % 4)], slot=f"xs{d % 4}")
                    S.op("vector", lambda E, d=d: E.tensor_sub(
                        out=tmp[d % 2][:], in0=xs[d % 4][:], in1=mb[:]),
                        reads=[("xs", d % 4), ("mbv",)],
                        writes=[("lt", d % 2)])
                    S.op("vector", lambda E, d=d, fg=fg: E.tensor_mul(
                        out=u[d][:, fg * 512:(fg + 1) * 512],
                        in0=tmp[d % 2][:], in1=rbc[:]),
                        reads=[("lt", d % 2), ("rbcv",)],
                        writes=[("u", d, fg)])
            emit_ln_stats_generic(
                S, nc, C, "L2", m2_row, r2_row, 1024, xs, sq, ps_sx, ps_sq,
                lambda d, fg, k: (u[d][:, fg * 512:(fg + 1) * 512],
                                  [("u", d, fg)]),
                ones_x=(C["onesAb"], ("cAb",)))
            for fg in range(2):
                c0 = fg * 512
                bcast(m2_row[:, c0:c0 + 512], mb, [("L2mrow", fg)], "mbv")
                bcast(r2_row[:, c0:c0 + 512], rbc, [("L2rrow",)], "rbcv")
                for d in range(DCH):
                    S.op("vector", lambda E, d=d, c0=c0: E.tensor_sub(
                        out=tmp[d % 2][:], in0=u[d][:, c0:c0 + 512], in1=mb[:]),
                        reads=[("u", d, fg), ("mbv",)], writes=[("lt", d % 2)])
                    S.op("vector", lambda E, d=d: E.tensor_mul(
                        out=tmp[d % 2][:], in0=tmp[d % 2][:], in1=rbc[:]),
                        reads=[("lt", d % 2), ("rbcv",)], writes=[("lt", d % 2)])
                    S.op("vector", lambda E, d=d, c0=c0: E.tensor_scalar(
                        out=h[d][:, c0:c0 + 512], in0=tmp[d % 2][:],
                        scalar1=gam_sb[:, d:d + 1], scalar2=bet_sb[:, d:d + 1],
                        op0=ALU.mult, op1=ALU.add),
                        reads=[("lt", d % 2), ("cgam",), ("cbet",)],
                        writes=[("h", d, fg)])
            for t in range(4):
                c0 = t * 256
                gc0 = tq0 + c0
                for f in range(32):
                    pp = f % 2
                    S.op("tensor", lambda E, f=f, pp=pp, c0=c0: [
                        E.matmul(ps_y[pp][:], w1[d][:, f * 128:(f + 1) * 128],
                                 h[d][:, c0:c0 + 256],
                                 start=(d == 0), stop=(d == DCH - 1))
                        for d in range(DCH)][-1],
                        reads=[("w1", d) for d in range(DCH)]
                        + [("h", d, c0 // 512) for d in range(DCH)]
                        + [("psyr", pp)],
                        writes=[("psy", pp)])
                    S.op("scalar", lambda E, f=f, pp=pp: E.activation(
                        out=y1[f][:], in_=ps_y[pp][:], func=AF.Silu,
                        bias=b1_sb[:, f:f + 1], scale=1.0),
                        reads=[("psy", pp), ("cb1",)],
                        writes=[("y1", f), ("psyr", pp)])
                bcast(m1_row[:, gc0:gc0 + 256], mb, [("L1mrow", gc0 // 512)],
                      "mbv")
                bcast(r1_row[:, gc0:gc0 + 256], rbc, [("L1rrow",)], "rbcv")
                for ho in range(8):
                    pz = ho % 2
                    S.dma("sync",
                          w2sl[ho % 2][:].rearrange("p (c h) -> p c h", h=128),
                          w2s[:, ho * 128:(ho + 1) * 128]
                          .rearrange("(c p) h -> p c h", p=128),
                          reads=[("w2s", f) for f in range(32)],
                          writes=[("w2sl", ho % 2)], slot=f"wsl{ho % 2}")
                    S.op("tensor", lambda E, ho=ho, pz=pz: [
                        E.matmul(ps_z[pz][:],
                                 w2sl[ho % 2][:, f * 128:(f + 1) * 128],
                                 y1[f][:], start=(f == 0), stop=(f == 31))
                        for f in range(32)][-1],
                        reads=[("w2sl", ho % 2)]
                        + [("y1", f) for f in range(32)] + [("pszr", pz)],
                        writes=[("psz", pz)])
                    S.op("scalar", lambda E, ho=ho, pz=pz: E.activation(
                        out=zb[:], in_=ps_z[pz][:], func=AF.Identity,
                        bias=b2_sb[:, ho:ho + 1], scale=1.0),
                        reads=[("psz", pz), ("cb2",), ("zbr",)],
                        writes=[("zb",), ("pszr", pz)])
                    S.dma("sync", xs[ho % 4][:, 0:256],
                          x[ho * 128:(ho + 1) * 128, gc0:gc0 + 256],
                          writes=[("xs", ho % 4)], slot=f"xs{ho % 4}")
                    S.op("vector", lambda E, ho=ho: E.tensor_sub(
                        out=usl[:], in0=xs[ho % 4][:, 0:256], in1=mb[:, 0:256]),
                        reads=[("xs", ho % 4), ("mbv",)], writes=[("usl",)])
                    S.op("vector", lambda E: E.tensor_mul(
                        out=usl[:], in0=usl[:], in1=rbc[:, 0:256]),
                        reads=[("rbcv",)], writes=[("usl",)])
                    S.op("vector", lambda E, ho=ho: E.scalar_tensor_tensor(
                        out=o1[:], in0=usl[:], scalar=0.5,
                        in1=xs[ho % 4][:, 0:256],
                        op0=ALU.mult, op1=ALU.add),
                        reads=[("usl",), ("xs", ho % 4)], writes=[("o1",)])
                    si = (t * 8 + ho) % 4
                    S.op("vector", lambda E, si=si: E.scalar_tensor_tensor(
                        out=ost[si][:], in0=zb[:], scalar=0.25, in1=o1[:],
                        op0=ALU.mult, op1=ALU.add),
                        reads=[("zb",), ("o1",), ("ostr", si)],
                        writes=[("ost", si), ("zbr",)])
                    S.dma("gpsimd",
                          y[ho * 128:(ho + 1) * 128, gc0:gc0 + 256],
                          ost[si][:],
                          reads=[("ost", si)],
                          writes=[(tag + "yout", ho, qt * 4 + t),
                                  ("ostr", si)],
                          slot=f"yo{ho % 4}")
        barrier(S)



def build_qkv0(rope=True):
    nc = bass.Bass("TRN2", target_bir_lowering=False)
    io = {}
    io["x"] = nc.dram_tensor("x", [D, T], F32, kind="ExternalInput").ap()
    io["qkvw"] = nc.dram_tensor("qkvw", [D, 3072], F32, kind="ExternalInput").ap()
    io["qkvb"] = nc.dram_tensor("qkvb", [128, 24], F32, kind="ExternalInput").ap()
    io["onesA"] = nc.dram_tensor("onesA", [128, 1], F32, kind="ExternalInput").ap()
    io["onesB"] = nc.dram_tensor("onesB", [1, 128], F32, kind="ExternalInput").ap()
    io["fconst"] = nc.dram_tensor("fconst", [128, len(CONST_VALS)], F32,
                                  kind="ExternalInput").ap()
    io["qkvbp"] = nc.dram_tensor("qkvbp", [128, 24], F32,
                                 kind="ExternalInput").ap()
    io["cost"] = nc.dram_tensor("cost", [128, T], F32, kind="ExternalInput").ap()
    io["sint"] = nc.dram_tensor("sint", [128, T], F32, kind="ExternalInput").ap()
    io["qro"] = nc.dram_tensor("qro", [8, 128, T], BF16, kind="ExternalOutput").ap()
    io["kro"] = nc.dram_tensor("kro", [8, 128, T], BF16, kind="ExternalOutput").ap()
    io["vtm"] = nc.dram_tensor("vtm", [8, 128, T], BF16, kind="ExternalOutput").ap()
    with ExitStack() as ctx:
        S = Sched(nc, ctx)
        emit_qkv0(S, nc, ctx, io, tag="a0q")
        S.op("gpsimd", lambda E: E.nop(nofuse=True),
             reads=[("a0qout", w, p) for w in range(2) for p in range(8)]
             + [("a0qoutv", p) for p in range(8)])
        S.emit()
    return nc


def emit_qkv0(S, nc, ctx, io, tag, rope=True, post_qk=None):
    x, qkvw, qkvb = io["x"], io["qkvw"], io["qkvb"]
    onesA, onesB, fconst = io["onesA"], io["onesB"], io["fconst"]
    qro, kro, vtm = io["qro"], io["kro"], io["vtm"]
    if rope:
        qkvbp, cost, sint = io["qkvbp"], io["cost"], io["sint"]
    if True:
        C = {}
        C["onesA"] = sbuf(ctx, nc, "onesA", [128, 1], F32)
        C["onesB"] = sbuf(ctx, nc, "onesB", [1, 128], F32)
        C["fc"] = sbuf(ctx, nc, "fc", [128, len(CONST_VALS)], F32)
        b_sb = sbuf(ctx, nc, "b", [128, 24], F32)
        for t_, src, key in ((C["onesA"], onesA, "cA"), (C["onesB"], onesB, "cB"),
                             (C["fc"], fconst, "cfc"), (b_sb, qkvb, "cb")):
            S.dma("sync", t_[:], src, writes=[(key,)], slot="c")
        if rope:
            bp_sb = sbuf(ctx, nc, "bp", [128, 24], F32)
            cos_sb = sbuf(ctx, nc, "cos", [128, T], F32)
            sin_sb = sbuf(ctx, nc, "sin", [128, T], F32)
            S.dma("sync", bp_sb[:], qkvbp, writes=[("cbp",)], slot="c")
            S.dma("sync", cos_sb[:], cost, writes=[("ccos",)], slot="c")
            S.dma("sync", sin_sb[:], sint, writes=[("csin",)], slot="c")
        xln = [sbuf(ctx, nc, f"xln{d}", [128, T], BF16) for d in range(DCH)]
        m_row = sbuf(ctx, nc, "mrow", [1, T], F32)
        r_row = sbuf(ctx, nc, "rrow", [1, T], F32)

        xs = [sbuf(ctx, nc, f"xs{i}", [128, 512], F32) for i in range(4)]
        sq = [sbuf(ctx, nc, f"sq{i}", [128, 512], F32) for i in range(2)]
        ps_sx = [psum(ctx, nc, "psx", [1, 512])]
        ps_sq = [psum(ctx, nc, "psq", [1, 512])]
        emit_ln_stats_generic(S, nc, C, "L1", m_row, r_row, T, xs, sq,
                              ps_sx, ps_sq,
                              lambda d, fg, k: dma_tile(S, xs, k, x, d, fg))
        mb = sbuf(ctx, nc, "mb", [128, 512], F32)
        rbc = sbuf(ctx, nc, "rbc", [128, 512], F32)
        pbc = psum(ctx, nc, "pbc", [128, 512])
        tmp = [sbuf(ctx, nc, f"lt{i}", [128, 512], F32) for i in range(2)]

        def bcast(row_ap, dst, rkeys, dkey):
            nn = row_ap.shape[-1]
            S.op("tensor", lambda E, nn=nn, row_ap=row_ap: E.matmul(
                pbc[:, 0:nn], C["onesB"][:], row_ap, start=True, stop=True),
                reads=list(rkeys) + [("cB",), (dkey,)], writes=[("pbcq",)])
            S.op("vector", lambda E, nn=nn, dst=dst: E.tensor_copy(
                out=dst[:, 0:nn], in_=pbc[:, 0:nn]),
                reads=[("pbcq",)], writes=[(dkey,)])
        for fg in range(T // 512):
            c0 = fg * 512
            bcast(m_row[:, c0:c0 + 512], mb, [("L1mrow", fg)], "mbv")
            bcast(r_row[:, c0:c0 + 512], rbc, [("L1rrow",)], "rbcv")
            for d in range(DCH):
                S.dma("sync", xs[d % 4][:], x[d * 128:(d + 1) * 128, c0:c0 + 512],
                      writes=[("xs", d % 4)], slot=f"xs{d % 4}")
                S.op("vector", lambda E, d=d: E.tensor_sub(
                    out=tmp[d % 2][:], in0=xs[d % 4][:], in1=mb[:]),
                    reads=[("xs", d % 4), ("mbv",)], writes=[("lt", d % 2)])
                S.op("vector", lambda E, d=d, c0=c0: E.tensor_mul(
                    out=xln[d][:, c0:c0 + 512], in0=tmp[d % 2][:], in1=rbc[:]),
                    reads=[("lt", d % 2), ("rbcv",)], writes=[("xln", d, fg)])
        barrier(S)

        wsl = [sbuf(ctx, nc, f"wsl{i}", [128, 1024], BF16) for i in range(3)]
        wpm = sbuf(ctx, nc, "wpm", [128, 1024], BF16)
        ps_q = psum(ctx, nc, "ps_q", [128, 512])
        ps_qp = psum(ctx, nc, "ps_qp", [128, 512])
        ps_v = psum(ctx, nc, "ps_v", [128, 128])
        t1 = sbuf(ctx, nc, "t1", [128, 512], F32)
        t2 = sbuf(ctx, nc, "t2", [128, 512], F32)
        t3 = sbuf(ctx, nc, "t3", [128, 512], F32)
        qout = sbuf(ctx, nc, "qout", [128, T], BF16)
        vout = sbuf(ctx, nc, "vout", [128, T], BF16)

        def load_w(c, sl, key):
            S.dma("gpsimd", wsl[sl][:].rearrange("p (d c) -> p d c", c=128),
                  qkvw[:, c * 128:(c + 1) * 128]
                  .rearrange("(d p) c -> p d c", p=128),
                  writes=[key], slot=f"w{sl}")

        for p in range(8):
            for which, cbase, outdram in ((0, p, qro), (1, 8 + p, kro)):
                load_w(cbase, which, ("wq", which))
                pass
                if rope:
                    S.op("vector", lambda E, which=which: E.tensor_copy(
                        out=wpm[:].rearrange("p (a c) -> p a c", c=64)[:, :, 0:32],
                        in_=wsl[which][:]
                        .rearrange("p (a c) -> p a c", c=64)[:, :, 32:64]),
                        reads=[("wq", which)], writes=[("wpm",)])
                    S.op("vector", lambda E, which=which: E.tensor_copy(
                        out=wpm[:].rearrange("p (a c) -> p a c", c=64)[:, :, 32:64],
                        in_=wsl[which][:]
                        .rearrange("p (a c) -> p a c", c=64)[:, :, 0:32]),
                        reads=[("wq", which)], writes=[("wpm",)])
                for tg in range(8):
                    c0 = tg * 512
                    S.op("tensor", lambda E, which=which, c0=c0: [
                        E.matmul(ps_q[:], wsl[which][:, d * 128:(d + 1) * 128],
                                 xln[d][:, c0:c0 + 512],
                                 start=(d == 0), stop=(d == DCH - 1))
                        for d in range(DCH)][-1],
                        reads=[("wq", which)]
                        + [("xln", d, c0 // 512) for d in range(DCH)]
                        + [("psqr",)],
                        writes=[("psq",)])
                    if rope:
                        S.op("tensor", lambda E, c0=c0: [
                            E.matmul(ps_qp[:], wpm[:, d * 128:(d + 1) * 128],
                                     xln[d][:, c0:c0 + 512],
                                     start=(d == 0), stop=(d == DCH - 1))
                            for d in range(DCH)][-1],
                            reads=[("wpm",)]
                            + [("xln", d, c0 // 512) for d in range(DCH)]
                            + [("psqpr",)],
                            writes=[("psqp",)])
                        S.op("scalar", lambda E, cbase=cbase: E.activation(
                            out=t1[:], in_=ps_q[:], func=AF.Identity,
                            bias=b_sb[:, cbase:cbase + 1], scale=1.0),
                            reads=[("psq",), ("cb",)],
                            writes=[("t1",), ("psqr",)])
                        S.op("scalar", lambda E, cbase=cbase: E.activation(
                            out=t2[:], in_=ps_qp[:], func=AF.Identity,
                            bias=bp_sb[:, cbase:cbase + 1], scale=1.0),
                            reads=[("psqp",), ("cbp",)],
                            writes=[("t2",), ("psqpr",)])
                        S.op("vector", lambda E, c0=c0: E.tensor_mul(
                            out=t3[:], in0=t1[:], in1=cos_sb[:, c0:c0 + 512]),
                            reads=[("t1",), ("ccos",)], writes=[("t3",)])
                        S.op("vector", lambda E, c0=c0: E.tensor_mul(
                            out=t2[:], in0=t2[:], in1=sin_sb[:, c0:c0 + 512]),
                            reads=[("t2",), ("csin",)], writes=[("t2",)])
                        S.op("vector", lambda E, c0=c0: E.tensor_add(
                            out=qout[:, c0:c0 + 512], in0=t3[:], in1=t2[:]),
                            reads=[("t3",), ("t2",)], writes=[("qo", c0 // 512)])
                    else:
                        S.op("scalar", lambda E, cbase=cbase, c0=c0:
                             E.activation(
                                 out=qout[:, c0:c0 + 512], in_=ps_q[:],
                                 func=AF.Identity,
                                 bias=b_sb[:, cbase:cbase + 1], scale=1.0),
                             reads=[("psq",), ("cb",)],
                             writes=[("qo", c0 // 512), ("psqr",)])
                S.dma("sync", outdram[p], qout[:],
                      reads=[("qo", i) for i in range(8)],
                      writes=[(tag + "out", which, p)], slot="o0")
        if post_qk is not None:
            post_qk()
        for p in range(8):
            load_w(16 + p, 2, ("wv",))
            for tt in range(32):
                S.op("tensor", lambda E, tt=tt: [
                    E.matmul(ps_v[:], xln[d][:, tt * 128:(tt + 1) * 128],
                             wsl[2][:, d * 128:(d + 1) * 128],
                             start=(d == 0), stop=(d == DCH - 1))
                    for d in range(DCH)][-1],
                    reads=[("wv",)]
                    + [("xln", d, tt // 4) for d in range(DCH)] + [("psvr",)],
                    writes=[("psv",)])
                S.op("scalar", lambda E, p=p, tt=tt: E.activation(
                    out=vout[:, tt * 128:(tt + 1) * 128], in_=ps_v[:],
                    func=AF.Identity, bias=b_sb[:, 16 + p:17 + p], scale=1.0),
                    reads=[("psv",), ("cb",)],
                    writes=[("vo", tt), ("psvr",)])
            S.dma("sync", vtm[p], vout[:],
                  reads=[("vo", tt) for tt in range(32)],
                  writes=[(tag + "outv", p)], slot="o1")
        barrier(S)


def host_rope_tables():
    hd = 64
    inv = (1.0 / 10000.0 ** (np.arange(0, hd, 2, dtype=np.float32) / hd))
    pos = np.arange(T, dtype=np.float32)
    freqs = pos[:, None] * inv[None, :]
    emb = np.concatenate([freqs, freqs], axis=1)
    cos = np.cos(emb).T.astype(np.float32)
    sin = np.sin(emb).T.astype(np.float32)
    sgn = np.ones((64, 1), np.float32)
    sgn[0:32] = -1.0
    sins = sin * sgn
    return (np.concatenate([cos, cos], 0).copy(),
            np.concatenate([sins, sins], 0).copy())



def build_attn0core():
    nc = bass.Bass("TRN2", target_bir_lowering=False)
    io = {}
    io["qz"] = nc.dram_tensor("qz", [8, 128, T], BF16, kind="ExternalInput").ap()
    io["kz"] = nc.dram_tensor("kz", [8, 128, T], BF16, kind="ExternalInput").ap()
    io["vz"] = nc.dram_tensor("vz", [8, 128, T], BF16, kind="ExternalInput").ap()
    io["onesB"] = nc.dram_tensor("onesB", [1, 128], F32, kind="ExternalInput").ap()
    io["fconst"] = nc.dram_tensor("fconst", [128, len(CONST_VALS)], F32,
                                  kind="ExternalInput").ap()
    io["oz"] = nc.dram_tensor("oz", [8, 2, 64, T], BF16, kind="ExternalOutput").ap()
    with ExitStack() as ctx:
        S = Sched(nc, ctx)
        emit_attn0core(S, nc, ctx, io, tag="a0c")
        S.op("gpsimd", lambda E: E.nop(nofuse=True),
             reads=[("a0coz", b2, h) for b2 in range(8) for h in range(2)])
        S.emit()
    return nc


def emit_attn0core(S, nc, ctx, io, tag):
    qz, kz, vz = io["qz"], io["kz"], io["vz"]
    onesB, fconst, oz = io["onesB"], io["fconst"], io["oz"]
    if True:
        C = {}
        C["onesB"] = sbuf(ctx, nc, "onesB", [1, 128], F32)
        C["fc"] = sbuf(ctx, nc, "fc", [128, len(CONST_VALS)], F32)
        S.dma("sync", C["onesB"][:], onesB, writes=[("cB",)], slot="c")
        S.dma("sync", C["fc"][:], fconst, writes=[("cfc",)], slot="c")
        q_sb = [sbuf(ctx, nc, f"q{i}", [128, T], BF16) for i in range(2)]
        k_sb = [sbuf(ctx, nc, f"k{i}", [128, T], BF16) for i in range(2)]
        va_sb = [sbuf(ctx, nc, f"va{i}", [128, 65 * 32], BF16) for i in range(2)]
        vt_sb = [sbuf(ctx, nc, f"vt{i}", [128, T], BF16) for i in range(2)]
        ps_s = psum(ctx, nc, "ps_s", [128, 1024])
        ps_o = [psum(ctx, nc, f"ps_o{i}", [128, 512]) for i in range(2)]
        pt = sbuf(ctx, nc, "pt", [128, 1024], BF16)
        dinv = sbuf(ctx, nc, "dinv", [1, 512], F32)
        dbc = sbuf(ctx, nc, "dbc", [128, 512], F32)
        pbc = psum(ctx, nc, "pbc", [128, 512])
        ob = sbuf(ctx, nc, "ob", [64, T], BF16)

        for b2 in range(8):
            sl = b2 % 2
            S.dma("sync", q_sb[sl][:], qz[b2], writes=[("q", sl)], slot=f"q{sl}")
            S.dma("sync", k_sb[sl][:], kz[b2], writes=[("k", sl)], slot=f"k{sl}")
            S.dma("sync", vt_sb[sl][:], vz[b2],
                  writes=[("vt", sl)], slot=f"v{sl}")
            for h in range(2):
                hb = 64 * h
                S.op("vector", lambda E, sl=sl: E.memset(va_sb[sl][:], 1.0),
                     reads=[("var", sl)], writes=[("va", sl)])
                S.op("vector", lambda E, sl=sl, hb=hb: E.tensor_copy(
                    out=va_sb[sl][:].rearrange("p (tt c) -> p tt c", c=65)
                    [:, :, 0:64],
                    in_=vt_sb[sl][:].rearrange("p (tt c) -> p tt c", c=128)
                    [:, :, hb:hb + 64]),
                    reads=[("vt", sl)], writes=[("va", sl)])
                for qg in range(8):
                    t0 = qg * 512
                    for kc2 in range(16):
                        S.op("tensor", lambda E, sl=sl, hb=hb, kc2=kc2, t0=t0: [
                            E.matmul(ps_s[:, i * 512:(i + 1) * 512],
                                     k_sb[sl][hb:hb + 64,
                                              (kc2 * 2 + i) * 128:
                                              (kc2 * 2 + i + 1) * 128],
                                     q_sb[sl][hb:hb + 64, t0:t0 + 512],
                                     start=True, stop=True)
                            for i in range(2)][-1],
                            reads=[("k", sl), ("q", sl), ("pssr",)],
                            writes=[("pss",)])
                        S.op("scalar", lambda E: E.activation(
                            out=pt[:], in_=ps_s[:], func=AF.Exp,
                            bias=FC(C, "mtwenty"), scale=FC(C, "eighth")),
                            reads=[("pss",), ("cfc",)],
                            writes=[("pt",), ("pssr",)])
                        S.op("tensor", lambda E, sl=sl, kc2=kc2, qg=qg: [
                            E.matmul(ps_o[qg % 2][0:65, :],
                                     va_sb[sl][:, (kc2 * 2 + i) * 65:
                                               (kc2 * 2 + i + 1) * 65],
                                     pt[:, i * 512:(i + 1) * 512],
                                     start=(kc2 == 0 and i == 0),
                                     stop=(kc2 == 15 and i == 1))
                            for i in range(2)][-1],
                            reads=[("va", sl), ("pt",)],
                            writes=[("pso", qg % 2)])
                    S.op("scalar", lambda E, qg=qg: E.activation(
                        out=dinv[:], in_=ps_o[qg % 2][64:65, :], func=AF.Ln),
                        reads=[("pso", qg % 2)], writes=[("dinv",)])
                    S.op("scalar", lambda E: E.activation(
                        out=dinv[:], in_=dinv[:], func=AF.Exp, bias=0.0,
                        scale=FC(C, "mone", 1)),
                        reads=[("dinv",)], writes=[("dinv",)])
                    S.op("tensor", lambda E: E.matmul(
                        pbc[:], C["onesB"][:], dinv[:], start=True, stop=True),
                        reads=[("dinv",), ("cB",), ("dbcv",)],
                        writes=[("pbc",)])
                    S.op("vector", lambda E: E.tensor_copy(out=dbc[:],
                                                           in_=pbc[:]),
                         reads=[("pbc",)], writes=[("dbcv",)])
                    S.op("vector", lambda E, qg=qg, t0=t0: E.tensor_mul(
                        out=ob[:, t0:t0 + 512], in0=ps_o[qg % 2][0:64, :],
                        in1=dbc[0:64, :]),
                        reads=[("pso", qg % 2), ("dbcv",)],
                        writes=[("ob", qg)])
                S.dma("sync", oz[b2, h], ob[:],
                      reads=[("ob", qg) for qg in range(8)],
                      writes=[(tag + "oz", b2, h), ("var", sl)], slot="oz")
        barrier(S)



def build_attn0proj():
    nc = bass.Bass("TRN2", target_bir_lowering=False)
    io = {}
    io["sc"] = nc.dram_tensor("sc", [D, T], BF16, kind="ExternalInput").ap()
    io["x"] = nc.dram_tensor("x", [D, T], F32, kind="ExternalInput").ap()
    io["pw"] = nc.dram_tensor("pw", [D, D], F32, kind="ExternalInput").ap()
    io["pb"] = nc.dram_tensor("pb", [128, 8], F32, kind="ExternalInput").ap()
    io["y"] = nc.dram_tensor("y", [D, T], F32, kind="ExternalOutput").ap()
    with ExitStack() as ctx:
        S = Sched(nc, ctx)
        emit_attn0proj(S, nc, ctx, io, tag="a0p")
        S.op("gpsimd", lambda E: E.nop(nofuse=True),
             reads=[("a0py", ho, tg) for ho in range(8) for tg in range(8)])
        S.emit()
    return nc


def emit_attn0proj(S, nc, ctx, io, tag, sc_sb=None):
    sc, x, pw, pb, y = io["sc"], io["x"], io["pw"], io["pb"], io["y"]
    if True:
        pb_sb = sbuf(ctx, nc, "pb", [128, 8], F32)
        S.dma("sync", pb_sb[:], pb, writes=[("cpb",)], slot="c")
        w_sb = [sbuf(ctx, nc, f"w{d}", [128, 1024], BF16) for d in range(8)]
        for d in range(8):
            S.dma("gpsimd", w_sb[d][:], pw[d * 128:(d + 1) * 128, :],
                  writes=[("w", d)], slot=f"w{d % 2}")
        scb = [sbuf(ctx, nc, f"sc{d}", [128, 512], BF16) for d in range(8)]
        xsl = [sbuf(ctx, nc, f"xs{i}", [128, 512], F32) for i in range(2)]
        ys = [sbuf(ctx, nc, f"ys{i}", [128, 512], F32) for i in range(4)]
        ps = [psum(ctx, nc, f"ps{i}", [128, 512]) for i in range(2)]
        for tg in range(8):
            t0 = tg * 512
            if sc_sb is None:
                for d in range(8):
                    S.dma("sync", scb[d][:],
                          sc[d * 128:(d + 1) * 128, t0:t0 + 512],
                          writes=[("sc", d)], slot=f"sc{d % 4}")
            for ho in range(8):
                pp = ho % 2
                S.op("tensor", lambda E, ho=ho, pp=pp: [
                    E.matmul(ps[pp][:], w_sb[d][:, ho * 128:(ho + 1) * 128],
                             scb[d][:], start=(d == 0), stop=(d == 7))
                    for d in range(8)][-1],
                    reads=[("w", d) for d in range(8)]
                    + [("sc", d) for d in range(8)] + [("psr", pp)],
                    writes=[("ps", pp)])
                S.dma("sync", xsl[pp][:], x[ho * 128:(ho + 1) * 128,
                                            t0:t0 + 512],
                      reads=[("xslr", pp)],
                      writes=[("xsl", pp)], slot=f"x{pp}")
                S.op("scalar", lambda E, ho=ho, pp=pp: E.activation(
                    out=ys[ho % 4][:], in_=ps[pp][:], func=AF.Identity,
                    bias=pb_sb[:, ho:ho + 1], scale=1.0),
                    reads=[("ps", pp), ("cpb",), ("ysr", ho % 4)],
                    writes=[("ys", ho % 4), ("psr", pp)])
                S.op("vector", lambda E, ho=ho, pp=pp: E.tensor_add(
                    out=ys[ho % 4][:], in0=ys[ho % 4][:], in1=xsl[pp][:]),
                    reads=[("xsl", pp), ("ys", ho % 4)],
                    writes=[("ys", ho % 4), ("xslr", pp)])
                S.dma("gpsimd", y[ho * 128:(ho + 1) * 128, t0:t0 + 512],
                      ys[ho % 4][:],
                      reads=[("ys", ho % 4)],
                      writes=[(tag + "y", ho, tg), ("ysr", ho % 4)],
                      slot=f"y{ho % 4}")
        barrier(S)



def build_attn1():
    nc = bass.Bass("TRN2", target_bir_lowering=False)
    io = {}
    io["x"] = nc.dram_tensor("x", [D, T], F32, kind="ExternalInput").ap()
    io["qkvw"] = nc.dram_tensor("qkvw", [D, 3072], F32, kind="ExternalInput").ap()
    io["qkvb"] = nc.dram_tensor("qkvb", [128, 24], F32, kind="ExternalInput").ap()
    io["pw"] = nc.dram_tensor("pw", [D, D], F32, kind="ExternalInput").ap()
    io["pb"] = nc.dram_tensor("pb", [128, 8], F32, kind="ExternalInput").ap()
    io["onesA"] = nc.dram_tensor("onesA", [128, 1], F32, kind="ExternalInput").ap()
    io["onesB"] = nc.dram_tensor("onesB", [1, 128], F32, kind="ExternalInput").ap()
    io["fconst"] = nc.dram_tensor("fconst", [128, len(CONST_VALS)], F32,
                                  kind="ExternalInput").ap()
    io["iot"] = nc.dram_tensor("iot", [128, T], F32, kind="ExternalInput").ap()
    io["iop"] = nc.dram_tensor("iop", [128, 1], F32, kind="ExternalInput").ap()
    io["tri1"] = nc.dram_tensor("tri1", [128, 256], F32, kind="ExternalInput").ap()
    io["tri2"] = nc.dram_tensor("tri2", [128, 256], F32, kind="ExternalInput").ap()
    io["tri1c"] = nc.dram_tensor("tri1c", [128, 256], BF16,
                                 kind="ExternalInput").ap()
    io["tri2c"] = nc.dram_tensor("tri2c", [128, 256], BF16,
                                 kind="ExternalInput").ap()
    io["emask"] = nc.dram_tensor("emask", [128, 1], I32, kind="ExternalInput").ap()
    io["y"] = nc.dram_tensor("y", [D, T], F32, kind="ExternalOutput").ap()
    io["qs"] = nc.dram_tensor("qs", [8, 128, T], BF16, kind="Internal").ap()
    io["ks"] = nc.dram_tensor("ks", [8, 128, T], BF16, kind="Internal").ap()
    io["vas"] = nc.dram_tensor("vas", [8, 2, 128, 65 * 32], BF16,
                               kind="Internal").ap()
    with ExitStack() as ctx:
        S = Sched(nc, ctx)
        emit_attn1(S, nc, ctx, io, tag="a1")
        S.op("gpsimd", lambda E: E.nop(nofuse=True),
             reads=[("a1y", ho, g) for ho in range(8) for g in range(16)])
        S.emit()
    return nc


def emit_attn1(S, nc, ctx, io, tag):
    x, qkvw, qkvb, pw, pb = (io["x"], io["qkvw"], io["qkvb"], io["pw"],
                             io["pb"])
    onesA, onesB, fconst = io["onesA"], io["onesB"], io["fconst"]
    iot, iop, tri1, tri2 = io["iot"], io["iop"], io["tri1"], io["tri2"]
    tri1c, tri2c, emask, y = io["tri1c"], io["tri2c"], io["emask"], io["y"]
    qs, ks, vas = io["qs"], io["ks"], io["vas"]
    NQG = 16
    BUDGET = 818.0
    if True:
        C = {}
        C["onesA"] = sbuf(ctx, nc, "onesA", [128, 1], F32)
        C["onesB"] = sbuf(ctx, nc, "onesB", [1, 128], F32)
        C["fc"] = sbuf(ctx, nc, "fc", [128, len(CONST_VALS)], F32)
        b_sb = sbuf(ctx, nc, "b", [128, 24], F32)
        pb_sb = sbuf(ctx, nc, "pb", [128, 8], F32)
        iop_sb = sbuf(ctx, nc, "iop", [128, 1], F32)
        tri1_sb = sbuf(ctx, nc, "tri1", [128, 256], F32)
        tri2_sb = sbuf(ctx, nc, "tri2", [128, 256], F32)
        tri1c_sb = sbuf(ctx, nc, "tri1c", [128, 256], BF16)
        tri2c_sb = sbuf(ctx, nc, "tri2c", [128, 256], BF16)
        em_sb = sbuf(ctx, nc, "em", [128, 1], I32)
        iota_sb = sbuf(ctx, nc, "iota", [128, T], F32)
        for t_, src, key in ((C["onesA"], onesA, "cA"), (C["onesB"], onesB, "cB"),
                             (C["fc"], fconst, "cfc"), (b_sb, qkvb, "cb"),
                             (pb_sb, pb, "cpb"), (iop_sb, iop, "ciop"),
                             (tri1_sb, tri1, "ct1"), (tri2_sb, tri2, "ct2"),
                             (tri1c_sb, tri1c, "ct1c"), (tri2c_sb, tri2c, "ct2c"),
                             (em_sb, emask, "cem"), (iota_sb, iot, "cio")):
            S.dma("sync", t_[:], src, writes=[(key,)], slot="c")

        A1 = sbuf(ctx, nc, "A1", [128, 16384], F32)
        fft = sbuf(ctx, nc, "fft", [128, 32 * 256], F32)
        qout = fft[:, 0:2048].bitcast(BF16)
        vout = fft[:, 2048:4096].bitcast(BF16)
        xln = [A1[:, d * 2048:(d + 1) * 2048].bitcast(BF16) for d in range(DCH)]
        zb_ = A1[:, 0:4096]
        b1f = A1[:, 4096:8192]
        b2f = A1[:, 8192:12288]
        jnk = A1[:, 8192:12288].bitcast(BF16)[:, 0:4096]
        causb = jnk
        pwsb = [A1[:, 12288 + d * 512:12288 + (d + 1) * 512].bitcast(BF16)
                for d in range(8)]
        fftile = sbuf(ctx, nc, "fftile", [128, T], F32)
        m_row = fftile[0:1, :]
        r_row = fft[0:1, 0:4096]
        kp = [sbuf(ctx, nc, f"kp{i}", [128, T], BF16) for i in range(2)]
        mb = kp[0][:].bitcast(F32)[:, 0:512]
        rbc = kp[0][:].bitcast(F32)[:, 512:1024]

        xs = [sbuf(ctx, nc, f"xs{i}", [128, 512], F32) for i in range(2)]
        sq = [sbuf(ctx, nc, f"sq{i}", [128, 512], F32) for i in range(2)]
        ps_sx = [psum(ctx, nc, "psx", [1, 512])]
        ps_sq = [psum(ctx, nc, "psq", [1, 512])]
        pbc = psum(ctx, nc, "pbc", [128, 512])
        tmp = [sbuf(ctx, nc, f"lt{i}", [128, 512], F32) for i in range(2)]

        def get_x_tile(d, fg, k):
            xb = xs[k % 2]
            S.dma("sync", xb[:], x[d * 128:(d + 1) * 128,
                                   fg * 512:(fg + 1) * 512],
                  writes=[("xs", k % 2)], slot=f"xs{k % 2}")
            return xb[:], [("xs", k % 2)]
        emit_ln_stats_generic(S, nc, C, "L1", m_row, r_row, T, xs, sq,
                              ps_sx, ps_sq, get_x_tile)

        def bcast(row_ap, dst, rkeys, dkey):
            nn = row_ap.shape[-1]
            S.op("tensor", lambda E, nn=nn, row_ap=row_ap: E.matmul(
                pbc[:, 0:nn], C["onesB"][:], row_ap, start=True, stop=True),
                reads=list(rkeys) + [("cB",), (dkey,)], writes=[("pbcq",)])
            S.op("vector", lambda E, nn=nn, dst=dst: E.tensor_copy(
                out=dst[:, 0:nn], in_=pbc[:, 0:nn]),
                reads=[("pbcq",)], writes=[(dkey,)])
        for fg in range(T // 512):
            c0 = fg * 512
            bcast(m_row[:, c0:c0 + 512], mb, [("L1mrow", fg)], "mbv")
            bcast(r_row[:, c0:c0 + 512], rbc, [("L1rrow",)], "rbcv")
            for d in range(DCH):
                S.dma("sync", xs[d % 2][:], x[d * 128:(d + 1) * 128, c0:c0 + 512],
                      writes=[("xs", d % 2)], slot=f"xs{d % 2}")
                S.op("vector", lambda E, d=d: E.tensor_sub(
                    out=tmp[d % 2][:], in0=xs[d % 2][:], in1=mb[:]),
                    reads=[("xs", d % 2), ("mbv",)], writes=[("lt", d % 2)])
                S.op("vector", lambda E, d=d, c0=c0: E.tensor_mul(
                    out=xln[d][:, c0:c0 + 512], in0=tmp[d % 2][:], in1=rbc[:]),
                    reads=[("lt", d % 2), ("rbcv",)],
                    writes=[("xln", d, fg)])

        barrier(S)

        wsl = [sbuf(ctx, nc, f"wsl{i}", [128, 1024], BF16) for i in range(3)]
        ps_q = psum(ctx, nc, "ps_q", [128, 512])
        ps_v = psum(ctx, nc, "ps_v", [128, 128])
        va_sb = sbuf(ctx, nc, "va", [128, 65 * 32], BF16)

        def load_w(c, sl, key):
            S.dma("gpsimd", wsl[sl][:].rearrange("p (d c) -> p d c", c=128),
                  qkvw[:, c * 128:(c + 1) * 128]
                  .rearrange("(d p) c -> p d c", p=128),
                  writes=[key], slot=f"w{sl}")
        for p in range(8):
            for which, cbase, outdram in ((0, p, qs), (1, 8 + p, ks)):
                load_w(cbase, which, ("wq", which))
                for tg in range(8):
                    c0 = tg * 512
                    S.op("tensor", lambda E, which=which, c0=c0: [
                        E.matmul(ps_q[:], wsl[which][:, d * 128:(d + 1) * 128],
                                 xln[d][:, c0:c0 + 512],
                                 start=(d == 0), stop=(d == DCH - 1))
                        for d in range(DCH)][-1],
                        reads=[("wq", which)]
                        + [("xln", d, c0 // 512) for d in range(DCH)]
                        + [("psqr",)],
                        writes=[("psq",)])
                    S.op("scalar", lambda E, cbase=cbase, c0=c0: E.activation(
                        out=qout[:, c0:c0 + 512], in_=ps_q[:], func=AF.Identity,
                        bias=b_sb[:, cbase:cbase + 1], scale=1.0),
                        reads=[("psq",), ("cb",)],
                        writes=[("qo", c0 // 512), ("psqr",)])
                S.dma("sync", outdram[p], qout[:],
                      reads=[("qo", i) for i in range(8)],
                      writes=[("hbq", which, p)], slot="o0")
            load_w(16 + p, 2, ("wv",))
            for tt in range(32):
                S.op("tensor", lambda E, tt=tt: [
                    E.matmul(ps_v[:], xln[d][:, tt * 128:(tt + 1) * 128],
                             wsl[2][:, d * 128:(d + 1) * 128],
                             start=(d == 0), stop=(d == DCH - 1))
                    for d in range(DCH)][-1],
                    reads=[("wv",)]
                    + [("xln", d, tt // 4) for d in range(DCH)]
                    + [("psvr",)],
                    writes=[("psv",)])
                S.op("scalar", lambda E, p=p, tt=tt: E.activation(
                    out=vout[:, tt * 128:(tt + 1) * 128], in_=ps_v[:],
                    func=AF.Identity, bias=b_sb[:, 16 + p:17 + p], scale=1.0),
                    reads=[("psv",), ("cb",)],
                    writes=[("vo", tt), ("psvr",)])
            for hh in range(2):
                S.op("vector", lambda E: E.memset(va_sb[:], 1.0),
                     reads=[("var",)], writes=[("va",)])
                S.op("vector", lambda E, hh=hh: E.tensor_copy(
                    out=va_sb[:].rearrange("p (tt c) -> p tt c", c=65)
                    [:, :, 0:64],
                    in_=vout[:].rearrange("p (tt c) -> p tt c", c=128)
                    [:, :, 64 * hh:64 * hh + 64]),
                    reads=[("vo", tt) for tt in range(32)],
                    writes=[("va",)]),
                S.dma("sync", vas[p, hh], va_sb[:],
                      reads=[("va",)], writes=[("hbva", p, hh), ("var",)],
                      slot="o1")
        barrier(S)

        qpg = [sbuf(ctx, nc, f"qpg{i}", [128, 256], BF16) for i in range(2)]
        vap = [va_sb, sbuf(ctx, nc, "vap1", [128, 65 * 32], BF16)]
        ffe = sbuf(ctx, nc, "ffe", [128, 32 * 256], BF16)
        kp_sc = sbuf(ctx, nc, "kpsc", [128, 2048], BF16)
        qp_sc = sbuf(ctx, nc, "qpsc", [128, 256], BF16)
        s0b = sbuf(ctx, nc, "s0b", [128, 256], F32)
        carr = sbuf(ctx, nc, "carr", [128, 33], F32)
        og = [wsl[j][:, k * 256:(k + 1) * 256]
              for j in range(2) for k in range(4)]
        ps_o = [psum(ctx, nc, f"ps_o{i}", [128, 256]) for i in range(2)]
        ps_at1 = psum(ctx, nc, "ps_at1", [128, 512])
        ps_at = [ps_q[:], ps_at1[:]]
        ps_t = ps_v
        ptph = sbuf(ctx, nc, "ptph", [128, 2048], BF16)
        pt = [ptph[:, i * 512:(i + 1) * 512] for i in range(2)]
        phat = [ptph[:, 1024 + i * 512:1024 + (i + 1) * 512]
                for i in range(2)]
        dinv = sbuf(ctx, nc, "dinv", [1, 256], F32)
        dbc = sbuf(ctx, nc, "dbc", [128, 256], F32)
        xsl = sbuf(ctx, nc, "xsl", [128, 256], F32)
        ysl = [sbuf(ctx, nc, f"ysl{i}", [128, 256], F32) for i in range(2)]
        id_sb = sbuf(ctx, nc, "idm", [128, 128], F32)
        S.op("vector", lambda E: E.tensor_scalar(
            out=id_sb[:], in0=iota_sb[:, 0:128], scalar1=iop_sb[:],
            scalar2=None, op0=ALU.is_equal),
            reads=[("cio",), ("ciop",)], writes=[("cid",)])
        qcol = sbuf(ctx, nc, "qcol", [128, 1], F32)
        bcol = sbuf(ctx, nc, "bcol", [128, 1], F32)
        bm1 = sbuf(ctx, nc, "bm1", [128, 1], F32)
        rmax = sbuf(ctx, nc, "rmax", [128, 1], F32)
        cc = sbuf(ctx, nc, "cc", [128, 1], F32)
        c4 = sbuf(ctx, nc, "c4", [128, 1], F32)
        lo = sbuf(ctx, nc, "lo", [128, 1], F32)
        hi = sbuf(ctx, nc, "hi", [128, 1], F32)
        mid = sbuf(ctx, nc, "mid", [128, 1], F32)
        cnt = sbuf(ctx, nc, "cnt", [128, 1], F32)
        prd = sbuf(ctx, nc, "prd", [128, 1], mybir.dt.uint32)
        llo = sbuf(ctx, nc, "llo", [128, 1], F32)
        rrr = sbuf(ctx, nc, "rrr", [128, 1], F32)

        for d in range(8):
            S.dma("gpsimd", pwsb[d][:], pw[d * 128:(d + 1) * 128, :],
                  writes=[("pw", d)], slot="w0")
        S.op("vector", lambda E: E.memset(carr[:], 0.0), writes=[("carr",)])
        S.dma("sync", kp_sc[0:64, :], ks[0][0:64, 0:2048],
              writes=[("kpsc",)], slot="ksc")
        S.dma("sync", kp_sc[64:128, :], ks[0][0:64, 2048:4096],
              writes=[("kpsc",)], slot="ksc")

        def stage_X(g):
            t0 = g * 256
            nkc = 2 * g + 2
            buf = g % 2
            S.dma("sync", qp_sc[0:64, :], qs[0][0:64, t0:t0 + 256],
                  writes=[("qpsc",)], slot="qsc")
            S.dma("sync", qp_sc[64:128, :], qs[0][0:64, t0:t0 + 256],
                  writes=[("qpsc",)], slot="qsc")
            for kc in range(nkc):
                r0 = 64 * (kc // 16)
                c0s = (kc % 16) * 128
                for ih in range(2):
                    S.op("tensor", lambda E, r0=r0, c0s=c0s, ih=ih: E.matmul(
                        ps_t[:], kp_sc[r0:r0 + 64, c0s:c0s + 128],
                        qp_sc[r0:r0 + 64, ih * 128:(ih + 1) * 128],
                        start=True, stop=True),
                        reads=[("kpsc",), ("qpsc",), ("pst",)],
                        writes=[("pst",)])
                    S.op("vector", lambda E, ih=ih: E.tensor_scalar(
                        out=s0b[:, ih * 128:(ih + 1) * 128], in0=ps_t[:],
                        scalar1=0.125, scalar2=0.0,
                        op0=ALU.mult, op1=ALU.max),
                        reads=[("pst",), ("s0",)],
                        writes=[("s0",)])
                if kc == 2 * g:
                    S.op("vector", lambda E: E.tensor_mul(
                        out=s0b[:], in0=s0b[:], in1=tri1_sb[:]),
                        reads=[("ct1",), ("s0",)], writes=[("s0",)])
                if kc == 2 * g + 1:
                    S.op("vector", lambda E: E.tensor_mul(
                        out=s0b[:], in0=s0b[:], in1=tri2_sb[:]),
                        reads=[("ct2",), ("s0",)], writes=[("s0",)])
                if kc == 0:
                    S.op("vector", lambda E: E.memset(s0b[0:1, :], 0.0),
                         reads=[("s0",)], writes=[("s0",)])
                S.op("vector", lambda E, kc=kc: E.tensor_tensor_scan(
                    out=fft[:, kc * 256:(kc + 1) * 256], data0=s0b[:],
                    data1=s0b[:], initial=carr[:, kc:kc + 1],
                    op0=ALU.add, op1=ALU.bypass),
                    reads=[("s0",), ("carr",), ("fft", kc)],
                    writes=[("fft", kc)])
                S.op("vector", lambda E, kc=kc: E.tensor_copy(
                    out=carr[:, kc:kc + 1],
                    in_=fft[:, kc * 256 + 255:kc * 256 + 256]),
                    reads=[("fft", kc)], writes=[("carr",)])
                S.op("vector", lambda E, kc=kc: E.tensor_sub(
                    out=fft[:, kc * 256:(kc + 1) * 256],
                    in0=fft[:, kc * 256:(kc + 1) * 256], in1=s0b[:]),
                    reads=[("fft", kc), ("s0",)],
                    writes=[("fft", kc)])
                yield
            for sub in range(2):
                tl = 2 * g + sub
                if tl < 6:
                    continue
                nkt = tl + 1
                ncol = nkt * 128
                for kc in range(nkt):
                    S.op("tensor", lambda E, kc=kc, sub=sub: E.transpose(
                        ps_t[:], fft[:, kc * 256 + sub * 128:
                                     kc * 256 + sub * 128 + 128], id_sb[:]),
                        reads=[("fft", kc), ("cid",), ("pst",)],
                        writes=[("pst",)])
                    S.op("vector", lambda E, kc=kc: E.tensor_copy(
                        out=fftile[:, kc * 128:(kc + 1) * 128], in_=ps_t[:]),
                        reads=[("pst",)], writes=[("ft", kc)])
                    if kc % 8 == 7:
                        yield
                S.op("vector", lambda E, tl=tl: E.tensor_scalar(
                    out=qcol[:], in0=iop_sb[:], scalar1=float(tl * 128),
                    scalar2=None, op0=ALU.add),
                    reads=[("ciop",)], writes=[("qcol",)])
                S.op("vector", lambda E: E.tensor_scalar(
                    out=bcol[:], in0=qcol[:], scalar1=1.0, scalar2=BUDGET,
                    op0=ALU.add, op1=ALU.min),
                    reads=[("qcol",)], writes=[("bcol",)])
                S.op("vector", lambda E: E.tensor_scalar(
                    out=bm1[:], in0=bcol[:], scalar1=-1.0, scalar2=None,
                    op0=ALU.add),
                    reads=[("bcol",)], writes=[("bm1",)])
                S.op("vector", lambda E, ncol=ncol: E.tensor_scalar(
                    out=causb[:, 0:ncol], in0=iota_sb[:, 0:ncol],
                    scalar1=qcol[:], scalar2=None, op0=ALU.is_le),
                    reads=[("cio",), ("qcol",), ("jnkv",), ("b2",)],
                    writes=[("caus",)])
                S.op("vector", lambda E, ncol=ncol: E.tensor_reduce(
                    out=rmax[:], in_=fftile[:, 0:ncol], axis=AX.X,
                    op=ALU.max),
                    reads=[("ft", kc) for kc in range(nkt)],
                    writes=[("rmax",)])
                S.op("vector", lambda E: E.tensor_tensor(
                    out=cc.bitcast(I32)[:], in0=rmax.bitcast(I32)[:],
                    in1=em_sb[:], op=ALU.bitwise_and),
                    reads=[("rmax",), ("cem",)], writes=[("cc",)])
                S.op("vector", lambda E: E.tensor_scalar(
                    out=cc[:], in0=cc[:], scalar1=2.0, scalar2=1.0,
                    op0=ALU.mult, op1=ALU.max),
                    reads=[("cc",)], writes=[("cc",)])
                S.op("vector", lambda E: E.tensor_scalar(
                    out=c4[:], in0=cc[:], scalar1=4.0, scalar2=None,
                    op0=ALU.mult),
                    reads=[("cc",)], writes=[("c4",)])
                S.op("vector", lambda E, ncol=ncol: E.tensor_scalar(
                    out=zb_[:, 0:ncol], in0=fftile[:, 0:ncol], scalar1=cc[:],
                    scalar2=None, op0=ALU.add),
                    reads=[("ft", kc) for kc in range(nkt)] + [("cc",)],
                    writes=[("z",)])
                S.op("vector", lambda E, ncol=ncol: E.tensor_mul(
                    out=zb_[:, 0:ncol], in0=zb_[:, 0:ncol],
                    in1=causb[:, 0:ncol]),
                    reads=[("z",), ("caus",)], writes=[("z",)])
                S.op("vector", lambda E, ncol=ncol: E.tensor_scalar(
                    out=jnk[:, 0:ncol], in0=causb[:, 0:ncol], scalar1=c4[:],
                    scalar2=None, op0=ALU.mult),
                    reads=[("caus",), ("c4",)], writes=[("jnkv",)])
                S.op("vector", lambda E, ncol=ncol: E.tensor_scalar(
                    out=jnk[:, 0:ncol], in0=jnk[:, 0:ncol], scalar1=-1.0,
                    scalar2=c4[:], op0=ALU.mult, op1=ALU.add),
                    reads=[("jnkv",), ("c4",)], writes=[("jnkv",)])
                S.op("vector", lambda E, ncol=ncol: E.tensor_add(
                    out=zb_[:, 0:ncol], in0=zb_[:, 0:ncol],
                    in1=jnk[:, 0:ncol]),
                    reads=[("z",), ("jnkv",)], writes=[("z",)])
                S.op("vector", lambda E: E.tensor_copy(out=lo[:], in_=cc[:]),
                     reads=[("cc",)], writes=[("lo",)])
                S.op("vector", lambda E: E.tensor_scalar(
                    out=hi[:], in0=cc[:], scalar1=2.0, scalar2=None,
                    op0=ALU.mult),
                    reads=[("cc",)], writes=[("hi",)])
                yield
                for it in range(26):
                    S.op("vector", lambda E: E.tensor_add(
                        out=mid[:], in0=lo[:], in1=hi[:]),
                        reads=[("lo",), ("hi",)], writes=[("mid",)])
                    S.op("vector", lambda E: E.tensor_scalar(
                        out=mid[:], in0=mid[:], scalar1=0.5, scalar2=None,
                        op0=ALU.mult),
                        reads=[("mid",)], writes=[("mid",)])
                    S.op("vector", lambda E, ncol=ncol: E.tensor_scalar(
                        out=jnk[:, 0:ncol], in0=zb_[:, 0:ncol],
                        scalar1=mid[:], scalar2=0.0,
                        op0=ALU.is_lt, op1=ALU.add, accum_out=cnt[:]),
                        reads=[("z",), ("mid",), ("caus",)],
                        writes=[("jnkv",), ("cnt",)])
                    S.op("vector", lambda E: E.tensor_tensor(
                        out=prd[:], in0=cnt[:], in1=bm1[:], op=ALU.is_le),
                        reads=[("cnt",), ("bm1",)], writes=[("prd",)])
                    S.op("vector", lambda E: E.select(
                        out=lo[:], mask=prd[:], on_true=mid[:], on_false=lo[:],
                        add_drain=True),
                        reads=[("prd",), ("mid",), ("lo",)], writes=[("lo",)])
                    S.op("vector", lambda E: E.select(
                        out=hi[:], mask=prd[:], on_true=hi[:], on_false=mid[:],
                        add_drain=True),
                        reads=[("prd",), ("mid",), ("hi",)], writes=[("hi",)])
                    if it % 3 == 2:
                        yield
                S.op("vector", lambda E, ncol=ncol: E.tensor_scalar(
                    out=jnk[:, 0:ncol], in0=zb_[:, 0:ncol], scalar1=lo[:],
                    scalar2=0.0, op0=ALU.is_lt, op1=ALU.add,
                    accum_out=llo[:]),
                    reads=[("z",), ("lo",)], writes=[("jnkv",), ("llo",)])
                S.op("vector", lambda E: E.tensor_sub(
                    out=rrr[:], in0=bcol[:], in1=llo[:]),
                    reads=[("bcol",), ("llo",)], writes=[("rrr",)])
                S.op("vector", lambda E, ncol=ncol: E.tensor_scalar(
                    out=b1f[:, 0:ncol], in0=zb_[:, 0:ncol], scalar1=lo[:],
                    scalar2=None, op0=ALU.is_equal),
                    reads=[("z",), ("lo",), ("b1",)],
                    writes=[("eqt",), ("b1",)])
                S.op("vector", lambda E: E.memset(b2f[:, 0:1], 0.0),
                     reads=[("jnkv",), ("caus",)], writes=[("tie",), ("b2",)])
                S.op("vector", lambda E, ncol=ncol: E.tensor_tensor_scan(
                    out=b2f[:, 1:ncol], data0=b1f[:, 0:ncol - 1],
                    data1=b1f[:, 0:ncol - 1], initial=0.0,
                    op0=ALU.add, op1=ALU.bypass),
                    reads=[("eqt",)], writes=[("tie",), ("b2",)])
                yield
                S.op("vector", lambda E, ncol=ncol: E.tensor_scalar(
                    out=b2f[:, 0:ncol], in0=b2f[:, 0:ncol], scalar1=rrr[:],
                    scalar2=None, op0=ALU.is_lt),
                    reads=[("tie",), ("rrr",)], writes=[("tie",), ("b2",)])
                S.op("vector", lambda E, ncol=ncol: E.tensor_mul(
                    out=b1f[:, 0:ncol], in0=b1f[:, 0:ncol],
                    in1=b2f[:, 0:ncol]),
                    reads=[("eqt",), ("tie",), ("b2",)], writes=[("eqt",)])
                S.op("vector", lambda E, ncol=ncol: E.tensor_scalar(
                    out=b2f[:, 0:ncol], in0=zb_[:, 0:ncol], scalar1=lo[:],
                    scalar2=None, op0=ALU.is_lt),
                    reads=[("z",), ("lo",), ("tie",)], writes=[("b2",)])
                S.op("vector", lambda E, ncol=ncol: E.tensor_tensor(
                    out=b1f[:, 0:ncol], in0=b1f[:, 0:ncol],
                    in1=b2f[:, 0:ncol], op=ALU.max),
                    reads=[("eqt",), ("b2",)], writes=[("eqt",)])
                S.op("vector", lambda E, ncol=ncol: E.tensor_scalar(
                    out=b2f[:, 0:ncol], in0=iota_sb[:, 0:ncol],
                    scalar1=qcol[:], scalar2=None, op0=ALU.is_equal),
                    reads=[("cio",), ("qcol",), ("b2",)], writes=[("b2",)])
                S.op("vector", lambda E, ncol=ncol: E.tensor_tensor(
                    out=b1f[:, 0:ncol], in0=b1f[:, 0:ncol],
                    in1=b2f[:, 0:ncol], op=ALU.max),
                    reads=[("eqt",), ("b2",)], writes=[("eqt",)])
                S.op("vector", lambda E, ncol=ncol: E.tensor_scalar(
                    out=b1f[:, 0:ncol], in0=b1f[:, 0:ncol], scalar1=-1e5,
                    scalar2=1e5, op0=ALU.mult, op1=ALU.add),
                    reads=[("eqt",)], writes=[("eqt",)])
                for kc in range(nkt):
                    S.op("tensor", lambda E, kc=kc: E.transpose(
                        ps_t[:], b1f[:, kc * 128:(kc + 1) * 128], id_sb[:]),
                        reads=[("eqt",), ("cid",), ("pst",)],
                        writes=[("pst",)])
                    S.op("vector", lambda E, kc=kc, sub=sub: E.tensor_add(
                        out=fft[:, kc * 256 + sub * 128:
                                kc * 256 + sub * 128 + 128],
                        in0=fft[:, kc * 256 + sub * 128:
                                kc * 256 + sub * 128 + 128],
                        in1=ps_t[:]),
                        reads=[("pst",), ("fft", kc)],
                        writes=[("fft", kc)])
                    if kc % 8 == 7:
                        yield
                if sub == 0:
                    kl = 2 * g + 1
                    S.op("vector", lambda E, kl=kl: E.tensor_scalar(
                        out=fft[:, kl * 256:kl * 256 + 128],
                        in0=fft[:, kl * 256:kl * 256 + 128],
                        scalar1=1e5, scalar2=None, op0=ALU.add),
                        reads=[("fft", kl)], writes=[("fft", kl)])
                yield

        def emit_ffe(g):
            nkc = 2 * g + 2
            for kc in range(nkc):
                S.op("scalar", lambda E, kc=kc: E.activation(
                    out=ffe[:, kc * 256:(kc + 1) * 256],
                    in_=fft[:, kc * 256:(kc + 1) * 256], func=AF.Exp,
                    bias=0.0, scale=FC(C, "mone")),
                    reads=[("fft", kc), ("cfc",)], writes=[("ffe", kc)])

        def stage_Y(g):
            t0 = g * 256
            nkc = 2 * g + 2
            buf = g % 2
            for h in range(16):
                p, hh = h // 2, h % 2
                sl = p % 2
                if hh == 0:
                    S.dma("sync", kp[sl][:], ks[p],
                          reads=([("mbv",), ("rbcv",)]
                                 if (g == 0 and p == 0) else []),
                          writes=[("kp", sl)], slot=f"kq{sl}")
                    S.dma("sync", qpg[sl][:], qs[p][:, t0:t0 + 256],
                          writes=[("qpg", sl)], slot=f"kq{sl}")
                S.dma("sync", vap[sl][:], vas[p, hh], writes=[("vap", sl)],
                      slot=f"va{sl}")
                hb = 64 * hh

                npair = nkc // 2

                def av(j):
                    b = j % 2
                    S.op("tensor", lambda E, sl=sl, j=j, b=b, nkc=nkc, h=h: [
                        E.matmul(ps_o[h % 2][0:65, :],
                                 vap[sl][:, (2 * j + i) * 65:
                                         (2 * j + i + 1) * 65],
                                 phat[b][:, i * 256:(i + 1) * 256],
                                 start=(2 * j + i == 0),
                                 stop=(2 * j + i == nkc - 1))
                        for i in range(2)][-1],
                        reads=[("vap", sl), ("phat", b), ("pso", h % 2)],
                        writes=[("pso", h % 2)])
                for j in range(npair):
                    b = j % 2
                    S.op("tensor", lambda E, sl=sl, hb=hb, j=j, b=b: [
                        E.matmul(ps_at[b][:, i * 256:(i + 1) * 256],
                                 kp[sl][hb:hb + 64,
                                        (2 * j + i) * 128:
                                        (2 * j + i + 1) * 128],
                                 qpg[sl][hb:hb + 64, :],
                                 start=True, stop=True)
                        for i in range(2)][-1],
                        reads=[("kp", sl), ("qpg", sl), ("psat", b)],
                        writes=[("psat", b)])
                    if j > 0:
                        av(j - 1)
                    S.op("scalar", lambda E, b=b: E.activation(
                        out=pt[b][:], in_=ps_at[b][:], func=AF.Exp,
                        bias=FC(C, "mtwenty"), scale=FC(C, "eighth")),
                        reads=[("psat", b), ("cfc",)],
                        writes=[("pt", b)])
                    S.op("vector", lambda E, b=b, j=j: E.tensor_mul(
                        out=phat[b][:], in0=pt[b][:],
                        in1=ffe[:, j * 512:(j + 1) * 512]),
                        reads=[("pt", b), ("ffe", 2 * j), ("ffe", 2 * j + 1),
                               ("phat", b)],
                        writes=[("phat", b)])
                    if g < 3:
                        for i in range(2):
                            kc = 2 * j + i
                            if kc >= 2 * g:
                                tri = (tri1c_sb if kc == 2 * g else tri2c_sb)
                                S.op("vector", lambda E, tri=tri, b=b, i=i:
                                     E.tensor_mul(
                                         out=phat[b][:, i * 256:
                                                     (i + 1) * 256],
                                         in0=phat[b][:, i * 256:
                                                     (i + 1) * 256],
                                         in1=tri[:]),
                                     reads=[("ct1c",), ("ct2c",),
                                            ("phat", b)],
                                     writes=[("phat", b)])
                av(npair - 1)
                S.op("scalar", lambda E, h=h: E.activation(
                    out=dinv[:], in_=ps_o[h % 2][64:65, :], func=AF.Ln),
                    reads=[("pso", h % 2), ("dinv",)], writes=[("dinv",)])
                S.op("scalar", lambda E: E.activation(
                    out=dinv[:], in_=dinv[:], func=AF.Exp, bias=0.0,
                    scale=FC(C, "mone", 1)),
                    reads=[("dinv",)], writes=[("dinv",)])
                S.op("tensor", lambda E: E.matmul(
                    pbc[:, 0:256], C["onesB"][:], dinv[:], start=True,
                    stop=True),
                    reads=[("dinv",), ("cB",), ("pbcq",)], writes=[("pbcq",)])
                S.op("vector", lambda E: E.tensor_copy(
                    out=dbc[:], in_=pbc[:, 0:256]),
                    reads=[("pbcq",), ("dbcv",)], writes=[("dbcv",)])
                S.op("vector", lambda E, h=h, p=p, hb=hb: E.tensor_mul(
                    out=og[p][hb:hb + 64, :], in0=ps_o[h % 2][0:64, :],
                    in1=dbc[0:64, :]),
                    reads=[("pso", h % 2), ("dbcv",)],
                    writes=[("og", p, hh)])
                yield
            for ho in range(8):
                pz = ho % 2
                S.op("tensor", lambda E, ho=ho, pz=pz: [
                    E.matmul(ps_o[pz][:], pwsb[d][:, ho * 128:(ho + 1) * 128],
                             og[d][:], start=(d == 0), stop=(d == 7))
                    for d in range(8)][-1],
                    reads=[("pw", d) for d in range(8)]
                    + [("og", d, hh2) for d in range(8) for hh2 in range(2)]
                    + [("pso", pz)],
                    writes=[("pso", pz)])
                S.dma("sync", xsl[:], x[ho * 128:(ho + 1) * 128, t0:t0 + 256],
                      reads=[("xslr",)], writes=[("xsl",)], slot="xr")
                S.op("vector", lambda E, ho=ho, pz=pz: E.tensor_scalar(
                    out=ysl[pz][:], in0=ps_o[pz][:],
                    scalar1=pb_sb[:, ho:ho + 1], scalar2=None, op0=ALU.add),
                    reads=[("pso", pz), ("cpb",), ("ysl", pz)],
                    writes=[("ysl", pz)])
                S.op("vector", lambda E, pz=pz: E.tensor_add(
                    out=ysl[pz][:], in0=ysl[pz][:], in1=xsl[:]),
                    reads=[("xsl",), ("ysl", pz)],
                    writes=[("ysl", pz), ("xslr",)])
                S.dma("gpsimd", y[ho * 128:(ho + 1) * 128, t0:t0 + 256],
                      ysl[pz][:],
                      reads=[("ysl", pz)],
                      writes=[(tag + "y", ho, g), ("yslr", pz)],
                      slot=f"y{pz}")
                if ho % 4 == 3:
                    yield

        _SENT = object()
        ys = None
        for g in range(NQG):
            xit = stage_X(g)
            if ys is None:
                for _ in xit:
                    pass
            else:
                xd = yd = False
                while not (xd and yd):
                    for _ in range(3):
                        if not xd and next(xit, _SENT) is _SENT:
                            xd = True
                    if not yd and next(ys, _SENT) is _SENT:
                        yd = True
            emit_ffe(g)
            ys = stage_Y(g)
        for _ in ys:
            pass
        barrier(S)


def host_attn1_consts():
    iota = np.tile(np.arange(T, dtype=np.float32)[None, :], (128, 1))
    iop = np.arange(128, dtype=np.float32).reshape(128, 1)
    q = np.arange(256)[None, :]
    k1 = np.arange(128)[:, None]
    tri1 = (k1 < q).astype(np.float32)
    tri2 = ((k1 + 128) < q).astype(np.float32)
    tri1c = (k1 <= q).astype(np.float32)
    tri2c = ((k1 + 128) <= q).astype(np.float32)
    em = np.full((128, 1), 0x7F800000, dtype=np.int32)
    return (iota, iop, np.ascontiguousarray(tri1), np.ascontiguousarray(tri2),
            np.ascontiguousarray(tri1c), np.ascontiguousarray(tri2c), em)



def _run(nc, maps):
    return run_bass_kernel_spmd(nc, maps, core_ids=list(range(NCORES))).results


def kernel(**inputs):
    if os.environ.get("MK_UNFUSED") == "1":
        return kernel_unfused(**inputs)
    nc = build_fused()
    maps = kernel_fused_inputs(**inputs)
    r = _run(nc, maps)
    return np.stack([np.asarray(r[b]["y"]).T
                     for b in range(8)]).astype(np.float32)


def kernel_unfused(L0_attn_x, L0_attn_qkv_w, L0_attn_qkv_b, L0_attn_proj_w,
           L0_attn_proj_b, L0_ffn_Wg_s, L0_ffn_Wu_s, L0_ffn_Wd_s, L0_ffn_Wg,
           L0_ffn_Wu, L0_ffn_Wd, L0_ffn_Wr, L0_ffn_rb, L1_attn_W_attn,
           L1_attn_b_attn, L1_attn_W_proj, L1_attn_b_proj, L1_ffn_ln_gamma,
           L1_ffn_ln_beta, L1_ffn_fc1_q, L1_ffn_fc1_scales, L1_ffn_fc1_bias,
           L1_ffn_fc2_q, L1_ffn_fc2_scales, L1_ffn_fc2_bias):
    f32 = np.float32
    onesA = np.ones((128, 1), f32)
    onesB = np.ones((1, 128), f32)
    ident = np.eye(128, dtype=f32)
    fc = host_fconst()

    x0 = np.asarray(L0_attn_x, f32)
    xT = [np.ascontiguousarray(x0[b].T) for b in range(8)]

    wg9 = np.ascontiguousarray(np.concatenate(
        [np.asarray(L0_ffn_Wg_s, f32).T[None],
         np.transpose(np.asarray(L0_ffn_Wg, f32), (0, 2, 1))], axis=0))
    wu9 = np.ascontiguousarray(np.concatenate(
        [np.asarray(L0_ffn_Wu_s, f32).T[None],
         np.transpose(np.asarray(L0_ffn_Wu, f32), (0, 2, 1))], axis=0))
    wd9 = np.ascontiguousarray(np.concatenate(
        [np.asarray(L0_ffn_Wd_s, f32).T[None],
         np.transpose(np.asarray(L0_ffn_Wd, f32), (0, 2, 1))], axis=0))
    wrT = np.ascontiguousarray(np.asarray(L0_ffn_Wr, f32).T)
    rb8 = np.ascontiguousarray(np.asarray(L0_ffn_rb, f32).reshape(8, 1))

    nc_ffn0 = build_ffn_l0()
    ffn0_common = {"wg": wg9, "wu": wu9, "wd": wd9, "wr": wrT, "rb": rb8,
                   "onesA": onesA, "onesB": onesB, "ident": ident,
                   "fconst": fc}

    def run_ffn0(xTs):
        r = _run(nc_ffn0, [dict(ffn0_common, x=xTs[b]) for b in range(8)])
        return [r[b]["y"] for b in range(8)]

    qkv_w = np.asarray(L0_attn_qkv_w, f32)
    qkv_b = np.asarray(L0_attn_qkv_b, f32)
    qkv_bp = qkv_b.reshape(48, 2, 32)[:, ::-1, :].reshape(3072)
    cos, sin = host_rope_tables()
    nc_qkv0 = build_qkv0()
    qkv0_common = {
        "qkvw": np.ascontiguousarray(qkv_w.T),
        "qkvb": np.ascontiguousarray(qkv_b.reshape(24, 128).T),
        "qkvbp": np.ascontiguousarray(qkv_bp.reshape(24, 128).T),
        "cost": cos, "sint": sin, "onesA": onesA, "onesB": onesB,
        "fconst": fc,
    }
    nc_a0c = build_attn0core()
    nc_a0p = build_attn0proj()
    a0p_common = {
        "pw": np.ascontiguousarray(np.asarray(L0_attn_proj_w, f32).T),
        "pb": np.ascontiguousarray(
            np.asarray(L0_attn_proj_b, f32).reshape(8, 128).T),
    }

    def run_attn0(xTs):
        r1 = _run(nc_qkv0, [dict(qkv0_common, x=xTs[b]) for b in range(8)])
        maps2 = []
        for p in range(8):
            maps2.append({
                "qz": np.stack([r1[b2]["qro"][p] for b2 in range(8)]),
                "kz": np.stack([r1[b2]["kro"][p] for b2 in range(8)]),
                "vz": np.stack([r1[b2]["vtm"][p] for b2 in range(8)]),
                "onesB": onesB, "fconst": fc,
            })
        r2 = _run(nc_a0c, maps2)
        maps3 = []
        for c in range(8):
            o4 = np.asarray(r2[c]["oz"]).reshape(8, 2, 64, 256, 16)
            scr = np.ascontiguousarray(
                o4.transpose(4, 2, 1, 0, 3).reshape(1024, 4096))
            maps3.append(dict(a0p_common, sc=scr, x=xTs[c]))
        r3 = _run(nc_a0p, maps3)
        return [r3[c]["y"] for c in range(8)]

    nc_ffn1 = build_ffn_l1()
    ffn1_common = {
        "q1": np.asarray(L1_ffn_fc1_q, np.int32),
        "q2": np.asarray(L1_ffn_fc2_q, np.int32),
        "sc1": np.asarray(L1_ffn_fc1_scales, f32),
        "sc2": np.asarray(L1_ffn_fc2_scales, f32),
        "b1": np.ascontiguousarray(
            np.asarray(L1_ffn_fc1_bias, f32).reshape(32, 128).T),
        "b2": np.ascontiguousarray(
            np.asarray(L1_ffn_fc2_bias, f32).reshape(8, 128).T),
        "gam": np.ascontiguousarray(
            np.asarray(L1_ffn_ln_gamma, f32).reshape(8, 128).T),
        "bet": np.ascontiguousarray(
            np.asarray(L1_ffn_ln_beta, f32).reshape(8, 128).T),
        "onesA": onesA, "onesB": onesB, "fconst": fc,
    }

    def run_ffn1(xTs):
        r = _run(nc_ffn1, [dict(ffn1_common, x=xTs[b]) for b in range(8)])
        return [r[b]["y"] for b in range(8)]

    iota, iop, tri1, tri2, tri1c, tri2c, em = host_attn1_consts()
    nc_a1 = build_attn1()
    a1_common = {
        "qkvw": np.ascontiguousarray(np.asarray(L1_attn_W_attn, f32).T),
        "qkvb": np.ascontiguousarray(
            np.asarray(L1_attn_b_attn, f32).reshape(24, 128).T),
        "pw": np.ascontiguousarray(np.asarray(L1_attn_W_proj, f32).T),
        "pb": np.ascontiguousarray(
            np.asarray(L1_attn_b_proj, f32).reshape(8, 128).T),
        "onesA": onesA, "onesB": onesB, "fconst": fc,
        "iot": iota, "iop": iop, "tri1": tri1, "tri2": tri2,
        "tri1c": tri1c.astype(BF), "tri2c": tri2c.astype(BF), "emask": em,
    }

    def run_attn1(xTs):
        r = _run(nc_a1, [dict(a1_common, x=xTs[b]) for b in range(8)])
        return [r[b]["y"] for b in range(8)]

    xTs = xT
    xTs = run_ffn0(xTs)
    xTs = run_attn0(xTs)
    xTs = run_ffn0(xTs)
    xTs = run_ffn1(xTs)
    xTs = run_attn1(xTs)
    xTs = run_ffn1(xTs)
    out = np.stack([np.asarray(xTs[b]).T for b in range(8)]).astype(np.float32)
    return out



def emit_attn0_fused(S, nc, ctx, io, tag):
    qroG, kroG, vtmG = io["qroG"], io["kroG"], io["vtmG"]
    onesB, fconst = io["onesB"], io["fconst"]
    x, pw, pb, y = io["x"], io["pw"], io["pb"], io["y"]
    C = {}
    C["onesB"] = sbuf(ctx, nc, "onesB", [1, 128], F32)
    C["fc"] = sbuf(ctx, nc, "fc", [128, len(CONST_VALS)], F32)
    pb_sb = sbuf(ctx, nc, "pb", [128, 8], F32)
    S.dma("sync", C["onesB"][:], onesB, writes=[("cB",)], slot="c")
    S.dma("sync", C["fc"][:], fconst, writes=[("cfc",)], slot="c")
    S.dma("sync", pb_sb[:], pb, writes=[("cpb",)], slot="c")
    pwsb = [sbuf(ctx, nc, f"pw{d}", [128, 1024], BF16) for d in range(8)]
    for d in range(8):
        S.dma("gpsimd", pwsb[d][:], pw[d * 128:(d + 1) * 128, :],
              writes=[("pw", d)], slot=f"w{d % 2}")
    q_sb = [sbuf(ctx, nc, f"q{i}", [128, T], BF16) for i in range(2)]
    k_sb = [sbuf(ctx, nc, f"k{i}", [128, T], BF16) for i in range(2)]
    va_sb = [sbuf(ctx, nc, f"va{i}", [128, 65 * 32], BF16) for i in range(2)]
    vt_sb = [sbuf(ctx, nc, f"vt{i}", [128, T], BF16) for i in range(2)]
    ps_s = [psum(ctx, nc, f"ps_s{i}", [128, 1024]) for i in range(2)]
    ps_o = [psum(ctx, nc, f"ps_o{i}", [128, 512]) for i in range(2)]
    ps_p0 = psum(ctx, nc, "ps_p0", [128, 256])
    ps_p = [ps_p0, ps_p0]
    pt = [sbuf(ctx, nc, f"pt{i}", [128, 1024], BF16) for i in range(2)]
    dinv = sbuf(ctx, nc, "dinv", [1, 512], F32)
    dbc = sbuf(ctx, nc, "dbc", [128, 512], F32)
    pbc = psum(ctx, nc, "pbc", [128, 512])
    ob = sbuf(ctx, nc, "ob", [64, T], BF16)
    scu = [sbuf(ctx, nc, f"scu{c2}", [128, 256], BF16) for c2 in range(8)]
    xsl = sbuf(ctx, nc, "xsl", [128, 256], F32)
    ysl = [sbuf(ctx, nc, f"ysl{i}", [128, 256], F32) for i in range(2)]
    _rowcache = {}

    def _grow(E, b2, gten):
        key = (id(E), b2)
        if key not in _rowcache:
            _rowcache[key] = E.snap(b2 * 8 + E.partition_id(),
                                    min_val=0, max_val=63)
        return gten[ds(_rowcache[key], 1)].rearrange("a p t -> (a p) t")
    for b2 in range(8):
        sl = b2 % 2
        S.dma("sync", q_sb[sl][:],
              (lambda E, b2=b2: _grow(E, b2, qroG)),
              reads=[("ag", "qro")], writes=[("q", sl)], slot=f"q{sl}")
        S.dma("sync", k_sb[sl][:],
              (lambda E, b2=b2: _grow(E, b2, kroG)),
              reads=[("ag", "kro")], writes=[("k", sl)], slot=f"k{sl}")
        S.dma("sync", vt_sb[sl][:],
              (lambda E, b2=b2: _grow(E, b2, vtmG)),
              reads=[("ag", "vtm")], writes=[("vt", sl)], slot=f"v{sl}")
        for h in range(2):
            hb = 64 * h
            S.op("vector", lambda E, sl=sl: E.memset(va_sb[sl][:], 1.0),
                 reads=[("var", sl)], writes=[("va", sl)])
            S.op("vector", lambda E, sl=sl, hb=hb: E.tensor_copy(
                out=va_sb[sl][:].rearrange("p (tt c) -> p tt c", c=65)
                [:, :, 0:64],
                in_=vt_sb[sl][:].rearrange("p (tt c) -> p tt c", c=128)
                [:, :, hb:hb + 64]),
                reads=[("vt", sl)], writes=[("va", sl)])
            for qg in range(8):
                t0 = qg * 512

                def emit_av(kc2, qg=qg, sl=sl):
                    b = kc2 % 2
                    S.op("tensor", lambda E, sl=sl, kc2=kc2, qg=qg, b=b: [
                        E.matmul(ps_o[qg % 2][0:65, :],
                                 va_sb[sl][:, (kc2 * 2 + i) * 65:
                                           (kc2 * 2 + i + 1) * 65],
                                 pt[b][:, i * 512:(i + 1) * 512],
                                 start=(kc2 == 0 and i == 0),
                                 stop=(kc2 == 15 and i == 1))
                        for i in range(2)][-1],
                        reads=[("va", sl), ("pt", b), ("pso", qg % 2)],
                        writes=[("pso", qg % 2)])
                for kc2 in range(16):
                    b = kc2 % 2
                    S.op("tensor", lambda E, sl=sl, hb=hb, kc2=kc2, t0=t0,
                         b=b: [
                        E.matmul(ps_s[b][:, i * 512:(i + 1) * 512],
                                 k_sb[sl][hb:hb + 64,
                                          (kc2 * 2 + i) * 128:
                                          (kc2 * 2 + i + 1) * 128],
                                 q_sb[sl][hb:hb + 64, t0:t0 + 512],
                                 start=True, stop=True)
                        for i in range(2)][-1],
                        reads=[("k", sl), ("q", sl), ("pss", b)],
                        writes=[("pss", b)])
                    if kc2 > 0:
                        emit_av(kc2 - 1)
                    S.op("scalar", lambda E, b=b: E.activation(
                        out=pt[b][:], in_=ps_s[b][:], func=AF.Exp,
                        bias=FC(C, "mtwenty"), scale=FC(C, "eighth")),
                        reads=[("pss", b), ("cfc",)],
                        writes=[("pt", b)])
                emit_av(15)
                S.op("scalar", lambda E, qg=qg: E.activation(
                    out=dinv[:], in_=ps_o[qg % 2][64:65, :], func=AF.Ln),
                    reads=[("pso", qg % 2)], writes=[("dinv",)])
                S.op("scalar", lambda E: E.activation(
                    out=dinv[:], in_=dinv[:], func=AF.Exp, bias=0.0,
                    scale=FC(C, "mone", 1)),
                    reads=[("dinv",)], writes=[("dinv",)])
                S.op("tensor", lambda E: E.matmul(
                    pbc[:], C["onesB"][:], dinv[:], start=True, stop=True),
                    reads=[("dinv",), ("cB",), ("dbcv",)], writes=[("pbc",)])
                S.op("vector", lambda E: E.tensor_copy(out=dbc[:], in_=pbc[:]),
                     reads=[("pbc",)], writes=[("dbcv",)])
                S.op("vector", lambda E, qg=qg, t0=t0: E.tensor_mul(
                    out=ob[:, t0:t0 + 512], in0=ps_o[qg % 2][0:64, :],
                    in1=dbc[0:64, :]),
                    reads=[("pso", qg % 2), ("dbcv",)],
                    writes=[("ob", qg)])
            for c2 in range(8):
                for j in range(2):
                    m = 2 * c2 + j
                    S.op("vector", lambda E, c2=c2, j=j, m=m: E.tensor_copy(
                        out=scu[c2][64 * j:64 * j + 64, :],
                        in_=ob[:].rearrange("p (G m) -> p G m", m=16)
                        [:, :, m]),
                        reads=[("ob", qg) for qg in range(8)]
                        + [("scur", c2)],
                        writes=[("scu", c2, j)])
            n0 = h * 2048 + b2 * 256
            for ho in range(8):
                pz = ho % 2
                S.op("tensor", lambda E, ho=ho: [
                    E.matmul(ps_p0[:], pwsb[d][:, ho * 128:(ho + 1) * 128],
                             scu[d][:], start=(d == 0), stop=(d == 7))
                    for d in range(8)][-1],
                    reads=[("pw", d) for d in range(8)]
                    + [("scu", d, j) for d in range(8) for j in range(2)]
                    + [("psp",)],
                    writes=[("psp",)])
                S.dma("sync", xsl[:], x[ho * 128:(ho + 1) * 128, n0:n0 + 256],
                      reads=[("xslr",)], writes=[("xsl",)], slot="xr")
                S.op("vector", lambda E, ho=ho, pz=pz: E.tensor_scalar(
                    out=ysl[pz][:], in0=ps_p0[:],
                    scalar1=pb_sb[:, ho:ho + 1], scalar2=None, op0=ALU.add),
                    reads=[("psp",), ("cpb",), ("yslr", pz)],
                    writes=[("ysl", pz), ("psp",)])
                S.op("vector", lambda E, pz=pz: E.tensor_add(
                    out=ysl[pz][:], in0=ysl[pz][:], in1=xsl[:]),
                    reads=[("xsl",), ("ysl", pz)],
                    writes=[("ysl", pz), ("xslr",)])
                S.dma("gpsimd", y[ho * 128:(ho + 1) * 128, n0:n0 + 256],
                      ysl[pz][:],
                      reads=[("ysl", pz)],
                      writes=[(tag + "y", ho, b2, h), ("yslr", pz)],
                      slot=f"y{pz}")
            S.op("vector", lambda E: E.engine_nop(),
                 reads=[("scu", c2, j) for c2 in range(8) for j in range(2)],
                 writes=[("scur0",)] + [("scur", c2) for c2 in range(8)])
    barrier(S)



def build_fused():
    nc = bass.Bass("TRN2", target_bir_lowering=False)
    EI = lambda n, s, d=F32: nc.dram_tensor(n, s, d, kind="ExternalInput").ap()
    IN = lambda n, s, d=F32: nc.dram_tensor(n, s, d, kind="Internal").ap()
    cm = {}
    cm["onesA"] = EI("onesA", [128, 1])
    cm["onesB"] = EI("onesB", [1, 128])
    cm["ident"] = EI("ident", [128, 128])
    cm["fconst"] = EI("fconst", [128, len(CONST_VALS)])
    x0 = EI("x", [D, T])
    yf = nc.dram_tensor("y", [D, T], F32, kind="ExternalOutput").ap()
    x1, x2, x3, x4, x5 = (IN(f"xs{i}", [D, T]) for i in range(5))
    f0 = dict(cm)
    f0.update(wg=EI("wg", [9, D, 2048]), wu=EI("wu", [9, D, 2048]),
              wd=EI("wd", [9, 2048, D]), wr=EI("wr", [D, 8]),
              rb=EI("rb", [8, 1]))
    a0 = dict(cm)
    a0.update(qkvw=EI("a0qkvw", [D, 3072]), qkvb=EI("a0qkvb", [128, 24]),
              qkvbp=EI("a0qkvbp", [128, 24]), cost=EI("cost", [128, T]),
              sint=EI("sint", [128, T]),
              qro=IN("qroS", [8, 128, T], BF16),
              kro=IN("kroS", [8, 128, T], BF16),
              vtm=IN("vtmS", [8, 128, T], BF16),
              qroG=IN("qroG", [64, 128, T], BF16),
              kroG=IN("kroG", [64, 128, T], BF16),
              vtmG=IN("vtmG", [64, 128, T], BF16),
              pw=EI("a0pw", [D, D]), pb=EI("a0pb", [128, 8]))
    f1 = dict(cm)
    f1.update(q1=EI("q1", [D, 4096], I32), q2=EI("q2", [4096, D], I32),
              sc1=EI("sc1", [8, 4096]), sc2=EI("sc2", [32, D]),
              b1=EI("b1", [128, 32]), b2=EI("b2", [128, 8]),
              gam=EI("gam", [128, 8]), bet=EI("bet", [128, 8]),
              w2s=IN("w2s", [4096, D], BF16))
    a1 = dict(cm)
    a1.update(qkvw=EI("a1qkvw", [D, 3072]), qkvb=EI("a1qkvb", [128, 24]),
              pw=EI("a1pw", [D, D]), pb=EI("a1pb", [128, 8]),
              iot=EI("iot", [128, T]), iop=EI("iop", [128, 1]),
              tri1=EI("tri1", [128, 256]), tri2=EI("tri2", [128, 256]),
              tri1c=EI("tri1c", [128, 256], BF16),
              tri2c=EI("tri2c", [128, 256], BF16),
              emask=EI("emask", [128, 1], I32),
              qs=IN("a1qs", [8, 128, T], BF16),
              ks=IN("a1ks", [8, 128, T], BF16),
              vas=IN("a1vas", [8, 2, 128, 65 * 32], BF16))

    with ExitStack() as ctx:
        S = Sched(nc, ctx)
        with ExitStack() as c:
            emit_ffn_l0(S, nc, c, dict(f0, x=x0, y=x1), tag="f0a")

        def ag(name, src_t, dst_t, rkeys):
            S.op("gpsimd",
                 lambda E, s_=src_t, d_=dst_t: E.collective_compute(
                     "AllGather", ALU.bypass,
                     replica_groups=[list(range(8))],
                     ins=[s_.rearrange("a b c -> a (b c)").opt()],
                     outs=[d_.rearrange("a b c -> a (b c)").opt()]),
                 reads=rkeys, writes=[("ag", name)], sem=f"_ag_{name}")

        def post_qk():
            ag("qro", a0["qro"], a0["qroG"],
               [("a0qout", 0, p) for p in range(8)])
            ag("kro", a0["kro"], a0["kroG"],
               [("a0qout", 1, p) for p in range(8)])
        with ExitStack() as c:
            emit_qkv0(S, nc, c, dict(a0, x=x1), tag="a0q", post_qk=post_qk)
        ag("vtm", a0["vtm"], a0["vtmG"],
           [("a0qoutv", p) for p in range(8)])
        with ExitStack() as c:
            emit_attn0_fused(S, nc, c, dict(a0, x=x1, y=x2), tag="a0f")
        with ExitStack() as c:
            emit_ffn_l0(S, nc, c, dict(f0, x=x2, y=x3), tag="f0b")
        with ExitStack() as c:
            emit_ffn_l1(S, nc, c, dict(f1, x=x3, y=x4), tag="f1a")
        with ExitStack() as c:
            emit_attn1(S, nc, c, dict(a1, x=x4, y=x5), tag="a1")
        with ExitStack() as c:
            emit_ffn_l1(S, nc, c, dict(f1, x=x5, y=yf), tag="f1b")
        S.emit()
    return nc


def kernel_fused_inputs(L0_attn_x, L0_attn_qkv_w, L0_attn_qkv_b,
                        L0_attn_proj_w, L0_attn_proj_b, L0_ffn_Wg_s,
                        L0_ffn_Wu_s, L0_ffn_Wd_s, L0_ffn_Wg, L0_ffn_Wu,
                        L0_ffn_Wd, L0_ffn_Wr, L0_ffn_rb, L1_attn_W_attn,
                        L1_attn_b_attn, L1_attn_W_proj, L1_attn_b_proj,
                        L1_ffn_ln_gamma, L1_ffn_ln_beta, L1_ffn_fc1_q,
                        L1_ffn_fc1_scales, L1_ffn_fc1_bias, L1_ffn_fc2_q,
                        L1_ffn_fc2_scales, L1_ffn_fc2_bias):
    f32 = np.float32
    cos, sin = host_rope_tables()
    iota, iop, tri1, tri2, tri1c, tri2c, em = host_attn1_consts()
    qkv_b = np.asarray(L0_attn_qkv_b, f32)
    qkv_bp = qkv_b.reshape(48, 2, 32)[:, ::-1, :].reshape(3072)
    common = {
        "onesA": np.ones((128, 1), f32), "onesB": np.ones((1, 128), f32),
        "ident": np.eye(128, dtype=f32), "fconst": host_fconst(),
        "wg": np.ascontiguousarray(np.concatenate(
            [np.asarray(L0_ffn_Wg_s, f32).T[None],
             np.transpose(np.asarray(L0_ffn_Wg, f32), (0, 2, 1))], axis=0)),
        "wu": np.ascontiguousarray(np.concatenate(
            [np.asarray(L0_ffn_Wu_s, f32).T[None],
             np.transpose(np.asarray(L0_ffn_Wu, f32), (0, 2, 1))], axis=0)),
        "wd": np.ascontiguousarray(np.concatenate(
            [np.asarray(L0_ffn_Wd_s, f32).T[None],
             np.transpose(np.asarray(L0_ffn_Wd, f32), (0, 2, 1))], axis=0)),
        "wr": np.ascontiguousarray(np.asarray(L0_ffn_Wr, f32).T),
        "rb": np.ascontiguousarray(np.asarray(L0_ffn_rb, f32).reshape(8, 1)),
        "a0qkvw": np.ascontiguousarray(np.asarray(L0_attn_qkv_w, f32).T),
        "a0qkvb": np.ascontiguousarray(qkv_b.reshape(24, 128).T),
        "a0qkvbp": np.ascontiguousarray(qkv_bp.reshape(24, 128).T),
        "cost": cos, "sint": sin,
        "a0pw": np.ascontiguousarray(np.asarray(L0_attn_proj_w, f32).T),
        "a0pb": np.ascontiguousarray(
            np.asarray(L0_attn_proj_b, f32).reshape(8, 128).T),
        "q1": np.asarray(L1_ffn_fc1_q, np.int32),
        "q2": np.asarray(L1_ffn_fc2_q, np.int32),
        "sc1": np.asarray(L1_ffn_fc1_scales, f32),
        "sc2": np.asarray(L1_ffn_fc2_scales, f32),
        "b1": np.ascontiguousarray(
            np.asarray(L1_ffn_fc1_bias, f32).reshape(32, 128).T),
        "b2": np.ascontiguousarray(
            np.asarray(L1_ffn_fc2_bias, f32).reshape(8, 128).T),
        "gam": np.ascontiguousarray(
            np.asarray(L1_ffn_ln_gamma, f32).reshape(8, 128).T),
        "bet": np.ascontiguousarray(
            np.asarray(L1_ffn_ln_beta, f32).reshape(8, 128).T),
        "a1qkvw": np.ascontiguousarray(np.asarray(L1_attn_W_attn, f32).T),
        "a1qkvb": np.ascontiguousarray(
            np.asarray(L1_attn_b_attn, f32).reshape(24, 128).T),
        "a1pw": np.ascontiguousarray(np.asarray(L1_attn_W_proj, f32).T),
        "a1pb": np.ascontiguousarray(
            np.asarray(L1_attn_b_proj, f32).reshape(8, 128).T),
        "iot": iota, "iop": iop, "tri1": tri1, "tri2": tri2,
        "tri1c": tri1c.astype(BF), "tri2c": tri2c.astype(BF), "emask": em,
    }
    x0 = np.asarray(L0_attn_x, f32)
    return [dict(common, x=np.ascontiguousarray(x0[b])
                 .T.copy()) for b in range(8)]


# version 08 (final)
# speedup vs baseline: 1.0098x; 1.0098x over previous
import math
import os
from contextlib import ExitStack

import numpy as np
import ml_dtypes

import concourse.bass as bass
from concourse.bass import ds
import concourse.mybir as mybir
from concourse.bass_utils import run_bass_kernel_spmd

F32 = mybir.dt.float32
BF16 = mybir.dt.bfloat16
I32 = mybir.dt.int32
AF = mybir.ActivationFunctionType
ALU = mybir.AluOpType
AX = mybir.AxisListType

NCORES = 8
D = 1024
T = 4096
NH = 16
HD = 64
DCH = D // 128
BF = ml_dtypes.bfloat16

ENGINES = ("sync", "gpsimd", "vector", "scalar", "tensor")


class Sched:

    def __init__(self, nc, ctx):
        self.nc = nc
        self.ctx = ctx
        self.ops = {e: [] for e in ENGINES}
        self.esem = {e: f"_e_{e}" for e in ENGINES}
        self.cnt = {}
        for s in self.esem.values():
            self.cnt[s] = 0
        self.last_w = {}
        self.readers = {}
        self.n_emitted = {e: 0 for e in ENGINES}

    def _edges(self, eng, sem, reads, writes):
        waits = {}

        def add(dep):
            if dep is None:
                return
            dsem, dval = dep
            if dsem == sem and eng == "tensor":
                return
            if waits.get(dsem, -1) < dval:
                waits[dsem] = dval
        for k in reads:
            add(self.last_w.get(k))
        for k in writes:
            add(self.last_w.get(k))
            for r in self.readers.get(k, ()):
                add(r)
        return waits

    def _record(self, sem, val, reads, writes):
        for k in writes:
            self.last_w[k] = (sem, val)
            self.readers[k] = []
        for k in reads:
            self.readers.setdefault(k, []).append((sem, val))

    def op(self, eng, fn, reads=(), writes=(), sem=None):
        sem = sem or self.esem[eng]
        if sem not in self.cnt:
            self.cnt[sem] = 0
        waits = self._edges(eng, sem, reads, writes)
        self.cnt[sem] += 1
        v = self.cnt[sem]
        self.ops[eng].append((fn, waits, sem, v, 1))
        self._record(sem, v, reads, writes)
        return v

    def dma(self, eng, out, in_, reads=(), writes=(), slot="d", **kw):
        qsem = f"_d_{eng}_{slot}"
        if qsem not in self.cnt:
            self.cnt[qsem] = 0
        waits = self._edges(eng, qsem, reads, writes)
        if self.cnt[qsem] > 0:
            if waits.get(qsem, -1) < self.cnt[qsem]:
                waits[qsem] = self.cnt[qsem]
        self.cnt[qsem] += 16
        v = self.cnt[qsem]
        fn = lambda E: E.dma_start(
            out=out, in_=(in_(E) if callable(in_) else in_), **kw)
        self.ops[eng].append((fn, waits, qsem, v, 16))
        self._record(qsem, v, list(reads), list(writes) + [("_dmaslot", qsem)])
        return v

    def emit(self):
        import kernel as _k
        _k._LAST_S = self
        sems = {}
        for s, c in self.cnt.items():
            if c > 0:
                sems[s] = self.ctx.enter_context(self.nc.semaphore(s))
        block = self.ctx.enter_context(self.nc.Block())
        for eng in ENGINES:
            items = self.ops[eng]
            if not items:
                continue

            def body(E, items=items, eng=eng):
                waited = {}
                for fn, waits, sem, v, n in items:
                    for ws, wv in sorted(waits.items()):
                        if waited.get(ws, 0) >= wv:
                            continue
                        E.wait_ge(sems[ws], wv)
                        waited[ws] = wv
                    ins = fn(E)
                    assert ins is not None
                    ins.then_inc(sems[sem], n)
                    self.n_emitted[eng] += 1
            getattr(block, eng)(body)


_NAMEC = [0]


def sbuf(ctx, nc, name, shape, dtype):
    _NAMEC[0] += 1
    return ctx.enter_context(nc.sbuf_tensor(f"sb{_NAMEC[0]}_{name}", shape, dtype))


def psum(ctx, nc, name, shape, dtype=F32):
    _NAMEC[0] += 1
    return ctx.enter_context(nc.psum_tensor(f"ps{_NAMEC[0]}_{name}", shape, dtype))


def barrier(S):
    S._barno = getattr(S, "_barno", 0) + 1
    b = S._barno
    slots = [("_dmaslot", s) for s in S.cnt
             if s.startswith("_d_") and S.cnt[s] > 0]
    for eng in ENGINES:
        S.op(eng, lambda E: E.nop(nofuse=True), reads=slots,
             writes=((("_bar", b, eng)),), sem=f"_bs_{eng}")
    for eng in ENGINES:
        S.op(eng, lambda E: E.nop(nofuse=True),
             reads=[("_bar", b, e2) for e2 in ENGINES], sem=f"_bs_{eng}")


CONST_VALS = {"eps": 1e-5, "mhalf": -0.5, "half": 0.5, "m128": -128.0,
              "quart": 0.25, "mtwenty": -20.0, "one": 1.0, "big": 1e5,
              "eighth": 0.125, "mone": -1.0}
CONST_IDX = {k: i for i, k in enumerate(CONST_VALS)}


def FC(C, name, p=128):
    i = CONST_IDX[name]
    return C["fc"][0:p, i:i + 1]


def host_fconst():
    v = np.array(list(CONST_VALS.values()), dtype=np.float32)
    return np.tile(v[None, :], (128, 1))


def dma_tile(S, xs, k, x, d, fg):
    n = len(xs)
    xb = xs[k % n]
    S.dma("sync", xb[:], x[d * 128:(d + 1) * 128, fg * 512:(fg + 1) * 512],
          writes=[("xs", k % n)], slot=f"xs{k % n}")
    return xb[:], [("xs", k % n)]


def emit_ln_stats_generic(S, nc, C, tag, m_row, r_row, ntok, xs, sq,
                          ps_sx, ps_sq, get_tile, ones_x=None):
    if ones_x is None:
        ones_x = (C["onesA"], ("cA",))
    nfg = ntok // 512
    k = 0
    for fg in range(nfg):
        sub = 0
        for d in range(DCH):
            ap, rds = get_tile(d, fg, k)
            qb = sq[k % 2]
            S.op("scalar", lambda E, ap=ap, qb=qb: E.activation(
                out=qb[:], in_=ap, func=AF.Square),
                reads=rds, writes=[(tag + "sq", k % 2)])
            S.op("tensor", lambda E, ap=ap, sub=sub, d=d: E.matmul(
                ps_sx[sub][:], ones_x[0][:], ap,
                start=(d == 0), stop=(d == DCH - 1)),
                reads=rds + [ones_x[1], (tag + "psxr", sub)],
                writes=[(tag + "psx", sub)])
            S.op("tensor", lambda E, qb=qb, sub=sub, d=d, k=k: E.matmul(
                ps_sq[sub][:], C["onesA"][:], qb[:],
                start=(d == 0), stop=(d == DCH - 1)),
                reads=[(tag + "sq", k % 2), ("cA",), (tag + "psqr", sub)],
                writes=[(tag + "psq", sub)])
            k += 1
        sub = 0
        c0 = fg * 512
        S.op("vector", lambda E, sub=sub, c0=c0: E.tensor_scalar(
            out=m_row[:, c0:c0 + 512], in0=ps_sx[sub][:],
            scalar1=1.0 / D, scalar2=None, op0=ALU.mult),
            reads=[(tag + "psx", sub)],
            writes=[(tag + "mrow", fg), (tag + "psxr", sub)])
        S.op("vector", lambda E, sub=sub, c0=c0: E.tensor_scalar(
            out=r_row[:, c0:c0 + 512], in0=ps_sq[sub][:],
            scalar1=1.0 / D, scalar2=None, op0=ALU.mult),
            reads=[(tag + "psq", sub)],
            writes=[(tag + "vrow", fg), (tag + "psqr", sub)])
        S.op("vector", lambda E, c0=c0: E.scalar_tensor_tensor(
            out=sq[0][0:1, 0:512], in0=m_row[:, c0:c0 + 512],
            scalar=-1.0, in1=m_row[:, c0:c0 + 512],
            op0=ALU.mult, op1=ALU.mult),
            reads=[(tag + "mrow", fg)],
            writes=[(tag + "msq",), (tag + "sq", 0)])
        S.op("vector", lambda E, c0=c0: E.tensor_add(
            out=r_row[:, c0:c0 + 512], in0=r_row[:, c0:c0 + 512],
            in1=sq[0][0:1, 0:512]),
            reads=[(tag + "msq",), (tag + "sq", 0), (tag + "vrow", fg)],
            writes=[(tag + "vrow", fg)])
    S.op("scalar", lambda E: E.activation(
        out=r_row[:], in_=r_row[:], func=AF.Ln, bias=FC(C, "eps", 1),
        scale=1.0),
        reads=[(tag + "vrow", fg) for fg in range(nfg)] + [("cfc",)],
        writes=[(tag + "rln",)])
    S.op("scalar", lambda E: E.activation(
        out=r_row[:], in_=r_row[:], func=AF.Exp, bias=0.0,
        scale=FC(C, "mhalf", 1)),
        reads=[(tag + "rln",), ("cfc",)], writes=[(tag + "rrow",)])



def build_ffn_l0():
    nc = bass.Bass("TRN2", target_bir_lowering=False)
    io = {}
    io["x"] = nc.dram_tensor("x", [D, T], F32, kind="ExternalInput").ap()
    io["wg"] = nc.dram_tensor("wg", [9, D, 2048], F32, kind="ExternalInput").ap()
    io["wu"] = nc.dram_tensor("wu", [9, D, 2048], F32, kind="ExternalInput").ap()
    io["wd"] = nc.dram_tensor("wd", [9, 2048, D], F32, kind="ExternalInput").ap()
    io["wr"] = nc.dram_tensor("wr", [D, 8], F32, kind="ExternalInput").ap()
    io["rb"] = nc.dram_tensor("rb", [8, 1], F32, kind="ExternalInput").ap()
    io["onesA"] = nc.dram_tensor("onesA", [128, 1], F32, kind="ExternalInput").ap()
    io["onesB"] = nc.dram_tensor("onesB", [1, 128], F32, kind="ExternalInput").ap()
    io["ident"] = nc.dram_tensor("ident", [128, 128], F32, kind="ExternalInput").ap()
    io["fconst"] = nc.dram_tensor("fconst", [128, len(CONST_VALS)], F32,
                                  kind="ExternalInput").ap()
    io["y"] = nc.dram_tensor("y", [D, T], F32, kind="ExternalOutput").ap()
    with ExitStack() as ctx:
        S = Sched(nc, ctx)
        emit_ffn_l0(S, nc, ctx, io, tag="f0")
        S.op("gpsimd", lambda E: E.nop(nofuse=True),
             reads=[("f0y", ho, tb) for ho in range(8) for tb in range(16)])
        S.emit()
    return nc


def emit_ffn_l0(S, nc, ctx, io, tag):
    x, wg, wu, wd, wr, rb = (io["x"], io["wg"], io["wu"], io["wd"], io["wr"],
                             io["rb"])
    onesA, onesB, ident, fconst, y = (io["onesA"], io["onesB"], io["ident"],
                                      io["fconst"], io["y"])
    if True:
        C = {}
        C["onesA"] = sbuf(ctx, nc, "onesA", [128, 1], F32)
        C["onesB"] = sbuf(ctx, nc, "onesB", [1, 128], F32)
        C["rb"] = sbuf(ctx, nc, "rb", [8, 1], F32)
        C["wr"] = sbuf(ctx, nc, "wr", [128, 64], BF16)
        C["id"] = sbuf(ctx, nc, "id", [128, 128], F32)
        C["fc"] = sbuf(ctx, nc, "fc", [128, len(CONST_VALS)], F32)
        xln = [sbuf(ctx, nc, f"xln{d}", [128, 2048], BF16) for d in range(DCH)]
        asl = [sbuf(ctx, nc, f"asl{c}", [128, 2048], BF16) for c in range(16)]
        cwtok = sbuf(ctx, nc, "cwtok", [128, 128], F32)
        cw_bc = sbuf(ctx, nc, "cw_bc", [128, 2048], BF16)
        probs = sbuf(ctx, nc, "probs", [8, 2048], F32)
        stage = [sbuf(ctx, nc, f"stage{i}", [128, 512], F32) for i in range(4)]
        wgsl = [sbuf(ctx, nc, f"wgsl{i}", [128, 1024], BF16) for i in range(2)]
        wusl = [sbuf(ctx, nc, f"wusl{i}", [128, 1024], BF16) for i in range(2)]
        wdsl = [sbuf(ctx, nc, f"wdsl{i}", [128, 2048], BF16) for i in range(2)]

        S.dma("sync", C["onesA"][:], onesA, writes=[("cA",)], slot="c")
        S.dma("sync", C["onesB"][:], onesB, writes=[("cB",)], slot="c")
        S.dma("sync", C["rb"][:], rb, writes=[("crb",)], slot="c")
        S.dma("sync", C["id"][:], ident, writes=[("cid",)], slot="c")
        S.dma("sync", C["fc"][:], fconst, writes=[("cfc",)], slot="c")
        S.dma("gpsimd", C["wr"][:].rearrange("p (d e) -> p d e", e=8),
              wr.rearrange("(d p) e -> p d e", p=128), writes=[("cwr",)],
              slot="c")
        S.dma("gpsimd", y, x, slot="y0",
              writes=[(tag + "y", ho, tb) for ho in range(8)
                      for tb in range(16)])

        nhalf = int(os.environ.get("FFN0_HALVES", "2"))
        for half in range(nhalf):
            emit_ffn_l0_half(S, nc, half, x, y, wg, wu, wd, C, xln, asl,
                             cwtok, cw_bc, probs, stage, wgsl, wusl, wdsl, tag)
        barrier(S)


def emit_ffn_l0_half(S, nc, half, x, y, wg, wu, wd, C, xln, asl,
                     cwtok, cw_bc, probs, stage, wgsl, wusl, wdsl, tag="f0"):
    tok0 = half * 2048
    NFG = 4
    xh = lambda d, c0, n: x[d * 128:(d + 1) * 128, tok0 + c0:tok0 + c0 + n]

    with ExitStack() as ctx:
        m_row = sbuf(ctx, nc, "mrow", [1, 2048], F32)
        r_row = sbuf(ctx, nc, "rrow", [1, 2048], F32)
        msq = sbuf(ctx, nc, "msq", [1, 512], F32)
        xs = [sbuf(ctx, nc, f"xs{i}", [128, 512], F32) for i in range(4)]
        sq = [sbuf(ctx, nc, f"sq{i}", [128, 512], F32) for i in range(2)]
        ps_sx = [psum(ctx, nc, f"psx{i}", [1, 512]) for i in range(2)]
        ps_sq = [psum(ctx, nc, f"psq{i}", [1, 512]) for i in range(2)]
        k = 0
        for pair in range(2):
            for d in range(DCH):
                for sub in range(2):
                    fg = pair * 2 + sub
                    xb, qb = xs[k % 4], sq[k % 2]
                    S.dma("sync", xb[:], xh(d, fg * 512, 512),
                          writes=[("xs", k % 4)], slot=f"xs{k % 4}")
                    S.op("scalar", lambda E, xb=xb, qb=qb: E.activation(
                        out=qb[:], in_=xb[:], func=AF.Square),
                        reads=[("xs", k % 4)], writes=[("sq", k % 2)])
                    S.op("tensor", lambda E, xb=xb, sub=sub, d=d: E.matmul(
                        ps_sx[sub][:], C["onesA"][:], xb[:],
                        start=(d == 0), stop=(d == DCH - 1)),
                        reads=[("xs", k % 4), ("cA",), ("psxr", sub)],
                        writes=[("psx", sub)])
                    S.op("tensor", lambda E, qb=qb, sub=sub, d=d, k=k: E.matmul(
                        ps_sq[sub][:], C["onesA"][:], qb[:],
                        start=(d == 0), stop=(d == DCH - 1)),
                        reads=[("sq", k % 2), ("cA",), ("psqr", sub)],
                        writes=[("psq", sub)])
                    k += 1
            for sub in range(2):
                fg = pair * 2 + sub
                c0 = fg * 512
                S.op("vector", lambda E, sub=sub, c0=c0: E.tensor_scalar(
                    out=m_row[:, c0:c0 + 512], in0=ps_sx[sub][:],
                    scalar1=1.0 / D, scalar2=None, op0=ALU.mult),
                    reads=[("psx", sub)], writes=[("mrow", fg), ("psxr", sub)])
                S.op("vector", lambda E, sub=sub, c0=c0: E.tensor_scalar(
                    out=r_row[:, c0:c0 + 512], in0=ps_sq[sub][:],
                    scalar1=1.0 / D, scalar2=None, op0=ALU.mult),
                    reads=[("psq", sub)], writes=[("vrow", fg), ("psqr", sub)])
                S.op("vector", lambda E, c0=c0: E.tensor_mul(
                    out=msq[:], in0=m_row[:, c0:c0 + 512],
                    in1=m_row[:, c0:c0 + 512]),
                    reads=[("mrow", fg)], writes=[("msq",)])
                S.op("vector", lambda E, c0=c0: E.tensor_sub(
                    out=r_row[:, c0:c0 + 512], in0=r_row[:, c0:c0 + 512],
                    in1=msq[:]),
                    reads=[("msq",), ("vrow", fg)], writes=[("vrow", fg)])
        S.op("scalar", lambda E: E.activation(
            out=r_row[:], in_=r_row[:], func=AF.Ln, bias=FC(C, "eps", 1),
            scale=1.0),
            reads=[("vrow", fg) for fg in range(NFG)] + [("cfc",)],
            writes=[("rln",)])
        S.op("scalar", lambda E: E.activation(
            out=r_row[:], in_=r_row[:], func=AF.Exp, bias=0.0,
            scale=FC(C, "mhalf", 1)),
            reads=[("rln",), ("cfc",)], writes=[("rrow",)])

        mb = sbuf(ctx, nc, "mb", [128, 512], F32)
        rbc = sbuf(ctx, nc, "rbc", [128, 512], F32)
        pbc = psum(ctx, nc, "pbc", [128, 512])
        tmp = [sbuf(ctx, nc, f"lntmp{i}", [128, 512], F32) for i in range(2)]
        for fg in range(NFG):
            c0 = fg * 512
            S.op("tensor", lambda E, c0=c0: E.matmul(
                pbc[:], C["onesB"][:], m_row[:, c0:c0 + 512], start=True,
                stop=True),
                reads=[("mrow", fg), ("cB",), ("mbv",)], writes=[("pbc",)])
            S.op("vector", lambda E: E.tensor_copy(out=mb[:], in_=pbc[:]),
                 reads=[("pbc",)], writes=[("mbv",)])
            S.op("tensor", lambda E, c0=c0: E.matmul(
                pbc[:], C["onesB"][:], r_row[:, c0:c0 + 512], start=True,
                stop=True),
                reads=[("rrow",), ("cB",), ("mbv",), ("rbcv",)],
                writes=[("pbc",)])
            S.op("vector", lambda E: E.tensor_copy(out=rbc[:], in_=pbc[:]),
                 reads=[("pbc",)], writes=[("rbcv",)])
            for d in range(DCH):
                xb = xs[d % 4]
                S.dma("sync", xb[:], xh(d, fg * 512, 512),
                      writes=[("xs", d % 4)], slot=f"xs{d % 4}")
                S.op("vector", lambda E, xb=xb, d=d: E.tensor_sub(
                    out=tmp[d % 2][:], in0=xb[:], in1=mb[:]),
                    reads=[("xs", d % 4), ("mbv",)], writes=[("lntmp", d % 2)])
                S.op("vector", lambda E, d=d, c0=c0: E.tensor_mul(
                    out=xln[d][:, c0:c0 + 512], in0=tmp[d % 2][:], in1=rbc[:]),
                    reads=[("lntmp", d % 2), ("rbcv",)], writes=[("xln", d, fg)])
        ps_r = psum(ctx, nc, "ps_r", [8, 512])
        for fg in range(NFG):
            c0 = fg * 512
            S.op("tensor", lambda E, c0=c0: [
                E.matmul(ps_r[:], C["wr"][:, 8 * d:8 * d + 8],
                         xln[d][:, c0:c0 + 512],
                         start=(d == 0), stop=(d == DCH - 1))
                for d in range(DCH)][-1],
                reads=[("xln", d, fg) for d in range(DCH)]
                + [("cwr",), ("psrr",)],
                writes=[("psr",)])
            S.op("scalar", lambda E, c0=c0, fg=fg: E.activation(
                out=probs[:, c0:c0 + 512], in_=ps_r[:], func=AF.Sigmoid,
                bias=C["rb"][:], scale=1.0),
                reads=[("psr",), ("crb",)], writes=[("probs", fg), ("psrr",)])
        ps_pt = psum(ctx, nc, "ps_pt", [128, 8])
        ptok = sbuf(ctx, nc, "ptok", [128, 8], F32)
        mx8 = sbuf(ctx, nc, "mx8", [128, 8], F32)
        ge8 = sbuf(ctx, nc, "ge8", [128, 8], F32)
        ntile = int(os.environ.get("FFN0_NTILE", "16"))
        for tile in range(ntile):
            S.op("tensor", lambda E, tile=tile: E.transpose(
                ps_pt[:], probs[:, tile * 128:(tile + 1) * 128],
                C["id"][0:8, 0:8]),
                reads=[("probs", tile // 4), ("cid",), ("ptokv",)],
                writes=[("pspt",)])
            S.op("vector", lambda E: E.tensor_copy(out=ptok[:], in_=ps_pt[:]),
                 reads=[("pspt",)], writes=[("ptokv",)])
            S.op("vector", lambda E: E.max(out=mx8[:], in_=ptok[:]),
                 reads=[("ptokv",)], writes=[("mx8",)])
            S.op("vector", lambda E: E.tensor_scalar(
                out=ge8[:], in0=ptok[:], scalar1=mx8[:, 1:2], scalar2=None,
                op0=ALU.is_ge),
                reads=[("mx8",), ("ptokv",)], writes=[("ge8",)])
            S.op("vector", lambda E, tile=tile: E.tensor_mul(
                out=cwtok[:, tile * 8:(tile + 1) * 8], in0=ptok[:], in1=ge8[:]),
                reads=[("ge8",), ("ptokv",)], writes=[("cwtok", tile)])
        barrier(S)

    with ExitStack() as ctx:
        ps_g = [psum(ctx, nc, f"psg{i}", [128, 512]) for i in range(2)]
        ps_u = [psum(ctx, nc, f"psu{i}", [128, 512]) for i in range(2)]
        ps_z = [psum(ctx, nc, f"psz{i}", [128, 512]) for i in range(2)]
        ps_row = psum(ctx, nc, "ps_row", [1, 512])
        ps_cb = psum(ctx, nc, "ps_cb", [128, 512])
        cwrow = sbuf(ctx, nc, "cwrow", [1, 2048], F32)
        sil = [sbuf(ctx, nc, f"sil{i}", [128, 512], BF16) for i in range(2)]
        a0b = [sbuf(ctx, nc, f"a0b{i}", [128, 512], BF16) for i in range(2)]
        for e in range(int(os.environ.get("FFN0_NEXP", "9"))):
            if e > 0:
                for grp in range(4):
                    S.op("tensor", lambda E, grp=grp, e=e: [
                        E.transpose(ps_row[:, i * 128:(i + 1) * 128],
                                    cwtok[:, (grp * 4 + i) * 8 + (e - 1):
                                          (grp * 4 + i) * 8 + e],
                                    C["id"][:])
                        for i in range(4)][-1],
                        reads=[("cwtok", grp * 4 + i) for i in range(4)]
                        + [("cid",), ("psrowr",)],
                        writes=[("psrow",)])
                    S.op("vector", lambda E, grp=grp: E.tensor_copy(
                        out=cwrow[:, grp * 512:(grp + 1) * 512], in_=ps_row[:]),
                        reads=[("psrow",)],
                        writes=[("cwrowv", grp), ("psrowr",)])
                    S.op("tensor", lambda E, grp=grp: E.matmul(
                        ps_cb[:], C["onesB"][:],
                        cwrow[:, grp * 512:(grp + 1) * 512],
                        start=True, stop=True),
                        reads=[("cwrowv", grp), ("cB",), ("pscbr",)],
                        writes=[("pscb",)])
                    S.op("vector", lambda E, grp=grp: E.tensor_copy(
                        out=cw_bc[:, grp * 512:(grp + 1) * 512], in_=ps_cb[:]),
                        reads=[("pscb",)], writes=[("cwbcv", grp), ("pscbr",)])
            for c in range(16):
                sl = c % 2
                S.dma("gpsimd", wgsl[sl][:].rearrange("p (d c) -> p d c", c=128),
                      wg[e, :, c * 128:(c + 1) * 128]
                      .rearrange("(d p) c -> p d c", p=128),
                      writes=[("wgsl", sl)], slot=f"wg{sl}")
                S.dma("gpsimd", wusl[sl][:].rearrange("p (d c) -> p d c", c=128),
                      wu[e, :, c * 128:(c + 1) * 128]
                      .rearrange("(d p) c -> p d c", p=128),
                      writes=[("wusl", sl)], slot=f"wu{sl}")
                for tb in range(4):
                    pp = (c * 4 + tb) % 2
                    S.op("tensor", lambda E, sl=sl, tb=tb, pp=pp: [
                        E.matmul(ps_g[pp][:],
                                 wgsl[sl][:, d * 128:(d + 1) * 128],
                                 xln[d][:, tb * 512:(tb + 1) * 512],
                                 start=(d == 0), stop=(d == DCH - 1))
                        for d in range(DCH)][-1],
                        reads=[("wgsl", sl)]
                        + [("xln", d, tb) for d in range(DCH)],
                        writes=[("psg", pp)])
                    S.op("tensor", lambda E, sl=sl, tb=tb, pp=pp: [
                        E.matmul(ps_u[pp][:],
                                 wusl[sl][:, d * 128:(d + 1) * 128],
                                 xln[d][:, tb * 512:(tb + 1) * 512],
                                 start=(d == 0), stop=(d == DCH - 1))
                        for d in range(DCH)][-1],
                        reads=[("wusl", sl)]
                        + [("xln", d, tb) for d in range(DCH)],
                        writes=[("psu", pp)])
                    S.op("scalar", lambda E, pp=pp: E.activation(
                        out=sil[pp][:], in_=ps_g[pp][:], func=AF.Silu),
                        reads=[("psg", pp)], writes=[("sil", pp)])
                    S.op("vector", lambda E, pp=pp: E.tensor_mul(
                        out=a0b[pp][:], in0=sil[pp][:], in1=ps_u[pp][:]),
                        reads=[("sil", pp), ("psu", pp)],
                        writes=[("a0b", pp)])
                    if e > 0:
                        S.op("vector", lambda E, c=c, tb=tb, pp=pp:
                             E.tensor_mul(
                                 out=asl[c][:, tb * 512:(tb + 1) * 512],
                                 in0=a0b[pp][:],
                                 in1=cw_bc[:, tb * 512:(tb + 1) * 512]),
                             reads=[("a0b", pp), ("cwbcv", tb)],
                             writes=[("asl", c, tb)])
                    else:
                        S.op("vector", lambda E, c=c, tb=tb, pp=pp:
                             E.tensor_copy(
                                 out=asl[c][:, tb * 512:(tb + 1) * 512],
                                 in_=a0b[pp][:]),
                             reads=[("a0b", pp)], writes=[("asl", c, tb)])
            for ho in range(8):
                sl = ho % 2
                S.dma("gpsimd", wdsl[sl][:].rearrange("p (c h) -> p c h", h=128),
                      wd[e, :, ho * 128:(ho + 1) * 128]
                      .rearrange("(c p) h -> p c h", p=128),
                      writes=[("wdsl", sl)], slot=f"wd{sl}")
                for tq2 in range(4):
                    sg = stage[tq2 % 4]
                    pz = tq2 % 2
                    S.op("tensor", lambda E, sl=sl, tq2=tq2, pz=pz: [
                        E.matmul(ps_z[pz][:],
                                 wdsl[sl][:, c * 128:(c + 1) * 128],
                                 asl[c][:, tq2 * 512:(tq2 + 1) * 512],
                                 start=(c == 0), stop=(c == 15))
                        for c in range(16)][-1],
                        reads=[("wdsl", sl)]
                        + [("asl", c, tq2) for c in range(16)],
                        writes=[("psz", pz)])
                    S.op("vector", lambda E, sg=sg, pz=pz: E.tensor_scalar(
                        out=sg[:], in0=ps_z[pz][:], scalar1=0.5, scalar2=None,
                        op0=ALU.mult),
                        reads=[("psz", pz), ("stgr", tq2 % 4)],
                        writes=[("stg", tq2 % 4), ("psz", pz)])
                    yk = [(tag + "y", ho, half * 8 + tq2 * 2),
                          (tag + "y", ho, half * 8 + tq2 * 2 + 1)]
                    S.dma("gpsimd",
                          y[ho * 128:(ho + 1) * 128,
                            tok0 + tq2 * 512:tok0 + (tq2 + 1) * 512],
                          sg[:],
                          reads=[("stg", tq2 % 4)] + yk,
                          writes=yk + [("stgr", tq2 % 4)],
                          slot=f"y{ho % 4}", accum_op=ALU.add)
        barrier(S)



def build_ffn_l1():
    nc = bass.Bass("TRN2", target_bir_lowering=False)
    io = {}
    io["x"] = nc.dram_tensor("x", [D, T], F32, kind="ExternalInput").ap()
    io["q1"] = nc.dram_tensor("q1", [D, 4096], I32, kind="ExternalInput").ap()
    io["q2"] = nc.dram_tensor("q2", [4096, D], I32, kind="ExternalInput").ap()
    io["sc1"] = nc.dram_tensor("sc1", [8, 4096], F32, kind="ExternalInput").ap()
    io["sc2"] = nc.dram_tensor("sc2", [32, D], F32, kind="ExternalInput").ap()
    io["b1"] = nc.dram_tensor("b1", [128, 32], F32, kind="ExternalInput").ap()
    io["b2"] = nc.dram_tensor("b2", [128, 8], F32, kind="ExternalInput").ap()
    io["gam"] = nc.dram_tensor("gam", [128, 8], F32, kind="ExternalInput").ap()
    io["bet"] = nc.dram_tensor("bet", [128, 8], F32, kind="ExternalInput").ap()
    io["onesA"] = nc.dram_tensor("onesA", [128, 1], F32, kind="ExternalInput").ap()
    io["onesB"] = nc.dram_tensor("onesB", [1, 128], F32, kind="ExternalInput").ap()
    io["fconst"] = nc.dram_tensor("fconst", [128, len(CONST_VALS)], F32,
                                  kind="ExternalInput").ap()
    io["y"] = nc.dram_tensor("y", [D, T], F32, kind="ExternalOutput").ap()
    io["w2s"] = nc.dram_tensor("w2s", [4096, D], BF16, kind="Internal").ap()
    with ExitStack() as ctx:
        S = Sched(nc, ctx)
        emit_ffn_l1(S, nc, ctx, io, tag="f1")
        S.op("gpsimd", lambda E: E.nop(nofuse=True),
             reads=[("f1yout", ho, t) for ho in range(8) for t in range(16)])
        S.emit()
    return nc


def emit_ffn_l1(S, nc, ctx, io, tag):
    x, q1, q2, sc1, sc2 = io["x"], io["q1"], io["q2"], io["sc1"], io["sc2"]
    b1, b2, gam, bet = io["b1"], io["b2"], io["gam"], io["bet"]
    onesA, onesB, fconst, y, w2s = (io["onesA"], io["onesB"], io["fconst"],
                                    io["y"], io["w2s"])
    if True:
        C = {}
        C["onesA"] = sbuf(ctx, nc, "onesA", [128, 1], F32)
        C["onesB"] = sbuf(ctx, nc, "onesB", [1, 128], F32)
        C["fc"] = sbuf(ctx, nc, "fc", [128, len(CONST_VALS)], F32)
        b1_sb = sbuf(ctx, nc, "b1", [128, 32], F32)
        b2_sb = sbuf(ctx, nc, "b2", [128, 8], F32)
        gam_sb = sbuf(ctx, nc, "gam", [128, 8], F32)
        bet_sb = sbuf(ctx, nc, "bet", [128, 8], F32)
        for t_, src, key in ((C["onesA"], onesA, "cA"), (C["onesB"], onesB, "cB"),
                             (C["fc"], fconst, "cfc"), (b1_sb, b1, "cb1"),
                             (b2_sb, b2, "cb2"), (gam_sb, gam, "cgam"),
                             (bet_sb, bet, "cbet")):
            S.dma("sync", t_[:], src, writes=[(key,)], slot="c")
        w1 = [sbuf(ctx, nc, f"w1_{d}", [128, 4096], BF16) for d in range(8)]
        w2sl = [sbuf(ctx, nc, f"w2sl{i}", [128, 4096], BF16) for i in range(2)]
        C["onesAb"] = sbuf(ctx, nc, "onesAb", [128, 1], BF16)
        S.op("vector", lambda E: E.tensor_copy(out=C["onesAb"][:],
                                               in_=C["onesA"][:]),
             reads=[("cA",)], writes=[("cAb",)])
        m1_row = sbuf(ctx, nc, "m1row", [1, T], F32)
        r1_row = sbuf(ctx, nc, "r1row", [1, T], F32)

        with ExitStack() as c3:
            qb = [sbuf(c3, nc, f"qb{i}", [128, 2048], I32) for i in range(2)]
            qf = sbuf(c3, nc, "qf", [128, 2048], F32)
            scb = sbuf(c3, nc, "scb", [128, 2048], F32)
            scr = sbuf(c3, nc, "scr", [1, 2048], F32)
            pbc2 = psum(c3, nc, "pbc2", [128, 512])
            w2d = [sbuf(c3, nc, f"w2d{i}", [128, 1024], BF16) for i in range(2)]

            def dequant(dst, dkey, qsrc, scsrc, ncols):
                for hh in range(max(1, ncols // 2048)):
                    c0 = hh * 2048
                    nn = min(2048, ncols)
                    hb = qb[hh % 2]
                    S.dma("sync", scr[:, 0:nn], scsrc[:, c0:c0 + nn],
                          writes=[("scr",)], slot="scr")
                    for fg in range(nn // 512):
                        S.op("tensor", lambda E, fg=fg: E.matmul(
                            pbc2[:], C["onesB"][:],
                            scr[:, fg * 512:(fg + 1) * 512],
                            start=True, stop=True),
                            reads=[("scr",), ("cB",), ("scbv", fg)],
                            writes=[("pbc2",)])
                        S.op("vector", lambda E, fg=fg: E.tensor_copy(
                            out=scb[:, fg * 512:(fg + 1) * 512], in_=pbc2[:]),
                            reads=[("pbc2",)], writes=[("scbv", fg)])
                    S.dma("sync", hb[:, 0:nn], qsrc[:, c0:c0 + nn],
                          writes=[("qb", hh % 2)], slot=f"q{hh % 2}")
                    S.op("vector", lambda E, hb=hb, nn=nn: E.tensor_copy(
                        out=qf[:, 0:nn], in_=hb[:, 0:nn]),
                        reads=[("qb", hh % 2)], writes=[("qf",)])
                    S.op("vector", lambda E, c0=c0, nn=nn, dst=dst:
                         E.scalar_tensor_tensor(
                             out=dst[:, c0:c0 + nn], in0=qf[:, 0:nn],
                             scalar=-128.0, in1=scb[:, 0:nn],
                             op0=ALU.add, op1=ALU.mult),
                         reads=[("qf",)] + [("scbv", fg)
                                            for fg in range(nn // 512)],
                         writes=[dkey])
            for d in range(8):
                dequant(w1[d], ("w1", d), q1[d * 128:(d + 1) * 128, :],
                        sc1[d:d + 1, :], 4096)
            for f in range(32):
                dequant(w2d[f % 2], ("w2d", f % 2),
                        q2[f * 128:(f + 1) * 128, :], sc2[f:f + 1, :], 1024)
                S.dma("sync", w2s[f * 128:(f + 1) * 128, :], w2d[f % 2][:],
                      reads=[("w2d", f % 2)], writes=[("w2s", f)],
                      slot=f"w2s{f % 2}")
            barrier(S)

        with ExitStack() as c2:
            xs2 = [sbuf(c2, nc, f"xs{i}", [128, 512], F32) for i in range(4)]
            sq2 = [sbuf(c2, nc, f"sq{i}", [128, 512], F32) for i in range(2)]
            psx2 = [psum(c2, nc, "psx", [1, 512])]
            psq2 = [psum(c2, nc, "psq", [1, 512])]
            emit_ln_stats_generic(
                S, nc, C, "L1", m1_row, r1_row, T, xs2, sq2, psx2, psq2,
                lambda d, fg, k: dma_tile(S, xs2, k, x, d, fg))
            barrier(S)

        u = [sbuf(ctx, nc, f"u{d}", [128, 1024], BF16) for d in range(DCH)]
        h = [sbuf(ctx, nc, f"h{d}", [128, 1024], BF16) for d in range(DCH)]
        m2_row = sbuf(ctx, nc, "m2row", [1, 1024], F32)
        r2_row = sbuf(ctx, nc, "r2row", [1, 1024], F32)
        xs = [sbuf(ctx, nc, f"xsq{i}", [128, 512], F32) for i in range(4)]
        sq = [sbuf(ctx, nc, f"sqq{i}", [128, 512], F32) for i in range(2)]
        ps_sx = [psum(ctx, nc, "psx", [1, 512])]
        ps_sq = [psum(ctx, nc, "psq", [1, 512])]
        mb = sbuf(ctx, nc, "mb", [128, 512], F32)
        rbc = sbuf(ctx, nc, "rbc", [128, 512], F32)
        pbc = psum(ctx, nc, "pbc", [128, 512])
        tmp = [sbuf(ctx, nc, f"lt{i}", [128, 512], F32) for i in range(2)]
        ps_y = [psum(ctx, nc, f"psy{i}", [128, 256]) for i in range(2)]
        ps_z = [psum(ctx, nc, f"psz{i}", [128, 256]) for i in range(2)]
        y1 = [sbuf(ctx, nc, f"y1_{f}", [128, 256], BF16) for f in range(32)]
        zb = sbuf(ctx, nc, "zb", [128, 256], F32)
        usl = sbuf(ctx, nc, "usl", [128, 256], F32)
        o1 = sbuf(ctx, nc, "o1", [128, 256], F32)
        ost = [sbuf(ctx, nc, f"ost{i}", [128, 256], F32) for i in range(4)]

        def bcast(row_ap, dst, rkeys, dkey):
            nn = row_ap.shape[-1]
            S.op("tensor", lambda E, nn=nn, row_ap=row_ap, dst=dst: E.matmul(
                pbc[:, 0:nn], C["onesB"][:], row_ap, start=True, stop=True),
                reads=list(rkeys) + [("cB",), (dkey,)], writes=[("pbcq",)])
            S.op("vector", lambda E, nn=nn, dst=dst: E.tensor_copy(
                out=dst[:, 0:nn], in_=pbc[:, 0:nn]),
                reads=[("pbcq",)], writes=[(dkey,)])

        for qt in range(4):
            tq0 = qt * 1024
            for fg in range(2):
                c0 = tq0 + fg * 512
                bcast(m1_row[:, c0:c0 + 512], mb, [("L1mrow", c0 // 512)],
                      "mbv")
                bcast(r1_row[:, c0:c0 + 512], rbc, [("L1rrow",)], "rbcv")
                for d in range(DCH):
                    S.dma("sync", xs[d % 4][:],
                          x[d * 128:(d + 1) * 128, c0:c0 + 512],
                          writes=[("xs", d % 4)], slot=f"xs{d % 4}")
                    S.op("vector", lambda E, d=d: E.tensor_sub(
                        out=tmp[d % 2][:], in0=xs[d % 4][:], in1=mb[:]),
                        reads=[("xs", d % 4), ("mbv",)],
                        writes=[("lt", d % 2)])
                    S.op("vector", lambda E, d=d, fg=fg: E.tensor_mul(
                        out=u[d][:, fg * 512:(fg + 1) * 512],
                        in0=tmp[d % 2][:], in1=rbc[:]),
                        reads=[("lt", d % 2), ("rbcv",)],
                        writes=[("u", d, fg)])
            emit_ln_stats_generic(
                S, nc, C, "L2", m2_row, r2_row, 1024, xs, sq, ps_sx, ps_sq,
                lambda d, fg, k: (u[d][:, fg * 512:(fg + 1) * 512],
                                  [("u", d, fg)]),
                ones_x=(C["onesAb"], ("cAb",)))
            for fg in range(2):
                c0 = fg * 512
                bcast(m2_row[:, c0:c0 + 512], mb, [("L2mrow", fg)], "mbv")
                bcast(r2_row[:, c0:c0 + 512], rbc, [("L2rrow",)], "rbcv")
                for d in range(DCH):
                    S.op("vector", lambda E, d=d, c0=c0: E.tensor_sub(
                        out=tmp[d % 2][:], in0=u[d][:, c0:c0 + 512], in1=mb[:]),
                        reads=[("u", d, fg), ("mbv",)], writes=[("lt", d % 2)])
                    S.op("vector", lambda E, d=d: E.tensor_mul(
                        out=tmp[d % 2][:], in0=tmp[d % 2][:], in1=rbc[:]),
                        reads=[("lt", d % 2), ("rbcv",)], writes=[("lt", d % 2)])
                    S.op("vector", lambda E, d=d, c0=c0: E.tensor_scalar(
                        out=h[d][:, c0:c0 + 512], in0=tmp[d % 2][:],
                        scalar1=gam_sb[:, d:d + 1], scalar2=bet_sb[:, d:d + 1],
                        op0=ALU.mult, op1=ALU.add),
                        reads=[("lt", d % 2), ("cgam",), ("cbet",)],
                        writes=[("h", d, fg)])
            for t in range(4):
                c0 = t * 256
                gc0 = tq0 + c0
                for f in range(32):
                    pp = f % 2
                    S.op("tensor", lambda E, f=f, pp=pp, c0=c0: [
                        E.matmul(ps_y[pp][:], w1[d][:, f * 128:(f + 1) * 128],
                                 h[d][:, c0:c0 + 256],
                                 start=(d == 0), stop=(d == DCH - 1))
                        for d in range(DCH)][-1],
                        reads=[("w1", d) for d in range(DCH)]
                        + [("h", d, c0 // 512) for d in range(DCH)]
                        + [("psyr", pp)],
                        writes=[("psy", pp)])
                    S.op("scalar", lambda E, f=f, pp=pp: E.activation(
                        out=y1[f][:], in_=ps_y[pp][:], func=AF.Silu,
                        bias=b1_sb[:, f:f + 1], scale=1.0),
                        reads=[("psy", pp), ("cb1",)],
                        writes=[("y1", f), ("psyr", pp)])
                bcast(m1_row[:, gc0:gc0 + 256], mb, [("L1mrow", gc0 // 512)],
                      "mbv")
                bcast(r1_row[:, gc0:gc0 + 256], rbc, [("L1rrow",)], "rbcv")
                for ho in range(8):
                    pz = ho % 2
                    S.dma("sync",
                          w2sl[ho % 2][:].rearrange("p (c h) -> p c h", h=128),
                          w2s[:, ho * 128:(ho + 1) * 128]
                          .rearrange("(c p) h -> p c h", p=128),
                          reads=[("w2s", f) for f in range(32)],
                          writes=[("w2sl", ho % 2)], slot=f"wsl{ho % 2}")
                    S.op("tensor", lambda E, ho=ho, pz=pz: [
                        E.matmul(ps_z[pz][:],
                                 w2sl[ho % 2][:, f * 128:(f + 1) * 128],
                                 y1[f][:], start=(f == 0), stop=(f == 31))
                        for f in range(32)][-1],
                        reads=[("w2sl", ho % 2)]
                        + [("y1", f) for f in range(32)] + [("pszr", pz)],
                        writes=[("psz", pz)])
                    S.op("scalar", lambda E, ho=ho, pz=pz: E.activation(
                        out=zb[:], in_=ps_z[pz][:], func=AF.Identity,
                        bias=b2_sb[:, ho:ho + 1], scale=1.0),
                        reads=[("psz", pz), ("cb2",), ("zbr",)],
                        writes=[("zb",), ("pszr", pz)])
                    S.dma("sync", xs[ho % 4][:, 0:256],
                          x[ho * 128:(ho + 1) * 128, gc0:gc0 + 256],
                          writes=[("xs", ho % 4)], slot=f"xs{ho % 4}")
                    S.op("vector", lambda E, ho=ho: E.tensor_sub(
                        out=usl[:], in0=xs[ho % 4][:, 0:256], in1=mb[:, 0:256]),
                        reads=[("xs", ho % 4), ("mbv",)], writes=[("usl",)])
                    S.op("vector", lambda E: E.tensor_mul(
                        out=usl[:], in0=usl[:], in1=rbc[:, 0:256]),
                        reads=[("rbcv",)], writes=[("usl",)])
                    S.op("vector", lambda E, ho=ho: E.scalar_tensor_tensor(
                        out=o1[:], in0=usl[:], scalar=0.5,
                        in1=xs[ho % 4][:, 0:256],
                        op0=ALU.mult, op1=ALU.add),
                        reads=[("usl",), ("xs", ho % 4)], writes=[("o1",)])
                    si = (t * 8 + ho) % 4
                    S.op("vector", lambda E, si=si: E.scalar_tensor_tensor(
                        out=ost[si][:], in0=zb[:], scalar=0.25, in1=o1[:],
                        op0=ALU.mult, op1=ALU.add),
                        reads=[("zb",), ("o1",), ("ostr", si)],
                        writes=[("ost", si), ("zbr",)])
                    S.dma("gpsimd",
                          y[ho * 128:(ho + 1) * 128, gc0:gc0 + 256],
                          ost[si][:],
                          reads=[("ost", si)],
                          writes=[(tag + "yout", ho, qt * 4 + t),
                                  ("ostr", si)],
                          slot=f"yo{ho % 4}")
        barrier(S)



def build_qkv0(rope=True):
    nc = bass.Bass("TRN2", target_bir_lowering=False)
    io = {}
    io["x"] = nc.dram_tensor("x", [D, T], F32, kind="ExternalInput").ap()
    io["qkvw"] = nc.dram_tensor("qkvw", [D, 3072], F32, kind="ExternalInput").ap()
    io["qkvb"] = nc.dram_tensor("qkvb", [128, 24], F32, kind="ExternalInput").ap()
    io["onesA"] = nc.dram_tensor("onesA", [128, 1], F32, kind="ExternalInput").ap()
    io["onesB"] = nc.dram_tensor("onesB", [1, 128], F32, kind="ExternalInput").ap()
    io["fconst"] = nc.dram_tensor("fconst", [128, len(CONST_VALS)], F32,
                                  kind="ExternalInput").ap()
    io["qkvbp"] = nc.dram_tensor("qkvbp", [128, 24], F32,
                                 kind="ExternalInput").ap()
    io["cost"] = nc.dram_tensor("cost", [128, T], F32, kind="ExternalInput").ap()
    io["sint"] = nc.dram_tensor("sint", [128, T], F32, kind="ExternalInput").ap()
    io["qro"] = nc.dram_tensor("qro", [8, 128, T], BF16, kind="ExternalOutput").ap()
    io["kro"] = nc.dram_tensor("kro", [8, 128, T], BF16, kind="ExternalOutput").ap()
    io["vtm"] = nc.dram_tensor("vtm", [8, 128, T], BF16, kind="ExternalOutput").ap()
    with ExitStack() as ctx:
        S = Sched(nc, ctx)
        emit_qkv0(S, nc, ctx, io, tag="a0q")
        S.op("gpsimd", lambda E: E.nop(nofuse=True),
             reads=[("a0qout", w, p) for w in range(2) for p in range(8)]
             + [("a0qoutv", p) for p in range(8)])
        S.emit()
    return nc


def emit_qkv0(S, nc, ctx, io, tag, rope=True, post_qk=None):
    x, qkvw, qkvb = io["x"], io["qkvw"], io["qkvb"]
    onesA, onesB, fconst = io["onesA"], io["onesB"], io["fconst"]
    qro, kro, vtm = io["qro"], io["kro"], io["vtm"]
    if rope:
        qkvbp, cost, sint = io["qkvbp"], io["cost"], io["sint"]
    if True:
        C = {}
        C["onesA"] = sbuf(ctx, nc, "onesA", [128, 1], F32)
        C["onesB"] = sbuf(ctx, nc, "onesB", [1, 128], F32)
        C["fc"] = sbuf(ctx, nc, "fc", [128, len(CONST_VALS)], F32)
        b_sb = sbuf(ctx, nc, "b", [128, 24], F32)
        for t_, src, key in ((C["onesA"], onesA, "cA"), (C["onesB"], onesB, "cB"),
                             (C["fc"], fconst, "cfc"), (b_sb, qkvb, "cb")):
            S.dma("sync", t_[:], src, writes=[(key,)], slot="c")
        if rope:
            bp_sb = sbuf(ctx, nc, "bp", [128, 24], F32)
            cos_sb = sbuf(ctx, nc, "cos", [128, T], F32)
            sin_sb = sbuf(ctx, nc, "sin", [128, T], F32)
            S.dma("sync", bp_sb[:], qkvbp, writes=[("cbp",)], slot="c")
            S.dma("sync", cos_sb[:], cost, writes=[("ccos",)], slot="c")
            S.dma("sync", sin_sb[:], sint, writes=[("csin",)], slot="c")
        xln = [sbuf(ctx, nc, f"xln{d}", [128, T], BF16) for d in range(DCH)]
        m_row = sbuf(ctx, nc, "mrow", [1, T], F32)
        r_row = sbuf(ctx, nc, "rrow", [1, T], F32)

        xs = [sbuf(ctx, nc, f"xs{i}", [128, 512], F32) for i in range(4)]
        sq = [sbuf(ctx, nc, f"sq{i}", [128, 512], F32) for i in range(2)]
        ps_sx = [psum(ctx, nc, "psx", [1, 512])]
        ps_sq = [psum(ctx, nc, "psq", [1, 512])]
        emit_ln_stats_generic(S, nc, C, "L1", m_row, r_row, T, xs, sq,
                              ps_sx, ps_sq,
                              lambda d, fg, k: dma_tile(S, xs, k, x, d, fg))
        mb = sbuf(ctx, nc, "mb", [128, 512], F32)
        rbc = sbuf(ctx, nc, "rbc", [128, 512], F32)
        pbc = psum(ctx, nc, "pbc", [128, 512])
        tmp = [sbuf(ctx, nc, f"lt{i}", [128, 512], F32) for i in range(2)]

        def bcast(row_ap, dst, rkeys, dkey):
            nn = row_ap.shape[-1]
            S.op("tensor", lambda E, nn=nn, row_ap=row_ap: E.matmul(
                pbc[:, 0:nn], C["onesB"][:], row_ap, start=True, stop=True),
                reads=list(rkeys) + [("cB",), (dkey,)], writes=[("pbcq",)])
            S.op("vector", lambda E, nn=nn, dst=dst: E.tensor_copy(
                out=dst[:, 0:nn], in_=pbc[:, 0:nn]),
                reads=[("pbcq",)], writes=[(dkey,)])
        for fg in range(T // 512):
            c0 = fg * 512
            bcast(m_row[:, c0:c0 + 512], mb, [("L1mrow", fg)], "mbv")
            bcast(r_row[:, c0:c0 + 512], rbc, [("L1rrow",)], "rbcv")
            for d in range(DCH):
                S.dma("sync", xs[d % 4][:], x[d * 128:(d + 1) * 128, c0:c0 + 512],
                      writes=[("xs", d % 4)], slot=f"xs{d % 4}")
                S.op("vector", lambda E, d=d: E.tensor_sub(
                    out=tmp[d % 2][:], in0=xs[d % 4][:], in1=mb[:]),
                    reads=[("xs", d % 4), ("mbv",)], writes=[("lt", d % 2)])
                S.op("vector", lambda E, d=d, c0=c0: E.tensor_mul(
                    out=xln[d][:, c0:c0 + 512], in0=tmp[d % 2][:], in1=rbc[:]),
                    reads=[("lt", d % 2), ("rbcv",)], writes=[("xln", d, fg)])
        barrier(S)

        wsl = [sbuf(ctx, nc, f"wsl{i}", [128, 1024], BF16) for i in range(3)]
        wpm = sbuf(ctx, nc, "wpm", [128, 1024], BF16)
        ps_q = psum(ctx, nc, "ps_q", [128, 512])
        ps_qp = psum(ctx, nc, "ps_qp", [128, 512])
        ps_v = psum(ctx, nc, "ps_v", [128, 128])
        t1 = sbuf(ctx, nc, "t1", [128, 512], F32)
        t2 = sbuf(ctx, nc, "t2", [128, 512], F32)
        t3 = sbuf(ctx, nc, "t3", [128, 512], F32)
        qout = sbuf(ctx, nc, "qout", [128, T], BF16)
        vout = sbuf(ctx, nc, "vout", [128, T], BF16)

        def load_w(c, sl, key):
            S.dma("gpsimd", wsl[sl][:].rearrange("p (d c) -> p d c", c=128),
                  qkvw[:, c * 128:(c + 1) * 128]
                  .rearrange("(d p) c -> p d c", p=128),
                  writes=[key], slot=f"w{sl}")

        for p in range(8):
            for which, cbase, outdram in ((0, p, qro), (1, 8 + p, kro)):
                load_w(cbase, which, ("wq", which))
                pass
                if rope:
                    S.op("vector", lambda E, which=which: E.tensor_copy(
                        out=wpm[:].rearrange("p (a c) -> p a c", c=64)[:, :, 0:32],
                        in_=wsl[which][:]
                        .rearrange("p (a c) -> p a c", c=64)[:, :, 32:64]),
                        reads=[("wq", which)], writes=[("wpm",)])
                    S.op("vector", lambda E, which=which: E.tensor_copy(
                        out=wpm[:].rearrange("p (a c) -> p a c", c=64)[:, :, 32:64],
                        in_=wsl[which][:]
                        .rearrange("p (a c) -> p a c", c=64)[:, :, 0:32]),
                        reads=[("wq", which)], writes=[("wpm",)])
                for tg in range(8):
                    c0 = tg * 512
                    S.op("tensor", lambda E, which=which, c0=c0: [
                        E.matmul(ps_q[:], wsl[which][:, d * 128:(d + 1) * 128],
                                 xln[d][:, c0:c0 + 512],
                                 start=(d == 0), stop=(d == DCH - 1))
                        for d in range(DCH)][-1],
                        reads=[("wq", which)]
                        + [("xln", d, c0 // 512) for d in range(DCH)]
                        + [("psqr",)],
                        writes=[("psq",)])
                    if rope:
                        S.op("tensor", lambda E, c0=c0: [
                            E.matmul(ps_qp[:], wpm[:, d * 128:(d + 1) * 128],
                                     xln[d][:, c0:c0 + 512],
                                     start=(d == 0), stop=(d == DCH - 1))
                            for d in range(DCH)][-1],
                            reads=[("wpm",)]
                            + [("xln", d, c0 // 512) for d in range(DCH)]
                            + [("psqpr",)],
                            writes=[("psqp",)])
                        S.op("scalar", lambda E, cbase=cbase: E.activation(
                            out=t1[:], in_=ps_q[:], func=AF.Identity,
                            bias=b_sb[:, cbase:cbase + 1], scale=1.0),
                            reads=[("psq",), ("cb",)],
                            writes=[("t1",), ("psqr",)])
                        S.op("scalar", lambda E, cbase=cbase: E.activation(
                            out=t2[:], in_=ps_qp[:], func=AF.Identity,
                            bias=bp_sb[:, cbase:cbase + 1], scale=1.0),
                            reads=[("psqp",), ("cbp",)],
                            writes=[("t2",), ("psqpr",)])
                        S.op("vector", lambda E, c0=c0: E.tensor_mul(
                            out=t3[:], in0=t1[:], in1=cos_sb[:, c0:c0 + 512]),
                            reads=[("t1",), ("ccos",)], writes=[("t3",)])
                        S.op("vector", lambda E, c0=c0: E.tensor_mul(
                            out=t2[:], in0=t2[:], in1=sin_sb[:, c0:c0 + 512]),
                            reads=[("t2",), ("csin",)], writes=[("t2",)])
                        S.op("vector", lambda E, c0=c0: E.tensor_add(
                            out=qout[:, c0:c0 + 512], in0=t3[:], in1=t2[:]),
                            reads=[("t3",), ("t2",)], writes=[("qo", c0 // 512)])
                    else:
                        S.op("scalar", lambda E, cbase=cbase, c0=c0:
                             E.activation(
                                 out=qout[:, c0:c0 + 512], in_=ps_q[:],
                                 func=AF.Identity,
                                 bias=b_sb[:, cbase:cbase + 1], scale=1.0),
                             reads=[("psq",), ("cb",)],
                             writes=[("qo", c0 // 512), ("psqr",)])
                S.dma("sync", outdram[p], qout[:],
                      reads=[("qo", i) for i in range(8)],
                      writes=[(tag + "out", which, p)], slot="o0")
        if post_qk is not None:
            post_qk()
        for p in range(8):
            load_w(16 + p, 2, ("wv",))
            for tt in range(32):
                S.op("tensor", lambda E, tt=tt: [
                    E.matmul(ps_v[:], xln[d][:, tt * 128:(tt + 1) * 128],
                             wsl[2][:, d * 128:(d + 1) * 128],
                             start=(d == 0), stop=(d == DCH - 1))
                    for d in range(DCH)][-1],
                    reads=[("wv",)]
                    + [("xln", d, tt // 4) for d in range(DCH)] + [("psvr",)],
                    writes=[("psv",)])
                S.op("scalar", lambda E, p=p, tt=tt: E.activation(
                    out=vout[:, tt * 128:(tt + 1) * 128], in_=ps_v[:],
                    func=AF.Identity, bias=b_sb[:, 16 + p:17 + p], scale=1.0),
                    reads=[("psv",), ("cb",)],
                    writes=[("vo", tt), ("psvr",)])
            S.dma("sync", vtm[p], vout[:],
                  reads=[("vo", tt) for tt in range(32)],
                  writes=[(tag + "outv", p)], slot="o1")
        barrier(S)


def host_rope_tables():
    hd = 64
    inv = (1.0 / 10000.0 ** (np.arange(0, hd, 2, dtype=np.float32) / hd))
    pos = np.arange(T, dtype=np.float32)
    freqs = pos[:, None] * inv[None, :]
    emb = np.concatenate([freqs, freqs], axis=1)
    cos = np.cos(emb).T.astype(np.float32)
    sin = np.sin(emb).T.astype(np.float32)
    sgn = np.ones((64, 1), np.float32)
    sgn[0:32] = -1.0
    sins = sin * sgn
    return (np.concatenate([cos, cos], 0).copy(),
            np.concatenate([sins, sins], 0).copy())



def build_attn0core():
    nc = bass.Bass("TRN2", target_bir_lowering=False)
    io = {}
    io["qz"] = nc.dram_tensor("qz", [8, 128, T], BF16, kind="ExternalInput").ap()
    io["kz"] = nc.dram_tensor("kz", [8, 128, T], BF16, kind="ExternalInput").ap()
    io["vz"] = nc.dram_tensor("vz", [8, 128, T], BF16, kind="ExternalInput").ap()
    io["onesB"] = nc.dram_tensor("onesB", [1, 128], F32, kind="ExternalInput").ap()
    io["fconst"] = nc.dram_tensor("fconst", [128, len(CONST_VALS)], F32,
                                  kind="ExternalInput").ap()
    io["oz"] = nc.dram_tensor("oz", [8, 2, 64, T], BF16, kind="ExternalOutput").ap()
    with ExitStack() as ctx:
        S = Sched(nc, ctx)
        emit_attn0core(S, nc, ctx, io, tag="a0c")
        S.op("gpsimd", lambda E: E.nop(nofuse=True),
             reads=[("a0coz", b2, h) for b2 in range(8) for h in range(2)])
        S.emit()
    return nc


def emit_attn0core(S, nc, ctx, io, tag):
    qz, kz, vz = io["qz"], io["kz"], io["vz"]
    onesB, fconst, oz = io["onesB"], io["fconst"], io["oz"]
    if True:
        C = {}
        C["onesB"] = sbuf(ctx, nc, "onesB", [1, 128], F32)
        C["fc"] = sbuf(ctx, nc, "fc", [128, len(CONST_VALS)], F32)
        S.dma("sync", C["onesB"][:], onesB, writes=[("cB",)], slot="c")
        S.dma("sync", C["fc"][:], fconst, writes=[("cfc",)], slot="c")
        q_sb = [sbuf(ctx, nc, f"q{i}", [128, T], BF16) for i in range(2)]
        k_sb = [sbuf(ctx, nc, f"k{i}", [128, T], BF16) for i in range(2)]
        va_sb = [sbuf(ctx, nc, f"va{i}", [128, 65 * 32], BF16) for i in range(2)]
        vt_sb = [sbuf(ctx, nc, f"vt{i}", [128, T], BF16) for i in range(2)]
        ps_s = psum(ctx, nc, "ps_s", [128, 1024])
        ps_o = [psum(ctx, nc, f"ps_o{i}", [128, 512]) for i in range(2)]
        pt = sbuf(ctx, nc, "pt", [128, 1024], BF16)
        dinv = sbuf(ctx, nc, "dinv", [1, 512], F32)
        dbc = sbuf(ctx, nc, "dbc", [128, 512], F32)
        pbc = psum(ctx, nc, "pbc", [128, 512])
        ob = sbuf(ctx, nc, "ob", [64, T], BF16)

        for b2 in range(8):
            sl = b2 % 2
            S.dma("sync", q_sb[sl][:], qz[b2], writes=[("q", sl)], slot=f"q{sl}")
            S.dma("sync", k_sb[sl][:], kz[b2], writes=[("k", sl)], slot=f"k{sl}")
            S.dma("sync", vt_sb[sl][:], vz[b2],
                  writes=[("vt", sl)], slot=f"v{sl}")
            for h in range(2):
                hb = 64 * h
                S.op("vector", lambda E, sl=sl: E.memset(va_sb[sl][:], 1.0),
                     reads=[("var", sl)], writes=[("va", sl)])
                S.op("vector", lambda E, sl=sl, hb=hb: E.tensor_copy(
                    out=va_sb[sl][:].rearrange("p (tt c) -> p tt c", c=65)
                    [:, :, 0:64],
                    in_=vt_sb[sl][:].rearrange("p (tt c) -> p tt c", c=128)
                    [:, :, hb:hb + 64]),
                    reads=[("vt", sl)], writes=[("va", sl)])
                for qg in range(8):
                    t0 = qg * 512
                    for kc2 in range(16):
                        S.op("tensor", lambda E, sl=sl, hb=hb, kc2=kc2, t0=t0: [
                            E.matmul(ps_s[:, i * 512:(i + 1) * 512],
                                     k_sb[sl][hb:hb + 64,
                                              (kc2 * 2 + i) * 128:
                                              (kc2 * 2 + i + 1) * 128],
                                     q_sb[sl][hb:hb + 64, t0:t0 + 512],
                                     start=True, stop=True)
                            for i in range(2)][-1],
                            reads=[("k", sl), ("q", sl), ("pssr",)],
                            writes=[("pss",)])
                        S.op("scalar", lambda E: E.activation(
                            out=pt[:], in_=ps_s[:], func=AF.Exp,
                            bias=FC(C, "mtwenty"), scale=FC(C, "eighth")),
                            reads=[("pss",), ("cfc",)],
                            writes=[("pt",), ("pssr",)])
                        S.op("tensor", lambda E, sl=sl, kc2=kc2, qg=qg: [
                            E.matmul(ps_o[qg % 2][0:65, :],
                                     va_sb[sl][:, (kc2 * 2 + i) * 65:
                                               (kc2 * 2 + i + 1) * 65],
                                     pt[:, i * 512:(i + 1) * 512],
                                     start=(kc2 == 0 and i == 0),
                                     stop=(kc2 == 15 and i == 1))
                            for i in range(2)][-1],
                            reads=[("va", sl), ("pt",)],
                            writes=[("pso", qg % 2)])
                    S.op("scalar", lambda E, qg=qg: E.activation(
                        out=dinv[:], in_=ps_o[qg % 2][64:65, :], func=AF.Ln),
                        reads=[("pso", qg % 2)], writes=[("dinv",)])
                    S.op("scalar", lambda E: E.activation(
                        out=dinv[:], in_=dinv[:], func=AF.Exp, bias=0.0,
                        scale=FC(C, "mone", 1)),
                        reads=[("dinv",)], writes=[("dinv",)])
                    S.op("tensor", lambda E: E.matmul(
                        pbc[:], C["onesB"][:], dinv[:], start=True, stop=True),
                        reads=[("dinv",), ("cB",), ("dbcv",)],
                        writes=[("pbc",)])
                    S.op("vector", lambda E: E.tensor_copy(out=dbc[:],
                                                           in_=pbc[:]),
                         reads=[("pbc",)], writes=[("dbcv",)])
                    S.op("vector", lambda E, qg=qg, t0=t0: E.tensor_mul(
                        out=ob[:, t0:t0 + 512], in0=ps_o[qg % 2][0:64, :],
                        in1=dbc[0:64, :]),
                        reads=[("pso", qg % 2), ("dbcv",)],
                        writes=[("ob", qg)])
                S.dma("sync", oz[b2, h], ob[:],
                      reads=[("ob", qg) for qg in range(8)],
                      writes=[(tag + "oz", b2, h), ("var", sl)], slot="oz")
        barrier(S)



def build_attn0proj():
    nc = bass.Bass("TRN2", target_bir_lowering=False)
    io = {}
    io["sc"] = nc.dram_tensor("sc", [D, T], BF16, kind="ExternalInput").ap()
    io["x"] = nc.dram_tensor("x", [D, T], F32, kind="ExternalInput").ap()
    io["pw"] = nc.dram_tensor("pw", [D, D], F32, kind="ExternalInput").ap()
    io["pb"] = nc.dram_tensor("pb", [128, 8], F32, kind="ExternalInput").ap()
    io["y"] = nc.dram_tensor("y", [D, T], F32, kind="ExternalOutput").ap()
    with ExitStack() as ctx:
        S = Sched(nc, ctx)
        emit_attn0proj(S, nc, ctx, io, tag="a0p")
        S.op("gpsimd", lambda E: E.nop(nofuse=True),
             reads=[("a0py", ho, tg) for ho in range(8) for tg in range(8)])
        S.emit()
    return nc


def emit_attn0proj(S, nc, ctx, io, tag, sc_sb=None):
    sc, x, pw, pb, y = io["sc"], io["x"], io["pw"], io["pb"], io["y"]
    if True:
        pb_sb = sbuf(ctx, nc, "pb", [128, 8], F32)
        S.dma("sync", pb_sb[:], pb, writes=[("cpb",)], slot="c")
        w_sb = [sbuf(ctx, nc, f"w{d}", [128, 1024], BF16) for d in range(8)]
        for d in range(8):
            S.dma("gpsimd", w_sb[d][:], pw[d * 128:(d + 1) * 128, :],
                  writes=[("w", d)], slot=f"w{d % 2}")
        scb = [sbuf(ctx, nc, f"sc{d}", [128, 512], BF16) for d in range(8)]
        xsl = [sbuf(ctx, nc, f"xs{i}", [128, 512], F32) for i in range(2)]
        ys = [sbuf(ctx, nc, f"ys{i}", [128, 512], F32) for i in range(4)]
        ps = [psum(ctx, nc, f"ps{i}", [128, 512]) for i in range(2)]
        for tg in range(8):
            t0 = tg * 512
            if sc_sb is None:
                for d in range(8):
                    S.dma("sync", scb[d][:],
                          sc[d * 128:(d + 1) * 128, t0:t0 + 512],
                          writes=[("sc", d)], slot=f"sc{d % 4}")
            for ho in range(8):
                pp = ho % 2
                S.op("tensor", lambda E, ho=ho, pp=pp: [
                    E.matmul(ps[pp][:], w_sb[d][:, ho * 128:(ho + 1) * 128],
                             scb[d][:], start=(d == 0), stop=(d == 7))
                    for d in range(8)][-1],
                    reads=[("w", d) for d in range(8)]
                    + [("sc", d) for d in range(8)] + [("psr", pp)],
                    writes=[("ps", pp)])
                S.dma("sync", xsl[pp][:], x[ho * 128:(ho + 1) * 128,
                                            t0:t0 + 512],
                      reads=[("xslr", pp)],
                      writes=[("xsl", pp)], slot=f"x{pp}")
                S.op("scalar", lambda E, ho=ho, pp=pp: E.activation(
                    out=ys[ho % 4][:], in_=ps[pp][:], func=AF.Identity,
                    bias=pb_sb[:, ho:ho + 1], scale=1.0),
                    reads=[("ps", pp), ("cpb",), ("ysr", ho % 4)],
                    writes=[("ys", ho % 4), ("psr", pp)])
                S.op("vector", lambda E, ho=ho, pp=pp: E.tensor_add(
                    out=ys[ho % 4][:], in0=ys[ho % 4][:], in1=xsl[pp][:]),
                    reads=[("xsl", pp), ("ys", ho % 4)],
                    writes=[("ys", ho % 4), ("xslr", pp)])
                S.dma("gpsimd", y[ho * 128:(ho + 1) * 128, t0:t0 + 512],
                      ys[ho % 4][:],
                      reads=[("ys", ho % 4)],
                      writes=[(tag + "y", ho, tg), ("ysr", ho % 4)],
                      slot=f"y{ho % 4}")
        barrier(S)



def build_attn1():
    nc = bass.Bass("TRN2", target_bir_lowering=False)
    io = {}
    io["x"] = nc.dram_tensor("x", [D, T], F32, kind="ExternalInput").ap()
    io["qkvw"] = nc.dram_tensor("qkvw", [D, 3072], F32, kind="ExternalInput").ap()
    io["qkvb"] = nc.dram_tensor("qkvb", [128, 24], F32, kind="ExternalInput").ap()
    io["pw"] = nc.dram_tensor("pw", [D, D], F32, kind="ExternalInput").ap()
    io["pb"] = nc.dram_tensor("pb", [128, 8], F32, kind="ExternalInput").ap()
    io["onesA"] = nc.dram_tensor("onesA", [128, 1], F32, kind="ExternalInput").ap()
    io["onesB"] = nc.dram_tensor("onesB", [1, 128], F32, kind="ExternalInput").ap()
    io["fconst"] = nc.dram_tensor("fconst", [128, len(CONST_VALS)], F32,
                                  kind="ExternalInput").ap()
    io["iot"] = nc.dram_tensor("iot", [128, T], F32, kind="ExternalInput").ap()
    io["iop"] = nc.dram_tensor("iop", [128, 1], F32, kind="ExternalInput").ap()
    io["tri1"] = nc.dram_tensor("tri1", [128, 256], F32, kind="ExternalInput").ap()
    io["tri2"] = nc.dram_tensor("tri2", [128, 256], F32, kind="ExternalInput").ap()
    io["tri1c"] = nc.dram_tensor("tri1c", [128, 256], BF16,
                                 kind="ExternalInput").ap()
    io["tri2c"] = nc.dram_tensor("tri2c", [128, 256], BF16,
                                 kind="ExternalInput").ap()
    io["emask"] = nc.dram_tensor("emask", [128, 1], I32, kind="ExternalInput").ap()
    io["y"] = nc.dram_tensor("y", [D, T], F32, kind="ExternalOutput").ap()
    io["qs"] = nc.dram_tensor("qs", [8, 128, T], BF16, kind="Internal").ap()
    io["ks"] = nc.dram_tensor("ks", [8, 128, T], BF16, kind="Internal").ap()
    io["vas"] = nc.dram_tensor("vas", [8, 2, 128, 65 * 32], BF16,
                               kind="Internal").ap()
    with ExitStack() as ctx:
        S = Sched(nc, ctx)
        emit_attn1(S, nc, ctx, io, tag="a1")
        S.op("gpsimd", lambda E: E.nop(nofuse=True),
             reads=[("a1y", ho, g) for ho in range(8) for g in range(16)])
        S.emit()
    return nc


def emit_attn1(S, nc, ctx, io, tag):
    x, qkvw, qkvb, pw, pb = (io["x"], io["qkvw"], io["qkvb"], io["pw"],
                             io["pb"])
    onesA, onesB, fconst = io["onesA"], io["onesB"], io["fconst"]
    iot, iop, tri1, tri2 = io["iot"], io["iop"], io["tri1"], io["tri2"]
    tri1c, tri2c, emask, y = io["tri1c"], io["tri2c"], io["emask"], io["y"]
    qs, ks, vas = io["qs"], io["ks"], io["vas"]
    NQG = 16
    BUDGET = 818.0
    if True:
        C = {}
        C["onesA"] = sbuf(ctx, nc, "onesA", [128, 1], F32)
        C["onesB"] = sbuf(ctx, nc, "onesB", [1, 128], F32)
        C["fc"] = sbuf(ctx, nc, "fc", [128, len(CONST_VALS)], F32)
        b_sb = sbuf(ctx, nc, "b", [128, 24], F32)
        pb_sb = sbuf(ctx, nc, "pb", [128, 8], F32)
        iop_sb = sbuf(ctx, nc, "iop", [128, 1], F32)
        tri1_sb = sbuf(ctx, nc, "tri1", [128, 256], F32)
        tri2_sb = sbuf(ctx, nc, "tri2", [128, 256], F32)
        tri1c_sb = sbuf(ctx, nc, "tri1c", [128, 256], BF16)
        tri2c_sb = sbuf(ctx, nc, "tri2c", [128, 256], BF16)
        em_sb = sbuf(ctx, nc, "em", [128, 1], I32)
        iota_sb = sbuf(ctx, nc, "iota", [128, T], F32)
        for t_, src, key in ((C["onesA"], onesA, "cA"), (C["onesB"], onesB, "cB"),
                             (C["fc"], fconst, "cfc"), (b_sb, qkvb, "cb"),
                             (pb_sb, pb, "cpb"), (iop_sb, iop, "ciop"),
                             (tri1_sb, tri1, "ct1"), (tri2_sb, tri2, "ct2"),
                             (tri1c_sb, tri1c, "ct1c"), (tri2c_sb, tri2c, "ct2c"),
                             (em_sb, emask, "cem"), (iota_sb, iot, "cio")):
            S.dma("sync", t_[:], src, writes=[(key,)], slot="c")

        A1 = sbuf(ctx, nc, "A1", [128, 16384], F32)
        fft = sbuf(ctx, nc, "fft", [128, 32 * 256], F32)
        qout = fft[:, 0:2048].bitcast(BF16)
        vout = fft[:, 2048:4096].bitcast(BF16)
        xln = [A1[:, d * 2048:(d + 1) * 2048].bitcast(BF16) for d in range(DCH)]
        zb_ = A1[:, 0:4096]
        b1f = A1[:, 4096:8192]
        b2f = A1[:, 8192:12288]
        jnk = A1[:, 8192:12288].bitcast(BF16)[:, 0:4096]
        causb = jnk
        pwsb = [A1[:, 12288 + d * 512:12288 + (d + 1) * 512].bitcast(BF16)
                for d in range(8)]
        fftile = sbuf(ctx, nc, "fftile", [128, T], F32)
        m_row = fftile[0:1, :]
        r_row = fft[0:1, 0:4096]
        kp = [sbuf(ctx, nc, f"kp{i}", [128, T], BF16) for i in range(2)]
        mb = kp[0][:].bitcast(F32)[:, 0:512]
        rbc = kp[0][:].bitcast(F32)[:, 512:1024]

        xs = [sbuf(ctx, nc, f"xs{i}", [128, 512], F32) for i in range(2)]
        sq = [sbuf(ctx, nc, f"sq{i}", [128, 512], F32) for i in range(2)]
        ps_sx = [psum(ctx, nc, "psx", [1, 512])]
        ps_sq = [psum(ctx, nc, "psq", [1, 512])]
        pbc = psum(ctx, nc, "pbc", [128, 512])
        tmp = [sbuf(ctx, nc, f"lt{i}", [128, 512], F32) for i in range(2)]

        def get_x_tile(d, fg, k):
            xb = xs[k % 2]
            S.dma("sync", xb[:], x[d * 128:(d + 1) * 128,
                                   fg * 512:(fg + 1) * 512],
                  writes=[("xs", k % 2)], slot=f"xs{k % 2}")
            return xb[:], [("xs", k % 2)]
        emit_ln_stats_generic(S, nc, C, "L1", m_row, r_row, T, xs, sq,
                              ps_sx, ps_sq, get_x_tile)

        def bcast(row_ap, dst, rkeys, dkey):
            nn = row_ap.shape[-1]
            S.op("tensor", lambda E, nn=nn, row_ap=row_ap: E.matmul(
                pbc[:, 0:nn], C["onesB"][:], row_ap, start=True, stop=True),
                reads=list(rkeys) + [("cB",), (dkey,)], writes=[("pbcq",)])
            S.op("vector", lambda E, nn=nn, dst=dst: E.tensor_copy(
                out=dst[:, 0:nn], in_=pbc[:, 0:nn]),
                reads=[("pbcq",)], writes=[(dkey,)])
        for fg in range(T // 512):
            c0 = fg * 512
            bcast(m_row[:, c0:c0 + 512], mb, [("L1mrow", fg)], "mbv")
            bcast(r_row[:, c0:c0 + 512], rbc, [("L1rrow",)], "rbcv")
            for d in range(DCH):
                S.dma("sync", xs[d % 2][:], x[d * 128:(d + 1) * 128, c0:c0 + 512],
                      writes=[("xs", d % 2)], slot=f"xs{d % 2}")
                S.op("vector", lambda E, d=d: E.tensor_sub(
                    out=tmp[d % 2][:], in0=xs[d % 2][:], in1=mb[:]),
                    reads=[("xs", d % 2), ("mbv",)], writes=[("lt", d % 2)])
                S.op("vector", lambda E, d=d, c0=c0: E.tensor_mul(
                    out=xln[d][:, c0:c0 + 512], in0=tmp[d % 2][:], in1=rbc[:]),
                    reads=[("lt", d % 2), ("rbcv",)],
                    writes=[("xln", d, fg)])

        barrier(S)

        wsl = [sbuf(ctx, nc, f"wsl{i}", [128, 1024], BF16) for i in range(3)]
        ps_q = psum(ctx, nc, "ps_q", [128, 512])
        ps_v = psum(ctx, nc, "ps_v", [128, 128])
        va_sb = sbuf(ctx, nc, "va", [128, 65 * 32], BF16)

        def load_w(c, sl, key):
            S.dma("gpsimd", wsl[sl][:].rearrange("p (d c) -> p d c", c=128),
                  qkvw[:, c * 128:(c + 1) * 128]
                  .rearrange("(d p) c -> p d c", p=128),
                  writes=[key], slot=f"w{sl}")
        for p in range(8):
            for which, cbase, outdram in ((0, p, qs), (1, 8 + p, ks)):
                load_w(cbase, which, ("wq", which))
                for tg in range(8):
                    c0 = tg * 512
                    S.op("tensor", lambda E, which=which, c0=c0: [
                        E.matmul(ps_q[:], wsl[which][:, d * 128:(d + 1) * 128],
                                 xln[d][:, c0:c0 + 512],
                                 start=(d == 0), stop=(d == DCH - 1))
                        for d in range(DCH)][-1],
                        reads=[("wq", which)]
                        + [("xln", d, c0 // 512) for d in range(DCH)]
                        + [("psqr",)],
                        writes=[("psq",)])
                    S.op("scalar", lambda E, cbase=cbase, c0=c0: E.activation(
                        out=qout[:, c0:c0 + 512], in_=ps_q[:], func=AF.Identity,
                        bias=b_sb[:, cbase:cbase + 1], scale=1.0),
                        reads=[("psq",), ("cb",)],
                        writes=[("qo", c0 // 512), ("psqr",)])
                S.dma("sync", outdram[p], qout[:],
                      reads=[("qo", i) for i in range(8)],
                      writes=[("hbq", which, p)], slot="o0")
            load_w(16 + p, 2, ("wv",))
            for tt in range(32):
                S.op("tensor", lambda E, tt=tt: [
                    E.matmul(ps_v[:], xln[d][:, tt * 128:(tt + 1) * 128],
                             wsl[2][:, d * 128:(d + 1) * 128],
                             start=(d == 0), stop=(d == DCH - 1))
                    for d in range(DCH)][-1],
                    reads=[("wv",)]
                    + [("xln", d, tt // 4) for d in range(DCH)]
                    + [("psvr",)],
                    writes=[("psv",)])
                S.op("scalar", lambda E, p=p, tt=tt: E.activation(
                    out=vout[:, tt * 128:(tt + 1) * 128], in_=ps_v[:],
                    func=AF.Identity, bias=b_sb[:, 16 + p:17 + p], scale=1.0),
                    reads=[("psv",), ("cb",)],
                    writes=[("vo", tt), ("psvr",)])
            for hh in range(2):
                S.op("vector", lambda E: E.memset(va_sb[:], 1.0),
                     reads=[("var",)], writes=[("va",)])
                S.op("vector", lambda E, hh=hh: E.tensor_copy(
                    out=va_sb[:].rearrange("p (tt c) -> p tt c", c=65)
                    [:, :, 0:64],
                    in_=vout[:].rearrange("p (tt c) -> p tt c", c=128)
                    [:, :, 64 * hh:64 * hh + 64]),
                    reads=[("vo", tt) for tt in range(32)],
                    writes=[("va",)]),
                S.dma("sync", vas[p, hh], va_sb[:],
                      reads=[("va",)], writes=[("hbva", p, hh), ("var",)],
                      slot="o1")
        barrier(S)

        qpg = [sbuf(ctx, nc, f"qpg{i}", [128, 256], BF16) for i in range(2)]
        vap = [va_sb, sbuf(ctx, nc, "vap1", [128, 65 * 32], BF16)]
        ffe = sbuf(ctx, nc, "ffe", [128, 32 * 256], BF16)
        kp_sc = sbuf(ctx, nc, "kpsc", [128, 2048], BF16)
        qp_sc = sbuf(ctx, nc, "qpsc", [128, 256], BF16)
        s0b = sbuf(ctx, nc, "s0b", [128, 256], F32)
        carr = sbuf(ctx, nc, "carr", [128, 33], F32)
        og = [wsl[j][:, k * 256:(k + 1) * 256]
              for j in range(2) for k in range(4)]
        ps_o = [psum(ctx, nc, f"ps_o{i}", [128, 256]) for i in range(2)]
        ps_at1 = psum(ctx, nc, "ps_at1", [128, 512])
        ps_at = [ps_q[:], ps_at1[:]]
        ps_t = ps_v
        ptph = sbuf(ctx, nc, "ptph", [128, 2048], BF16)
        pt = [ptph[:, i * 512:(i + 1) * 512] for i in range(2)]
        phat = [ptph[:, 1024 + i * 512:1024 + (i + 1) * 512]
                for i in range(2)]
        dinv = sbuf(ctx, nc, "dinv", [1, 256], F32)
        dbc = sbuf(ctx, nc, "dbc", [128, 256], F32)
        xsl = sbuf(ctx, nc, "xsl", [128, 256], F32)
        ysl = [sbuf(ctx, nc, f"ysl{i}", [128, 256], F32) for i in range(2)]
        id_sb = sbuf(ctx, nc, "idm", [128, 128], F32)
        S.op("vector", lambda E: E.tensor_scalar(
            out=id_sb[:], in0=iota_sb[:, 0:128], scalar1=iop_sb[:],
            scalar2=None, op0=ALU.is_equal),
            reads=[("cio",), ("ciop",)], writes=[("cid",)])
        qcol = sbuf(ctx, nc, "qcol", [128, 1], F32)
        bcol = sbuf(ctx, nc, "bcol", [128, 1], F32)
        bm1 = sbuf(ctx, nc, "bm1", [128, 1], F32)
        rmax = sbuf(ctx, nc, "rmax", [128, 1], F32)
        cc = sbuf(ctx, nc, "cc", [128, 1], F32)
        c4 = sbuf(ctx, nc, "c4", [128, 1], F32)
        lo = sbuf(ctx, nc, "lo", [128, 1], F32)
        hi = sbuf(ctx, nc, "hi", [128, 1], F32)
        mid = sbuf(ctx, nc, "mid", [128, 1], F32)
        cnt = sbuf(ctx, nc, "cnt", [128, 1], F32)
        prd = sbuf(ctx, nc, "prd", [128, 1], mybir.dt.uint32)
        llo = sbuf(ctx, nc, "llo", [128, 1], F32)
        rrr = sbuf(ctx, nc, "rrr", [128, 1], F32)

        for d in range(8):
            S.dma("gpsimd", pwsb[d][:], pw[d * 128:(d + 1) * 128, :],
                  writes=[("pw", d)], slot="w0")
        S.op("vector", lambda E: E.memset(carr[:], 0.0), writes=[("carr",)])
        S.dma("sync", kp_sc[0:64, :], ks[0][0:64, 0:2048],
              writes=[("kpsc",)], slot="ksc")
        S.dma("sync", kp_sc[64:128, :], ks[0][0:64, 2048:4096],
              writes=[("kpsc",)], slot="ksc")

        def stage_X(g):
            t0 = g * 256
            nkc = 2 * g + 2
            buf = g % 2
            S.dma("sync", qp_sc[0:64, :], qs[0][0:64, t0:t0 + 256],
                  writes=[("qpsc",)], slot="qsc")
            S.dma("sync", qp_sc[64:128, :], qs[0][0:64, t0:t0 + 256],
                  writes=[("qpsc",)], slot="qsc")
            for kc in range(nkc):
                r0 = 64 * (kc // 16)
                c0s = (kc % 16) * 128
                for ih in range(2):
                    S.op("tensor", lambda E, r0=r0, c0s=c0s, ih=ih: E.matmul(
                        ps_t[:], kp_sc[r0:r0 + 64, c0s:c0s + 128],
                        qp_sc[r0:r0 + 64, ih * 128:(ih + 1) * 128],
                        start=True, stop=True),
                        reads=[("kpsc",), ("qpsc",), ("pst",)],
                        writes=[("pst",)])
                    S.op("vector", lambda E, ih=ih: E.tensor_scalar(
                        out=s0b[:, ih * 128:(ih + 1) * 128], in0=ps_t[:],
                        scalar1=0.125, scalar2=0.0,
                        op0=ALU.mult, op1=ALU.max),
                        reads=[("pst",), ("s0",)],
                        writes=[("s0",)])
                if kc == 2 * g:
                    S.op("vector", lambda E: E.tensor_mul(
                        out=s0b[:], in0=s0b[:], in1=tri1_sb[:]),
                        reads=[("ct1",), ("s0",)], writes=[("s0",)])
                if kc == 2 * g + 1:
                    S.op("vector", lambda E: E.tensor_mul(
                        out=s0b[:], in0=s0b[:], in1=tri2_sb[:]),
                        reads=[("ct2",), ("s0",)], writes=[("s0",)])
                if kc == 0:
                    S.op("vector", lambda E: E.memset(s0b[0:1, :], 0.0),
                         reads=[("s0",)], writes=[("s0",)])
                S.op("vector", lambda E, kc=kc: E.tensor_tensor_scan(
                    out=fft[:, kc * 256:(kc + 1) * 256], data0=s0b[:],
                    data1=s0b[:], initial=carr[:, kc:kc + 1],
                    op0=ALU.add, op1=ALU.bypass),
                    reads=[("s0",), ("carr",), ("fft", kc)],
                    writes=[("fft", kc)])
                S.op("vector", lambda E, kc=kc: E.tensor_copy(
                    out=carr[:, kc:kc + 1],
                    in_=fft[:, kc * 256 + 255:kc * 256 + 256]),
                    reads=[("fft", kc)], writes=[("carr",)])
                S.op("vector", lambda E, kc=kc: E.tensor_sub(
                    out=fft[:, kc * 256:(kc + 1) * 256],
                    in0=fft[:, kc * 256:(kc + 1) * 256], in1=s0b[:]),
                    reads=[("fft", kc), ("s0",)],
                    writes=[("fft", kc)])
                yield
            for sub in range(2):
                tl = 2 * g + sub
                if tl < 6:
                    continue
                nkt = tl + 1
                ncol = nkt * 128
                for kc in range(nkt):
                    S.op("tensor", lambda E, kc=kc, sub=sub: E.transpose(
                        ps_t[:], fft[:, kc * 256 + sub * 128:
                                     kc * 256 + sub * 128 + 128], id_sb[:]),
                        reads=[("fft", kc), ("cid",), ("pst",)],
                        writes=[("pst",)])
                    S.op("vector", lambda E, kc=kc: E.tensor_copy(
                        out=fftile[:, kc * 128:(kc + 1) * 128], in_=ps_t[:]),
                        reads=[("pst",)], writes=[("ft", kc)])
                    if kc % 8 == 7:
                        yield
                S.op("vector", lambda E, tl=tl: E.tensor_scalar(
                    out=qcol[:], in0=iop_sb[:], scalar1=float(tl * 128),
                    scalar2=None, op0=ALU.add),
                    reads=[("ciop",)], writes=[("qcol",)])
                S.op("vector", lambda E: E.tensor_scalar(
                    out=bcol[:], in0=qcol[:], scalar1=1.0, scalar2=BUDGET,
                    op0=ALU.add, op1=ALU.min),
                    reads=[("qcol",)], writes=[("bcol",)])
                S.op("vector", lambda E: E.tensor_scalar(
                    out=bm1[:], in0=bcol[:], scalar1=-1.0, scalar2=None,
                    op0=ALU.add),
                    reads=[("bcol",)], writes=[("bm1",)])
                S.op("vector", lambda E, ncol=ncol: E.tensor_scalar(
                    out=causb[:, 0:ncol], in0=iota_sb[:, 0:ncol],
                    scalar1=qcol[:], scalar2=None, op0=ALU.is_le),
                    reads=[("cio",), ("qcol",), ("jnkv",), ("b2",)],
                    writes=[("caus",)])
                S.op("vector", lambda E, ncol=ncol: E.tensor_reduce(
                    out=rmax[:], in_=fftile[:, 0:ncol], axis=AX.X,
                    op=ALU.max),
                    reads=[("ft", kc) for kc in range(nkt)],
                    writes=[("rmax",)])
                S.op("vector", lambda E: E.tensor_tensor(
                    out=cc.bitcast(I32)[:], in0=rmax.bitcast(I32)[:],
                    in1=em_sb[:], op=ALU.bitwise_and),
                    reads=[("rmax",), ("cem",)], writes=[("cc",)])
                S.op("vector", lambda E: E.tensor_scalar(
                    out=cc[:], in0=cc[:], scalar1=2.0, scalar2=1.0,
                    op0=ALU.mult, op1=ALU.max),
                    reads=[("cc",)], writes=[("cc",)])
                S.op("vector", lambda E: E.tensor_scalar(
                    out=c4[:], in0=cc[:], scalar1=4.0, scalar2=None,
                    op0=ALU.mult),
                    reads=[("cc",)], writes=[("c4",)])
                S.op("vector", lambda E, ncol=ncol: E.tensor_scalar(
                    out=zb_[:, 0:ncol], in0=fftile[:, 0:ncol], scalar1=cc[:],
                    scalar2=None, op0=ALU.add),
                    reads=[("ft", kc) for kc in range(nkt)] + [("cc",)],
                    writes=[("z",)])
                S.op("vector", lambda E, ncol=ncol: E.tensor_mul(
                    out=zb_[:, 0:ncol], in0=zb_[:, 0:ncol],
                    in1=causb[:, 0:ncol]),
                    reads=[("z",), ("caus",)], writes=[("z",)])
                S.op("vector", lambda E, ncol=ncol: E.tensor_scalar(
                    out=jnk[:, 0:ncol], in0=causb[:, 0:ncol], scalar1=c4[:],
                    scalar2=None, op0=ALU.mult),
                    reads=[("caus",), ("c4",)], writes=[("jnkv",)])
                S.op("vector", lambda E, ncol=ncol: E.tensor_scalar(
                    out=jnk[:, 0:ncol], in0=jnk[:, 0:ncol], scalar1=-1.0,
                    scalar2=c4[:], op0=ALU.mult, op1=ALU.add),
                    reads=[("jnkv",), ("c4",)], writes=[("jnkv",)])
                S.op("vector", lambda E, ncol=ncol: E.tensor_add(
                    out=zb_[:, 0:ncol], in0=zb_[:, 0:ncol],
                    in1=jnk[:, 0:ncol]),
                    reads=[("z",), ("jnkv",)], writes=[("z",)])
                S.op("vector", lambda E: E.tensor_copy(out=lo[:], in_=cc[:]),
                     reads=[("cc",)], writes=[("lo",)])
                S.op("vector", lambda E: E.tensor_scalar(
                    out=hi[:], in0=cc[:], scalar1=2.0, scalar2=None,
                    op0=ALU.mult),
                    reads=[("cc",)], writes=[("hi",)])
                yield
                for it in range(26):
                    S.op("vector", lambda E: E.tensor_add(
                        out=mid[:], in0=lo[:], in1=hi[:]),
                        reads=[("lo",), ("hi",)], writes=[("mid",)])
                    S.op("vector", lambda E: E.tensor_scalar(
                        out=mid[:], in0=mid[:], scalar1=0.5, scalar2=None,
                        op0=ALU.mult),
                        reads=[("mid",)], writes=[("mid",)])
                    S.op("vector", lambda E, ncol=ncol: E.tensor_scalar(
                        out=jnk[:, 0:ncol], in0=zb_[:, 0:ncol],
                        scalar1=mid[:], scalar2=0.0,
                        op0=ALU.is_lt, op1=ALU.add, accum_out=cnt[:]),
                        reads=[("z",), ("mid",), ("caus",)],
                        writes=[("jnkv",), ("cnt",)])
                    S.op("vector", lambda E: E.tensor_tensor(
                        out=prd[:], in0=cnt[:], in1=bm1[:], op=ALU.is_le),
                        reads=[("cnt",), ("bm1",)], writes=[("prd",)])
                    S.op("vector", lambda E: E.select(
                        out=lo[:], mask=prd[:], on_true=mid[:], on_false=lo[:],
                        add_drain=True),
                        reads=[("prd",), ("mid",), ("lo",)], writes=[("lo",)])
                    S.op("vector", lambda E: E.select(
                        out=hi[:], mask=prd[:], on_true=hi[:], on_false=mid[:],
                        add_drain=True),
                        reads=[("prd",), ("mid",), ("hi",)], writes=[("hi",)])
                    if it % 3 == 2:
                        yield
                S.op("vector", lambda E, ncol=ncol: E.tensor_scalar(
                    out=jnk[:, 0:ncol], in0=zb_[:, 0:ncol], scalar1=lo[:],
                    scalar2=0.0, op0=ALU.is_lt, op1=ALU.add,
                    accum_out=llo[:]),
                    reads=[("z",), ("lo",)], writes=[("jnkv",), ("llo",)])
                S.op("vector", lambda E: E.tensor_sub(
                    out=rrr[:], in0=bcol[:], in1=llo[:]),
                    reads=[("bcol",), ("llo",)], writes=[("rrr",)])
                S.op("vector", lambda E, ncol=ncol: E.tensor_scalar(
                    out=b1f[:, 0:ncol], in0=zb_[:, 0:ncol], scalar1=lo[:],
                    scalar2=None, op0=ALU.is_equal),
                    reads=[("z",), ("lo",), ("b1",)],
                    writes=[("eqt",), ("b1",)])
                S.op("vector", lambda E: E.memset(b2f[:, 0:1], 0.0),
                     reads=[("jnkv",), ("caus",)], writes=[("tie",), ("b2",)])
                S.op("vector", lambda E, ncol=ncol: E.tensor_tensor_scan(
                    out=b2f[:, 1:ncol], data0=b1f[:, 0:ncol - 1],
                    data1=b1f[:, 0:ncol - 1], initial=0.0,
                    op0=ALU.add, op1=ALU.bypass),
                    reads=[("eqt",)], writes=[("tie",), ("b2",)])
                yield
                S.op("vector", lambda E, ncol=ncol: E.tensor_scalar(
                    out=b2f[:, 0:ncol], in0=b2f[:, 0:ncol], scalar1=rrr[:],
                    scalar2=None, op0=ALU.is_lt),
                    reads=[("tie",), ("rrr",)], writes=[("tie",), ("b2",)])
                S.op("vector", lambda E, ncol=ncol: E.tensor_mul(
                    out=b1f[:, 0:ncol], in0=b1f[:, 0:ncol],
                    in1=b2f[:, 0:ncol]),
                    reads=[("eqt",), ("tie",), ("b2",)], writes=[("eqt",)])
                S.op("vector", lambda E, ncol=ncol: E.tensor_scalar(
                    out=b2f[:, 0:ncol], in0=zb_[:, 0:ncol], scalar1=lo[:],
                    scalar2=None, op0=ALU.is_lt),
                    reads=[("z",), ("lo",), ("tie",)], writes=[("b2",)])
                S.op("vector", lambda E, ncol=ncol: E.tensor_tensor(
                    out=b1f[:, 0:ncol], in0=b1f[:, 0:ncol],
                    in1=b2f[:, 0:ncol], op=ALU.max),
                    reads=[("eqt",), ("b2",)], writes=[("eqt",)])
                S.op("vector", lambda E, ncol=ncol: E.tensor_scalar(
                    out=b2f[:, 0:ncol], in0=iota_sb[:, 0:ncol],
                    scalar1=qcol[:], scalar2=None, op0=ALU.is_equal),
                    reads=[("cio",), ("qcol",), ("b2",)], writes=[("b2",)])
                S.op("vector", lambda E, ncol=ncol: E.tensor_tensor(
                    out=b1f[:, 0:ncol], in0=b1f[:, 0:ncol],
                    in1=b2f[:, 0:ncol], op=ALU.max),
                    reads=[("eqt",), ("b2",)], writes=[("eqt",)])
                S.op("vector", lambda E, ncol=ncol: E.tensor_scalar(
                    out=b1f[:, 0:ncol], in0=b1f[:, 0:ncol], scalar1=-1e5,
                    scalar2=1e5, op0=ALU.mult, op1=ALU.add),
                    reads=[("eqt",)], writes=[("eqt",)])
                for kc in range(nkt):
                    S.op("tensor", lambda E, kc=kc: E.transpose(
                        ps_t[:], b1f[:, kc * 128:(kc + 1) * 128], id_sb[:]),
                        reads=[("eqt",), ("cid",), ("pst",)],
                        writes=[("pst",)])
                    S.op("vector", lambda E, kc=kc, sub=sub: E.tensor_add(
                        out=fft[:, kc * 256 + sub * 128:
                                kc * 256 + sub * 128 + 128],
                        in0=fft[:, kc * 256 + sub * 128:
                                kc * 256 + sub * 128 + 128],
                        in1=ps_t[:]),
                        reads=[("pst",), ("fft", kc)],
                        writes=[("fft", kc)])
                    if kc % 8 == 7:
                        yield
                if sub == 0:
                    kl = 2 * g + 1
                    S.op("vector", lambda E, kl=kl: E.tensor_scalar(
                        out=fft[:, kl * 256:kl * 256 + 128],
                        in0=fft[:, kl * 256:kl * 256 + 128],
                        scalar1=1e5, scalar2=None, op0=ALU.add),
                        reads=[("fft", kl)], writes=[("fft", kl)])
                yield

        def emit_ffe(g):
            nkc = 2 * g + 2
            for kc in range(nkc):
                S.op("scalar", lambda E, kc=kc: E.activation(
                    out=ffe[:, kc * 256:(kc + 1) * 256],
                    in_=fft[:, kc * 256:(kc + 1) * 256], func=AF.Exp,
                    bias=0.0, scale=FC(C, "mone")),
                    reads=[("fft", kc), ("cfc",)], writes=[("ffe", kc)])

        def stage_Y(g):
            t0 = g * 256
            nkc = 2 * g + 2
            buf = g % 2
            for h in range(16):
                p, hh = h // 2, h % 2
                sl = p % 2
                if hh == 0:
                    S.dma("sync", kp[sl][:], ks[p],
                          reads=([("mbv",), ("rbcv",)]
                                 if (g == 0 and p == 0) else []),
                          writes=[("kp", sl)], slot=f"kq{sl}")
                    S.dma("sync", qpg[sl][:], qs[p][:, t0:t0 + 256],
                          writes=[("qpg", sl)], slot=f"kq{sl}")
                S.dma("sync", vap[sl][:], vas[p, hh], writes=[("vap", sl)],
                      slot=f"va{sl}")
                hb = 64 * hh

                npair = nkc // 2

                def av(j):
                    b = j % 2
                    S.op("tensor", lambda E, sl=sl, j=j, b=b, nkc=nkc, h=h: [
                        E.matmul(ps_o[h % 2][0:65, :],
                                 vap[sl][:, (2 * j + i) * 65:
                                         (2 * j + i + 1) * 65],
                                 phat[b][:, i * 256:(i + 1) * 256],
                                 start=(2 * j + i == 0),
                                 stop=(2 * j + i == nkc - 1))
                        for i in range(2)][-1],
                        reads=[("vap", sl), ("phat", b), ("pso", h % 2)],
                        writes=[("pso", h % 2)])
                for j in range(npair):
                    b = j % 2
                    S.op("tensor", lambda E, sl=sl, hb=hb, j=j, b=b: [
                        E.matmul(ps_at[b][:, i * 256:(i + 1) * 256],
                                 kp[sl][hb:hb + 64,
                                        (2 * j + i) * 128:
                                        (2 * j + i + 1) * 128],
                                 qpg[sl][hb:hb + 64, :],
                                 start=True, stop=True)
                        for i in range(2)][-1],
                        reads=[("kp", sl), ("qpg", sl), ("psat", b)],
                        writes=[("psat", b)])
                    if j > 0:
                        av(j - 1)
                    S.op("scalar", lambda E, b=b: E.activation(
                        out=pt[b][:], in_=ps_at[b][:], func=AF.Exp,
                        bias=FC(C, "mtwenty"), scale=FC(C, "eighth")),
                        reads=[("psat", b), ("cfc",)],
                        writes=[("pt", b)])
                    S.op("vector", lambda E, b=b, j=j: E.tensor_mul(
                        out=phat[b][:], in0=pt[b][:],
                        in1=ffe[:, j * 512:(j + 1) * 512]),
                        reads=[("pt", b), ("ffe", 2 * j), ("ffe", 2 * j + 1),
                               ("phat", b)],
                        writes=[("phat", b)])
                    if g < 3:
                        for i in range(2):
                            kc = 2 * j + i
                            if kc >= 2 * g:
                                tri = (tri1c_sb if kc == 2 * g else tri2c_sb)
                                S.op("vector", lambda E, tri=tri, b=b, i=i:
                                     E.tensor_mul(
                                         out=phat[b][:, i * 256:
                                                     (i + 1) * 256],
                                         in0=phat[b][:, i * 256:
                                                     (i + 1) * 256],
                                         in1=tri[:]),
                                     reads=[("ct1c",), ("ct2c",),
                                            ("phat", b)],
                                     writes=[("phat", b)])
                av(npair - 1)
                S.op("scalar", lambda E, h=h: E.activation(
                    out=dinv[:], in_=ps_o[h % 2][64:65, :], func=AF.Ln),
                    reads=[("pso", h % 2), ("dinv",)], writes=[("dinv",)])
                S.op("scalar", lambda E: E.activation(
                    out=dinv[:], in_=dinv[:], func=AF.Exp, bias=0.0,
                    scale=FC(C, "mone", 1)),
                    reads=[("dinv",)], writes=[("dinv",)])
                S.op("tensor", lambda E: E.matmul(
                    pbc[:, 0:256], C["onesB"][:], dinv[:], start=True,
                    stop=True),
                    reads=[("dinv",), ("cB",), ("pbcq",)], writes=[("pbcq",)])
                S.op("vector", lambda E: E.tensor_copy(
                    out=dbc[:], in_=pbc[:, 0:256]),
                    reads=[("pbcq",), ("dbcv",)], writes=[("dbcv",)])
                S.op("vector", lambda E, h=h, p=p, hb=hb: E.tensor_mul(
                    out=og[p][hb:hb + 64, :], in0=ps_o[h % 2][0:64, :],
                    in1=dbc[0:64, :]),
                    reads=[("pso", h % 2), ("dbcv",)],
                    writes=[("og", p, hh)])
                yield
            for ho in range(8):
                pz = ho % 2
                S.op("tensor", lambda E, ho=ho, pz=pz: [
                    E.matmul(ps_o[pz][:], pwsb[d][:, ho * 128:(ho + 1) * 128],
                             og[d][:], start=(d == 0), stop=(d == 7))
                    for d in range(8)][-1],
                    reads=[("pw", d) for d in range(8)]
                    + [("og", d, hh2) for d in range(8) for hh2 in range(2)]
                    + [("pso", pz)],
                    writes=[("pso", pz)])
                S.dma("sync", xsl[:], x[ho * 128:(ho + 1) * 128, t0:t0 + 256],
                      reads=[("xslr",)], writes=[("xsl",)], slot="xr")
                S.op("vector", lambda E, ho=ho, pz=pz: E.tensor_scalar(
                    out=ysl[pz][:], in0=ps_o[pz][:],
                    scalar1=pb_sb[:, ho:ho + 1], scalar2=None, op0=ALU.add),
                    reads=[("pso", pz), ("cpb",), ("ysl", pz)],
                    writes=[("ysl", pz)])
                S.op("vector", lambda E, pz=pz: E.tensor_add(
                    out=ysl[pz][:], in0=ysl[pz][:], in1=xsl[:]),
                    reads=[("xsl",), ("ysl", pz)],
                    writes=[("ysl", pz), ("xslr",)])
                S.dma("gpsimd", y[ho * 128:(ho + 1) * 128, t0:t0 + 256],
                      ysl[pz][:],
                      reads=[("ysl", pz)],
                      writes=[(tag + "y", ho, g), ("yslr", pz)],
                      slot=f"y{pz}")
                if ho % 4 == 3:
                    yield

        _SENT = object()
        ys = None
        for g in range(NQG):
            xit = stage_X(g)
            if ys is None:
                for _ in xit:
                    pass
            else:
                xd = yd = False
                while not (xd and yd):
                    for _ in range(3):
                        if not xd and next(xit, _SENT) is _SENT:
                            xd = True
                    if not yd and next(ys, _SENT) is _SENT:
                        yd = True
            emit_ffe(g)
            ys = stage_Y(g)
        for _ in ys:
            pass
        barrier(S)


def host_attn1_consts():
    iota = np.tile(np.arange(T, dtype=np.float32)[None, :], (128, 1))
    iop = np.arange(128, dtype=np.float32).reshape(128, 1)
    q = np.arange(256)[None, :]
    k1 = np.arange(128)[:, None]
    tri1 = (k1 < q).astype(np.float32)
    tri2 = ((k1 + 128) < q).astype(np.float32)
    tri1c = (k1 <= q).astype(np.float32)
    tri2c = ((k1 + 128) <= q).astype(np.float32)
    em = np.full((128, 1), 0x7F800000, dtype=np.int32)
    return (iota, iop, np.ascontiguousarray(tri1), np.ascontiguousarray(tri2),
            np.ascontiguousarray(tri1c), np.ascontiguousarray(tri2c), em)



def _run(nc, maps):
    return run_bass_kernel_spmd(nc, maps, core_ids=list(range(NCORES))).results


def kernel(**inputs):
    if os.environ.get("MK_UNFUSED") == "1":
        return kernel_unfused(**inputs)
    nc = build_fused()
    maps = kernel_fused_inputs(**inputs)
    r = _run(nc, maps)
    return np.stack([np.asarray(r[b]["y"]).T
                     for b in range(8)]).astype(np.float32)


def kernel_unfused(L0_attn_x, L0_attn_qkv_w, L0_attn_qkv_b, L0_attn_proj_w,
           L0_attn_proj_b, L0_ffn_Wg_s, L0_ffn_Wu_s, L0_ffn_Wd_s, L0_ffn_Wg,
           L0_ffn_Wu, L0_ffn_Wd, L0_ffn_Wr, L0_ffn_rb, L1_attn_W_attn,
           L1_attn_b_attn, L1_attn_W_proj, L1_attn_b_proj, L1_ffn_ln_gamma,
           L1_ffn_ln_beta, L1_ffn_fc1_q, L1_ffn_fc1_scales, L1_ffn_fc1_bias,
           L1_ffn_fc2_q, L1_ffn_fc2_scales, L1_ffn_fc2_bias):
    f32 = np.float32
    onesA = np.ones((128, 1), f32)
    onesB = np.ones((1, 128), f32)
    ident = np.eye(128, dtype=f32)
    fc = host_fconst()

    x0 = np.asarray(L0_attn_x, f32)
    xT = [np.ascontiguousarray(x0[b].T) for b in range(8)]

    wg9 = np.ascontiguousarray(np.concatenate(
        [np.asarray(L0_ffn_Wg_s, f32).T[None],
         np.transpose(np.asarray(L0_ffn_Wg, f32), (0, 2, 1))], axis=0))
    wu9 = np.ascontiguousarray(np.concatenate(
        [np.asarray(L0_ffn_Wu_s, f32).T[None],
         np.transpose(np.asarray(L0_ffn_Wu, f32), (0, 2, 1))], axis=0))
    wd9 = np.ascontiguousarray(np.concatenate(
        [np.asarray(L0_ffn_Wd_s, f32).T[None],
         np.transpose(np.asarray(L0_ffn_Wd, f32), (0, 2, 1))], axis=0))
    wrT = np.ascontiguousarray(np.asarray(L0_ffn_Wr, f32).T)
    rb8 = np.ascontiguousarray(np.asarray(L0_ffn_rb, f32).reshape(8, 1))

    nc_ffn0 = build_ffn_l0()
    ffn0_common = {"wg": wg9, "wu": wu9, "wd": wd9, "wr": wrT, "rb": rb8,
                   "onesA": onesA, "onesB": onesB, "ident": ident,
                   "fconst": fc}

    def run_ffn0(xTs):
        r = _run(nc_ffn0, [dict(ffn0_common, x=xTs[b]) for b in range(8)])
        return [r[b]["y"] for b in range(8)]

    qkv_w = np.asarray(L0_attn_qkv_w, f32)
    qkv_b = np.asarray(L0_attn_qkv_b, f32)
    qkv_bp = qkv_b.reshape(48, 2, 32)[:, ::-1, :].reshape(3072)
    cos, sin = host_rope_tables()
    nc_qkv0 = build_qkv0()
    qkv0_common = {
        "qkvw": np.ascontiguousarray(qkv_w.T),
        "qkvb": np.ascontiguousarray(qkv_b.reshape(24, 128).T),
        "qkvbp": np.ascontiguousarray(qkv_bp.reshape(24, 128).T),
        "cost": cos, "sint": sin, "onesA": onesA, "onesB": onesB,
        "fconst": fc,
    }
    nc_a0c = build_attn0core()
    nc_a0p = build_attn0proj()
    a0p_common = {
        "pw": np.ascontiguousarray(np.asarray(L0_attn_proj_w, f32).T),
        "pb": np.ascontiguousarray(
            np.asarray(L0_attn_proj_b, f32).reshape(8, 128).T),
    }

    def run_attn0(xTs):
        r1 = _run(nc_qkv0, [dict(qkv0_common, x=xTs[b]) for b in range(8)])
        maps2 = []
        for p in range(8):
            maps2.append({
                "qz": np.stack([r1[b2]["qro"][p] for b2 in range(8)]),
                "kz": np.stack([r1[b2]["kro"][p] for b2 in range(8)]),
                "vz": np.stack([r1[b2]["vtm"][p] for b2 in range(8)]),
                "onesB": onesB, "fconst": fc,
            })
        r2 = _run(nc_a0c, maps2)
        maps3 = []
        for c in range(8):
            o4 = np.asarray(r2[c]["oz"]).reshape(8, 2, 64, 256, 16)
            scr = np.ascontiguousarray(
                o4.transpose(4, 2, 1, 0, 3).reshape(1024, 4096))
            maps3.append(dict(a0p_common, sc=scr, x=xTs[c]))
        r3 = _run(nc_a0p, maps3)
        return [r3[c]["y"] for c in range(8)]

    nc_ffn1 = build_ffn_l1()
    ffn1_common = {
        "q1": np.asarray(L1_ffn_fc1_q, np.int32),
        "q2": np.asarray(L1_ffn_fc2_q, np.int32),
        "sc1": np.asarray(L1_ffn_fc1_scales, f32),
        "sc2": np.asarray(L1_ffn_fc2_scales, f32),
        "b1": np.ascontiguousarray(
            np.asarray(L1_ffn_fc1_bias, f32).reshape(32, 128).T),
        "b2": np.ascontiguousarray(
            np.asarray(L1_ffn_fc2_bias, f32).reshape(8, 128).T),
        "gam": np.ascontiguousarray(
            np.asarray(L1_ffn_ln_gamma, f32).reshape(8, 128).T),
        "bet": np.ascontiguousarray(
            np.asarray(L1_ffn_ln_beta, f32).reshape(8, 128).T),
        "onesA": onesA, "onesB": onesB, "fconst": fc,
    }

    def run_ffn1(xTs):
        r = _run(nc_ffn1, [dict(ffn1_common, x=xTs[b]) for b in range(8)])
        return [r[b]["y"] for b in range(8)]

    iota, iop, tri1, tri2, tri1c, tri2c, em = host_attn1_consts()
    nc_a1 = build_attn1()
    a1_common = {
        "qkvw": np.ascontiguousarray(np.asarray(L1_attn_W_attn, f32).T),
        "qkvb": np.ascontiguousarray(
            np.asarray(L1_attn_b_attn, f32).reshape(24, 128).T),
        "pw": np.ascontiguousarray(np.asarray(L1_attn_W_proj, f32).T),
        "pb": np.ascontiguousarray(
            np.asarray(L1_attn_b_proj, f32).reshape(8, 128).T),
        "onesA": onesA, "onesB": onesB, "fconst": fc,
        "iot": iota, "iop": iop, "tri1": tri1, "tri2": tri2,
        "tri1c": tri1c.astype(BF), "tri2c": tri2c.astype(BF), "emask": em,
    }

    def run_attn1(xTs):
        r = _run(nc_a1, [dict(a1_common, x=xTs[b]) for b in range(8)])
        return [r[b]["y"] for b in range(8)]

    xTs = xT
    xTs = run_ffn0(xTs)
    xTs = run_attn0(xTs)
    xTs = run_ffn0(xTs)
    xTs = run_ffn1(xTs)
    xTs = run_attn1(xTs)
    xTs = run_ffn1(xTs)
    out = np.stack([np.asarray(xTs[b]).T for b in range(8)]).astype(np.float32)
    return out



def emit_attn0_fused(S, nc, ctx, io, tag):
    qroG, kroG, vtmG = io["qroG"], io["kroG"], io["vtmG"]
    onesB, fconst = io["onesB"], io["fconst"]
    x, pw, pb, y = io["x"], io["pw"], io["pb"], io["y"]
    C = {}
    C["onesB"] = sbuf(ctx, nc, "onesB", [1, 128], F32)
    C["fc"] = sbuf(ctx, nc, "fc", [128, len(CONST_VALS)], F32)
    pb_sb = sbuf(ctx, nc, "pb", [128, 8], F32)
    S.dma("sync", C["onesB"][:], onesB, writes=[("cB",)], slot="c")
    S.dma("sync", C["fc"][:], fconst, writes=[("cfc",)], slot="c")
    S.dma("sync", pb_sb[:], pb, writes=[("cpb",)], slot="c")
    pwsb = [sbuf(ctx, nc, f"pw{d}", [128, 1024], BF16) for d in range(8)]
    for d in range(8):
        S.dma("gpsimd", pwsb[d][:], pw[d * 128:(d + 1) * 128, :],
              writes=[("pw", d)], slot=f"w{d % 2}")
    q_sb = [sbuf(ctx, nc, f"q{i}", [128, T], BF16) for i in range(2)]
    k_sb = [sbuf(ctx, nc, f"k{i}", [128, T], BF16) for i in range(2)]
    va_sb = [sbuf(ctx, nc, f"va{i}", [128, 65 * 32], BF16) for i in range(2)]
    vt_sb = [sbuf(ctx, nc, f"vt{i}", [128, T], BF16) for i in range(2)]
    ps_s = [psum(ctx, nc, f"ps_s{i}", [128, 1024]) for i in range(2)]
    ps_o = [psum(ctx, nc, f"ps_o{i}", [128, 512]) for i in range(2)]
    ps_p0 = psum(ctx, nc, "ps_p0", [128, 256])
    ps_p = [ps_p0, ps_p0]
    pt = [sbuf(ctx, nc, f"pt{i}", [128, 1024], BF16) for i in range(2)]
    dinv = sbuf(ctx, nc, "dinv", [1, 512], F32)
    dbc = sbuf(ctx, nc, "dbc", [128, 512], F32)
    pbc = psum(ctx, nc, "pbc", [128, 512])
    ob = sbuf(ctx, nc, "ob", [64, T], BF16)
    scu = [sbuf(ctx, nc, f"scu{c2}", [128, 256], BF16) for c2 in range(8)]
    xsl = sbuf(ctx, nc, "xsl", [128, 256], F32)
    ysl = [sbuf(ctx, nc, f"ysl{i}", [128, 256], F32) for i in range(2)]
    _rowcache = {}

    def _grow(E, b2, gten):
        key = (id(E), b2)
        if key not in _rowcache:
            _rowcache[key] = E.snap(b2 * 8 + E.partition_id(),
                                    min_val=0, max_val=63)
        return gten[ds(_rowcache[key], 1)].rearrange("a p t -> (a p) t")
    for b2 in range(8):
        sl = b2 % 2
        S.dma("sync", q_sb[sl][:],
              (lambda E, b2=b2: _grow(E, b2, qroG)),
              reads=[("ag", "qro")], writes=[("q", sl)], slot=f"q{sl}")
        S.dma("sync", k_sb[sl][:],
              (lambda E, b2=b2: _grow(E, b2, kroG)),
              reads=[("ag", "kro")], writes=[("k", sl)], slot=f"k{sl}")
        S.dma("sync", vt_sb[sl][:],
              (lambda E, b2=b2: _grow(E, b2, vtmG)),
              reads=[("ag", "vtm")], writes=[("vt", sl)], slot=f"v{sl}")
        for h in range(2):
            hb = 64 * h
            S.op("vector", lambda E, sl=sl: E.memset(va_sb[sl][:], 1.0),
                 reads=[("var", sl)], writes=[("va", sl)])
            S.op("vector", lambda E, sl=sl, hb=hb: E.tensor_copy(
                out=va_sb[sl][:].rearrange("p (tt c) -> p tt c", c=65)
                [:, :, 0:64],
                in_=vt_sb[sl][:].rearrange("p (tt c) -> p tt c", c=128)
                [:, :, hb:hb + 64]),
                reads=[("vt", sl)], writes=[("va", sl)])
            for qg in range(8):
                t0 = qg * 512

                def emit_av(kc2, qg=qg, sl=sl):
                    b = kc2 % 2
                    S.op("tensor", lambda E, sl=sl, kc2=kc2, qg=qg, b=b: [
                        E.matmul(ps_o[qg % 2][0:65, :],
                                 va_sb[sl][:, (kc2 * 2 + i) * 65:
                                           (kc2 * 2 + i + 1) * 65],
                                 pt[b][:, i * 512:(i + 1) * 512],
                                 start=(kc2 == 0 and i == 0),
                                 stop=(kc2 == 15 and i == 1))
                        for i in range(2)][-1],
                        reads=[("va", sl), ("pt", b), ("pso", qg % 2)],
                        writes=[("pso", qg % 2)])
                for kc2 in range(16):
                    b = kc2 % 2
                    S.op("tensor", lambda E, sl=sl, hb=hb, kc2=kc2, t0=t0,
                         b=b: [
                        E.matmul(ps_s[b][:, i * 512:(i + 1) * 512],
                                 k_sb[sl][hb:hb + 64,
                                          (kc2 * 2 + i) * 128:
                                          (kc2 * 2 + i + 1) * 128],
                                 q_sb[sl][hb:hb + 64, t0:t0 + 512],
                                 start=True, stop=True)
                        for i in range(2)][-1],
                        reads=[("k", sl), ("q", sl), ("pss", b)],
                        writes=[("pss", b)])
                    if kc2 > 0:
                        emit_av(kc2 - 1)
                    S.op("scalar", lambda E, b=b: E.activation(
                        out=pt[b][:], in_=ps_s[b][:], func=AF.Exp,
                        bias=FC(C, "mtwenty"), scale=FC(C, "eighth")),
                        reads=[("pss", b), ("cfc",)],
                        writes=[("pt", b)])
                emit_av(15)
                S.op("scalar", lambda E, qg=qg: E.activation(
                    out=dinv[:], in_=ps_o[qg % 2][64:65, :], func=AF.Ln),
                    reads=[("pso", qg % 2)], writes=[("dinv",)])
                S.op("scalar", lambda E: E.activation(
                    out=dinv[:], in_=dinv[:], func=AF.Exp, bias=0.0,
                    scale=FC(C, "mone", 1)),
                    reads=[("dinv",)], writes=[("dinv",)])
                S.op("tensor", lambda E: E.matmul(
                    pbc[:], C["onesB"][:], dinv[:], start=True, stop=True),
                    reads=[("dinv",), ("cB",), ("dbcv",)], writes=[("pbc",)])
                S.op("vector", lambda E: E.tensor_copy(out=dbc[:], in_=pbc[:]),
                     reads=[("pbc",)], writes=[("dbcv",)])
                S.op("vector", lambda E, qg=qg, t0=t0: E.tensor_mul(
                    out=ob[:, t0:t0 + 512], in0=ps_o[qg % 2][0:64, :],
                    in1=dbc[0:64, :]),
                    reads=[("pso", qg % 2), ("dbcv",)],
                    writes=[("ob", qg)])
            for c2 in range(8):
                for j in range(2):
                    m = 2 * c2 + j
                    S.op("vector", lambda E, c2=c2, j=j, m=m: E.tensor_copy(
                        out=scu[c2][64 * j:64 * j + 64, :],
                        in_=ob[:].rearrange("p (G m) -> p G m", m=16)
                        [:, :, m]),
                        reads=[("ob", qg) for qg in range(8)]
                        + [("scur", c2)],
                        writes=[("scu", c2, j)])
            n0 = h * 2048 + b2 * 256
            for ho in range(8):
                pz = ho % 2
                S.op("tensor", lambda E, ho=ho: [
                    E.matmul(ps_p0[:], pwsb[d][:, ho * 128:(ho + 1) * 128],
                             scu[d][:], start=(d == 0), stop=(d == 7))
                    for d in range(8)][-1],
                    reads=[("pw", d) for d in range(8)]
                    + [("scu", d, j) for d in range(8) for j in range(2)]
                    + [("psp",)],
                    writes=[("psp",)])
                S.dma("sync", xsl[:], x[ho * 128:(ho + 1) * 128, n0:n0 + 256],
                      reads=[("xslr",)], writes=[("xsl",)], slot="xr")
                S.op("vector", lambda E, ho=ho, pz=pz: E.tensor_scalar(
                    out=ysl[pz][:], in0=ps_p0[:],
                    scalar1=pb_sb[:, ho:ho + 1], scalar2=None, op0=ALU.add),
                    reads=[("psp",), ("cpb",), ("yslr", pz)],
                    writes=[("ysl", pz), ("psp",)])
                S.op("vector", lambda E, pz=pz: E.tensor_add(
                    out=ysl[pz][:], in0=ysl[pz][:], in1=xsl[:]),
                    reads=[("xsl",), ("ysl", pz)],
                    writes=[("ysl", pz), ("xslr",)])
                S.dma("gpsimd", y[ho * 128:(ho + 1) * 128, n0:n0 + 256],
                      ysl[pz][:],
                      reads=[("ysl", pz)],
                      writes=[(tag + "y", ho, b2, h), ("yslr", pz)],
                      slot=f"y{pz}")
            S.op("vector", lambda E: E.engine_nop(),
                 reads=[("scu", c2, j) for c2 in range(8) for j in range(2)],
                 writes=[("scur0",)] + [("scur", c2) for c2 in range(8)])
    barrier(S)



def build_fused():
    nc = bass.Bass("TRN2", target_bir_lowering=False)
    EI = lambda n, s, d=F32: nc.dram_tensor(n, s, d, kind="ExternalInput").ap()
    IN = lambda n, s, d=F32: nc.dram_tensor(n, s, d, kind="Internal").ap()
    cm = {}
    cm["onesA"] = EI("onesA", [128, 1])
    cm["onesB"] = EI("onesB", [1, 128])
    cm["ident"] = EI("ident", [128, 128])
    cm["fconst"] = EI("fconst", [128, len(CONST_VALS)])
    x0 = EI("x", [D, T])
    yf = nc.dram_tensor("y", [D, T], F32, kind="ExternalOutput").ap()
    x1, x2, x3, x4, x5 = (IN(f"xs{i}", [D, T]) for i in range(5))
    f0 = dict(cm)
    f0.update(wg=EI("wg", [9, D, 2048]), wu=EI("wu", [9, D, 2048]),
              wd=EI("wd", [9, 2048, D]), wr=EI("wr", [D, 8]),
              rb=EI("rb", [8, 1]))
    a0 = dict(cm)
    a0.update(qkvw=EI("a0qkvw", [D, 3072]), qkvb=EI("a0qkvb", [128, 24]),
              qkvbp=EI("a0qkvbp", [128, 24]), cost=EI("cost", [128, T]),
              sint=EI("sint", [128, T]),
              qro=IN("qroS", [8, 128, T], BF16),
              kro=IN("kroS", [8, 128, T], BF16),
              vtm=IN("vtmS", [8, 128, T], BF16),
              qroG=IN("qroG", [64, 128, T], BF16),
              kroG=IN("kroG", [64, 128, T], BF16),
              vtmG=IN("vtmG", [64, 128, T], BF16),
              pw=EI("a0pw", [D, D]), pb=EI("a0pb", [128, 8]))
    f1 = dict(cm)
    f1.update(q1=EI("q1", [D, 4096], I32), q2=EI("q2", [4096, D], I32),
              sc1=EI("sc1", [8, 4096]), sc2=EI("sc2", [32, D]),
              b1=EI("b1", [128, 32]), b2=EI("b2", [128, 8]),
              gam=EI("gam", [128, 8]), bet=EI("bet", [128, 8]),
              w2s=IN("w2s", [4096, D], BF16))
    a1 = dict(cm)
    a1.update(qkvw=EI("a1qkvw", [D, 3072]), qkvb=EI("a1qkvb", [128, 24]),
              pw=EI("a1pw", [D, D]), pb=EI("a1pb", [128, 8]),
              iot=EI("iot", [128, T]), iop=EI("iop", [128, 1]),
              tri1=EI("tri1", [128, 256]), tri2=EI("tri2", [128, 256]),
              tri1c=EI("tri1c", [128, 256], BF16),
              tri2c=EI("tri2c", [128, 256], BF16),
              emask=EI("emask", [128, 1], I32),
              qs=IN("a1qs", [8, 128, T], BF16),
              ks=IN("a1ks", [8, 128, T], BF16),
              vas=IN("a1vas", [8, 2, 128, 65 * 32], BF16))

    with ExitStack() as ctx:
        S = Sched(nc, ctx)
        with ExitStack() as c:
            emit_ffn_l0(S, nc, c, dict(f0, x=x0, y=x1), tag="f0a")

        def ag(name, src_t, dst_t, rkeys):
            S.op("gpsimd",
                 lambda E, s_=src_t, d_=dst_t: E.collective_compute(
                     "AllGather", ALU.bypass,
                     replica_groups=[list(range(8))],
                     ins=[s_.rearrange("a b c -> a (b c)").opt()],
                     outs=[d_.rearrange("a b c -> a (b c)").opt()]),
                 reads=rkeys, writes=[("ag", name)], sem=f"_ag_{name}")

        def post_qk():
            ag("qro", a0["qro"], a0["qroG"],
               [("a0qout", 0, p) for p in range(8)])
            ag("kro", a0["kro"], a0["kroG"],
               [("a0qout", 1, p) for p in range(8)])
        with ExitStack() as c:
            emit_qkv0(S, nc, c, dict(a0, x=x1), tag="a0q", post_qk=post_qk)
        ag("vtm", a0["vtm"], a0["vtmG"],
           [("a0qoutv", p) for p in range(8)])
        with ExitStack() as c:
            emit_attn0_fused(S, nc, c, dict(a0, x=x1, y=x2), tag="a0f")
        with ExitStack() as c:
            emit_ffn_l0(S, nc, c, dict(f0, x=x2, y=x3), tag="f0b")
        with ExitStack() as c:
            emit_ffn_l1(S, nc, c, dict(f1, x=x3, y=x4), tag="f1a")
        with ExitStack() as c:
            emit_attn1(S, nc, c, dict(a1, x=x4, y=x5), tag="a1")
        with ExitStack() as c:
            emit_ffn_l1(S, nc, c, dict(f1, x=x5, y=yf), tag="f1b")
        S.emit()
    return nc


def kernel_fused_inputs(L0_attn_x, L0_attn_qkv_w, L0_attn_qkv_b,
                        L0_attn_proj_w, L0_attn_proj_b, L0_ffn_Wg_s,
                        L0_ffn_Wu_s, L0_ffn_Wd_s, L0_ffn_Wg, L0_ffn_Wu,
                        L0_ffn_Wd, L0_ffn_Wr, L0_ffn_rb, L1_attn_W_attn,
                        L1_attn_b_attn, L1_attn_W_proj, L1_attn_b_proj,
                        L1_ffn_ln_gamma, L1_ffn_ln_beta, L1_ffn_fc1_q,
                        L1_ffn_fc1_scales, L1_ffn_fc1_bias, L1_ffn_fc2_q,
                        L1_ffn_fc2_scales, L1_ffn_fc2_bias):
    f32 = np.float32
    cos, sin = host_rope_tables()
    iota, iop, tri1, tri2, tri1c, tri2c, em = host_attn1_consts()
    qkv_b = np.asarray(L0_attn_qkv_b, f32)
    qkv_bp = qkv_b.reshape(48, 2, 32)[:, ::-1, :].reshape(3072)
    common = {
        "onesA": np.ones((128, 1), f32), "onesB": np.ones((1, 128), f32),
        "ident": np.eye(128, dtype=f32), "fconst": host_fconst(),
        "wg": np.ascontiguousarray(np.concatenate(
            [np.asarray(L0_ffn_Wg_s, f32).T[None],
             np.transpose(np.asarray(L0_ffn_Wg, f32), (0, 2, 1))], axis=0)),
        "wu": np.ascontiguousarray(np.concatenate(
            [np.asarray(L0_ffn_Wu_s, f32).T[None],
             np.transpose(np.asarray(L0_ffn_Wu, f32), (0, 2, 1))], axis=0)),
        "wd": np.ascontiguousarray(np.concatenate(
            [np.asarray(L0_ffn_Wd_s, f32).T[None],
             np.transpose(np.asarray(L0_ffn_Wd, f32), (0, 2, 1))], axis=0)),
        "wr": np.ascontiguousarray(np.asarray(L0_ffn_Wr, f32).T),
        "rb": np.ascontiguousarray(np.asarray(L0_ffn_rb, f32).reshape(8, 1)),
        "a0qkvw": np.ascontiguousarray(np.asarray(L0_attn_qkv_w, f32).T),
        "a0qkvb": np.ascontiguousarray(qkv_b.reshape(24, 128).T),
        "a0qkvbp": np.ascontiguousarray(qkv_bp.reshape(24, 128).T),
        "cost": cos, "sint": sin,
        "a0pw": np.ascontiguousarray(np.asarray(L0_attn_proj_w, f32).T),
        "a0pb": np.ascontiguousarray(
            np.asarray(L0_attn_proj_b, f32).reshape(8, 128).T),
        "q1": np.asarray(L1_ffn_fc1_q, np.int32),
        "q2": np.asarray(L1_ffn_fc2_q, np.int32),
        "sc1": np.asarray(L1_ffn_fc1_scales, f32),
        "sc2": np.asarray(L1_ffn_fc2_scales, f32),
        "b1": np.ascontiguousarray(
            np.asarray(L1_ffn_fc1_bias, f32).reshape(32, 128).T),
        "b2": np.ascontiguousarray(
            np.asarray(L1_ffn_fc2_bias, f32).reshape(8, 128).T),
        "gam": np.ascontiguousarray(
            np.asarray(L1_ffn_ln_gamma, f32).reshape(8, 128).T),
        "bet": np.ascontiguousarray(
            np.asarray(L1_ffn_ln_beta, f32).reshape(8, 128).T),
        "a1qkvw": np.ascontiguousarray(np.asarray(L1_attn_W_attn, f32).T),
        "a1qkvb": np.ascontiguousarray(
            np.asarray(L1_attn_b_attn, f32).reshape(24, 128).T),
        "a1pw": np.ascontiguousarray(np.asarray(L1_attn_W_proj, f32).T),
        "a1pb": np.ascontiguousarray(
            np.asarray(L1_attn_b_proj, f32).reshape(8, 128).T),
        "iot": iota, "iop": iop, "tri1": tri1, "tri2": tri2,
        "tri1c": tri1c.astype(BF), "tri2c": tri2c.astype(BF), "emask": em,
    }
    x0 = np.asarray(L0_attn_x, f32)
    return [dict(common, x=np.ascontiguousarray(x0[b])
                 .T.copy()) for b in range(8)]
